# version 14
# baseline (speedup 1.0000x reference)
import contextlib
import numpy as np

import concourse.bass as bass
import concourse.mybir as mybir
from concourse.bass_utils import run_bass_kernel_spmd
from concourse.alu_op_type import AluOpType as OP

F32 = mybir.dt.float32
BF16 = mybir.dt.bfloat16
AF = mybir.ActivationFunctionType
AX = mybir.AxisListType.X

NCORES = 8
SEQ = 4096
D = 1024
NH = 16
NHP = 8
NT = 8
TT = 512
NCH = 8
NE = 8
DI = 512
NST = 4
SCALE = 0.125
EPS = 1e-5
BIG = 1.0e30
NKV = 3

ENGINES = ("sync", "gpsimd", "tensor", "vector", "scalar")


class Prog:

    def __init__(self, nc, es):
        self.nc = nc
        self.q = {e: [] for e in ENGINES}
        self.cnt = {}
        self.sems = {}
        self.es = es
        self.extra_first = {}
        self.banks = {}

    def sem(self, name):
        h = self.es.enter_context(self.nc.semaphore(name))
        self.sems[name] = h
        self.cnt[name] = 0
        return name

    def bank_wait(self, i):
        return list(self.banks.get(i, []))

    def bank_mark(self, i, semname):
        self.banks[i] = [(semname, self.cnt[semname])]

    def op(self, eng, fn, waits=(), inc=None, n=1):
        waits = list(waits)
        if eng in self.extra_first:
            waits += self.extra_first.pop(eng)
        for s, t in waits:
            assert t <= self.cnt[s], (eng, s, t, self.cnt[s])
        if inc is not None:
            self.cnt[inc] += n
        self.q[eng].append((fn, tuple(waits), inc, n))
        return self.cnt[inc] if inc is not None else None

    def dma(self, eng, out, in_, sem, waits=()):
        return self.op(eng, lambda e, o=out, i=in_: e.dma_start(out=o, in_=i),
                       waits, sem, 16)

    def flush(self, block):
        for name in ENGINES:
            ops = self.q[name]

            def body(h, ops=ops):
                for fn, waits, inc, n in ops:
                    for s, t in waits:
                        h.wait_ge(self.sems[s], t)
                    r = fn(h)
                    if inc is not None:
                        r.then_inc(self.sems[inc], n)

            getattr(block, name)(body)
        self.q = {e: [] for e in ENGINES}


DEV_NT = None
DEV_SKIP_B = False


def build_nc(debug=False):
    nc = bass.Bass("TRN2", target_bir_lowering=False)

    def din(name, shape, dt=F32):
        return nc.dram_tensor(name, shape, dt, kind="ExternalInput").ap()

    xT = din("xT", [D, SEQ])
    wqkvT = din("wqkvT", [D, 3 * D])
    woutT = din("woutT", [D, D])
    gwT = din("gwT", [D, NE])
    bout = din("bout", [128, NCH])
    cbias = din("cbias", [1, NE])
    idxeps = din("idxeps", [128, NE])
    masks = din("masks", [4, 128, 2 * TT])
    onesf = din("onesf", [128, 128])
    hones = din("hones", [128, 2])
    i8 = din("i8", [8, 8])
    i128 = din("i128", [128, 128])
    wg = din("wg", [NE + 1, D, DI])
    wu = din("wu", [NE + 1, D, DI])
    wd = din("wd", [NE + 1, DI, D])

    yT = nc.dram_tensor("yT", [D, SEQ], F32, kind="ExternalOutput").ap()
    xp = nc.dram_tensor("xp", [D, SEQ], F32, kind="Internal").ap()
    ksp = nc.dram_tensor("ksp", [SEQ // 128, 128, 1024], BF16, kind="Internal").ap()
    vsp = nc.dram_tensor("vsp", [SEQ // 128, 128, 1040], BF16, kind="Internal").ap()

    dbg = {}
    if debug:
        for name, shape, dt in [
            ("d_h1T", [128, NCH * TT], BF16), ("d_ont", [128, NHP * TT], BF16),
            ("d_h2T", [128, NCH * TT], BF16), ("d_cwT", [32, 2 * TT], F32),
            ("d_nb", [128, 4], F32), ("d_bbt", [32, 48], F32),
            ("d_rmx", [32, NHP], F32),
        ]:
            dbg[name] = nc.dram_tensor(name, shape, dt, kind="ExternalOutput").ap()

    es = contextlib.ExitStack()
    with es:
        def sb(name, shape, dt):
            return es.enter_context(nc.sbuf_tensor(name, shape, dt))

        xio = sb("xio", [128, NCH * TT], F32)
        onesF = sb("onesF", [128, 128], F32)
        onesB = sb("onesB", [128, 128], BF16)
        i8F = sb("i8F", [8, 8], F32)
        i128F = sb("i128F", [128, 128], F32)
        boutF = sb("boutF", [128, NCH], F32)
        crow = sb("crow", [32, 16], F32)
        gcol = sb("gcol", [128, NE], F32)
        cbb = sb("cbb", [128, NE], F32)
        idxE = sb("idxE", [128, NE], F32)
        gwF = sb("gwF", [128, NCH * NE], F32)
        stg = sb("stg", [128, 2 * 1024], F32)
        r_q = sb("r_q", [1, TT], F32)
        r_mean = sb("r_mean", [1, TT], F32)
        r_q2 = sb("r_q2", [1, TT], F32)
        r_v = sb("r_v", [1, TT], F32)
        r_ve = sb("r_ve", [1, TT], F32)
        r_ln = sb("r_ln", [1, TT], F32)
        r_rstd = sb("r_rstd", [1, TT], F32)
        r_mr = sb("r_mr", [1, TT], F32)
        tA = sb("tA", [128, TT], F32)
        tB = sb("tB", [128, TT], F32)
        tC = sb("tC", [128, TT], F32)
        tD = sb("tD", [128, TT], F32)
        tE = sb("tE", [128, TT], F32)
        tF = sb("tF", [128, TT], F32)
        xbb = sb("xbb", [128, 3 * TT], BF16)
        xsq = sb("xsq", [128, 3 * TT], BF16)
        maskB = sb("maskB", [128, 4 * 2 * TT], BF16)
        lb = sb("lb", [128, 2 * TT], F32)
        h1T = sb("h1T", [128, NCH * TT], BF16)
        honesB = sb("honesB", [128, 2], BF16)
        rmxQ = sb("rmxQ", [32, NHP], F32)
        rmxK = sb("rmxK", [32, NHP], F32)
        sqb = sb("sqb", [128, 2 * TT], BF16)
        bbt = sb("bbt", [32, 48], F32)
        negBc = sb("negBc", [128, 4], F32)
        gtm = sb("gtm", [128, 4 * 10], F32)
        def rt(name):
            return sb(name, [128, 4 * NE], F32)
        rA_, rB_, rL_, rS_, rC_ = rt("rtA"), rt("rtB"), rt("rtL"), rt("rtS"), rt("rtC")
        rG4 = sb("rG4", [128, 4 * 4], F32)
        rM_ = sb("rM_", [128, 4 * 8], F32)
        rI_, rW_, rSM, rFC, rCW = rt("rtI"), rt("rtW"), rt("rtSM"), rt("rtFC"), rt("rtCW")
        glbuf = sb("glbuf", [8, TT], F32)
        cwT = sb("cwT", [32, 2 * TT], F32)

        PSA = es.enter_context(nc.psum_tensor("psA", [128, 2048], F32))
        PSB = es.enter_context(nc.psum_tensor("psB", [128, 2048], F32))
        PS = [PSA[:, 0:1024], PSA[:, 1024:2048], PSB[:, 0:1024], PSB[:, 1024:2048]]
        def bank(i):
            return (PSA if i < 4 else PSB)[:, 512 * (i % 4):512 * (i % 4) + 512]

        p = Prog(nc, es)
        for s in ("c_ld", "x_ld", "kv_ld", "w_ld", "kv_st", "xp_st", "sh_st",
                  "y_st", "t", "s", "v", "g"):
            p.sem(s)

        esA = es.enter_context(contextlib.ExitStack())
        def sbA(name, shape, dt):
            return esA.enter_context(nc.sbuf_tensor(name, shape, dt))

        wqkvB = sbA("wqkvB", [128, NCH * 3072], BF16)
        woutB = sbA("woutB", [128, NCH * 1024], BF16)
        Qp = sbA("Qp", [128, NHP * TT], BF16)
        Kc = sbA("Kc", [128, NHP * TT], BF16)
        Vc = sbA("Vc", [128, 4 * 1040], BF16)
        kvK = sbA("kvK", [128, NKV * 1024], BF16)
        kvV = sbA("kvV", [128, NKV * 1040], BF16)
        Pb = sbA("Pb", [128, 4 * 1024], BF16)
        OnT = sbA("OnT", [128, NHP * TT], BF16)
        Ot = sbA("Ot", [64, 2 * TT], BF16)

        blkA_es = contextlib.ExitStack()
        blkA = blkA_es.enter_context(nc.Block())

        for dst, src in [(onesF[:], onesf), (i8F[:], i8), (i128F[:], i128),
                         (boutF[:], bout), (idxE[:], idxeps),
                         (crow[0:1, 8:16], cbias)]:
            p.dma("sync", dst, src, "c_ld")
        for c in range(NCH):
            p.dma("sync", gwF[:, 8 * c:8 * c + 8], gwT[128 * c:128 * c + 128, :], "c_ld")
        for j in range(4):
            sl = 1024 * (j % 2)
            p.dma("sync", stg[:, sl:sl + 1024], masks[j], "c_ld",
                  waits=[("s", p.cnt["s"])])
            p.op("scalar", lambda e, j=j, sl=sl:
                 e.copy(out=maskB[:, 1024 * j:1024 * j + 1024],
                        in_=stg[:, sl:sl + 1024]),
                 waits=[("c_ld", p.cnt["c_ld"])], inc="s")
        p.dma("sync", stg[:, 0:2], hones, "c_ld", waits=[("s", p.cnt["s"])])
        p.op("scalar", lambda e: e.copy(out=onesB[:], in_=onesF[:]),
             waits=[("c_ld", p.cnt["c_ld"])], inc="s")
        p.op("scalar", lambda e: e.copy(out=honesB[:], in_=stg[:, 0:2]), inc="s")
        p.op("vector", lambda e: e.memset(rmxQ[:], 0.0), inc="v")
        p.op("vector", lambda e: e.memset(rmxK[:], 0.0), inc="v")
        for c in range(NCH):
            for h in range(3):
                sl = 1024 * (h % 2)
                w = p.cnt["s"]
                p.dma("sync", stg[:, sl:sl + 1024],
                      wqkvT[128 * c:128 * c + 128, 1024 * h:1024 * h + 1024],
                      "c_ld", waits=[("s", w)])
                p.op("scalar", lambda e, c=c, h=h, sl=sl:
                     e.copy(out=wqkvB[:, 3072 * c + 1024 * h:3072 * c + 1024 * h + 1024],
                            in_=stg[:, sl:sl + 1024]),
                     waits=[("c_ld", p.cnt["c_ld"])], inc="s")
        for c in range(NCH):
            sl = 1024 * (c % 2)
            w = p.cnt["s"]
            p.dma("sync", stg[:, sl:sl + 1024], woutT[128 * c:128 * c + 128, :],
                  "c_ld", waits=[("s", w)])
            p.op("scalar", lambda e, c=c, sl=sl:
                 e.copy(out=woutB[:, 1024 * c:1024 * c + 1024], in_=stg[:, sl:sl + 1024]),
                 waits=[("c_ld", p.cnt["c_ld"])], inc="s")
        CAST_DONE = p.cnt["s"]
        for c in range(NCH):
            p.op("tensor", lambda e, c=c: e.matmul(
                bank(0)[0:1, 0:8], onesF[:, 0:1], gwF[:, 8 * c:8 * c + 8],
                start=(c == 0), stop=(c == NCH - 1)),
                waits=[("c_ld", 16 * (6 + NCH))], inc="t")
        p.op("scalar", lambda e: e.copy(out=crow[0:1, 0:8], in_=bank(0)[0:1, 0:8]),
             waits=[("t", p.cnt["t"])], inc="s")
        p.bank_mark(0, "s")
        SROW = p.cnt["s"]
        for b in range(4):
            p.op("vector", lambda e, b=b: e.stream_shuffle(
                gcol[32 * b:32 * b + 32, :], crow[:, 0:8], mask=[0] * 32),
                waits=[("s", SROW), ("c_ld", 16 * 6)])
            p.op("vector", lambda e, b=b: e.stream_shuffle(
                cbb[32 * b:32 * b + 32, :], crow[:, 8:16], mask=[0] * 32),
                waits=[("s", SROW)], inc="v")
        p.op("vector", lambda e: e.memset(
            Vc[:].rearrange("p (k h x) -> p k h x", k=4, h=NH)[:, :, :, 64:65], 1.0),
            inc="v")

        def ln_tile(out_h1, out_base, xwaits):
            xwaits = list(xwaits)
            t_entry = p.cnt["t"]
            slot_guard = {i: [("t", t_entry)] for i in range(3)}
            for c in range(NCH):
                sl = (c % 3) * TT
                xa = xio[:, TT * c:TT * c + TT]
                w0 = (xwaits if c == 0 else []) + slot_guard.get(c % 3, [])
                p.op("scalar", lambda e, xa=xa, sl=sl:
                     e.copy(out=xbb[:, sl:sl + TT], in_=xa), waits=w0, inc="s")
                p.op("vector", lambda e, xa=xa, sl=sl:
                     e.tensor_tensor(out=xsq[:, sl:sl + TT], in0=xa, in1=xa,
                                     op=OP.mult), waits=w0, inc="v")
                p.op("tensor", lambda e, c=c, sl=sl: e.matmul(
                    bank(0)[0:1, 0:TT], onesB[:, 0:1], xbb[:, sl:sl + TT],
                    start=(c == 0), stop=(c == NCH - 1)),
                    waits=[("s", p.cnt["s"])] + (p.bank_wait(0) if c == 0 else []))
                p.op("tensor", lambda e, c=c, sl=sl: e.matmul(
                    bank(1)[0:1, 0:TT], onesB[:, 0:1], xsq[:, sl:sl + TT],
                    start=(c == 0), stop=(c == NCH - 1)),
                    waits=[("v", p.cnt["v"])] + (p.bank_wait(1) if c == 0 else []),
                    inc="t")
                slot_guard[c % 3] = [("t", p.cnt["t"])]
            TSUM = p.cnt["t"]
            p.op("scalar", lambda e: e.activation(out=r_q[:], in_=bank(0)[0:1, 0:TT],
                 func=AF.Copy, scale=1.0 / 32.0), waits=[("t", TSUM)], inc="s")
            p.op("scalar", lambda e: e.activation(out=r_mean[:], in_=bank(0)[0:1, 0:TT],
                 func=AF.Copy, scale=1.0 / 1024.0), inc="s")
            p.bank_mark(0, "s")
            p.op("vector", lambda e: e.tensor_tensor(out=r_q2[:], in0=r_q[:],
                 in1=r_q[:], op=OP.mult), waits=[("s", p.cnt["s"])], inc="v")
            p.op("vector", lambda e: e.tensor_tensor(out=r_v[:], in0=bank(1)[0:1, 0:TT],
                 in1=r_q2[:], op=OP.subtract), waits=[("v", p.cnt["v"]),
                 ("t", TSUM)], inc="v")
            p.bank_mark(1, "v")
            p.op("vector", lambda e: e.tensor_scalar(out=r_ve[:], in0=r_v[:],
                 scalar1=1.0 / 1024.0, scalar2=EPS, op0=OP.mult, op1=OP.add),
                 waits=[("v", p.cnt["v"])], inc="v")
            p.op("scalar", lambda e: e.activation(out=r_ln[:], in_=r_ve[:],
                 func=AF.Ln), waits=[("v", p.cnt["v"])], inc="s")
            p.op("scalar", lambda e: e.activation(out=r_rstd[:], in_=r_ln[:],
                 func=AF.Exp, scale=-0.5), waits=[("s", p.cnt["s"])], inc="s")
            SRS = p.cnt["s"]
            p.op("vector", lambda e: e.tensor_tensor(out=r_mr[:], in0=r_mean[:],
                 in1=r_rstd[:], op=OP.mult), waits=[("s", SRS)], inc="v")
            VMR = p.cnt["v"]
            p.op("tensor", lambda e: e.matmul(bank(2)[:], onesF[0:1, :],
                 r_mean[:], start=True, stop=True),
                 waits=[("s", SRS)] + p.bank_wait(2))
            p.op("tensor", lambda e: e.matmul(bank(3)[:], onesF[0:1, :],
                 r_rstd[:], start=True, stop=True),
                 waits=p.bank_wait(3), inc="t")
            TBC = p.cnt["t"]
            pend = []
            for c in range(NCH):
                xa = xio[:, TT * c:TT * c + TT]
                tmp = (tA, tB, tC)[c % 3]
                p.op("vector", lambda e, xa=xa, tmp=tmp: e.tensor_tensor(
                    out=tmp[:], in0=xa, in1=bank(2)[:], op=OP.subtract),
                    waits=[("t", TBC)], inc="v")
                pend.append((tmp, c))
                if len(pend) == 2:
                    tm, cc = pend.pop(0)
                    p.op("vector", lambda e, tm=tm, cc=cc: e.tensor_tensor(
                        out=out_h1[:, out_base + TT * cc:out_base + TT * cc + TT],
                        in0=tm[:], in1=bank(3)[:], op=OP.mult), inc="v")
            tm, cc = pend.pop(0)
            p.op("vector", lambda e, tm=tm, cc=cc: e.tensor_tensor(
                out=out_h1[:, out_base + TT * cc:out_base + TT * cc + TT],
                in0=tm[:], in1=bank(3)[:], op=OP.mult), inc="v")
            p.bank_mark(2, "v")
            p.bank_mark(3, "v")
            return p.cnt["v"], VMR

        kv_loads = 0
        stream_consumed = {}
        ot_guard = {}
        rotg = 0

        for T in range(DEV_NT or NT):
            xw = [("xp_st", p.cnt["xp_st"])]
            for c in range(NCH):
                p.dma("sync", xio[:, TT * c:TT * c + TT],
                      xT[128 * c:128 * c + 128, TT * T:TT * T + TT], "x_ld",
                      waits=xw if c == 0 else ())
            XL = p.cnt["x_ld"]
            H1R, _ = ln_tile(h1T, 0, [("x_ld", XL), ("s", CAST_DONE)])

            for hp in range(NHP):
                for base, dst in ((0, Qp), (1024, Kc)):
                    bi = rotg % 8; rotg += 1
                    bk = bank(bi)
                    for c in range(NCH):
                        p.op("tensor", lambda e, c=c, hp=hp, base=base, bk=bk: e.matmul(
                            bk, wqkvB[:, 3072 * c + base + 128 * hp:
                                      3072 * c + base + 128 * hp + 128],
                            h1T[:, TT * c:TT * c + TT],
                            start=(c == 0), stop=(c == NCH - 1)),
                            waits=([("v", H1R)] + p.bank_wait(bi)) if c == 0 else (),
                            inc=("t" if c == NCH - 1 else None))
                    p.op("scalar", lambda e, dst=dst, hp=hp, bk=bk: e.copy(
                        out=dst[:, TT * hp:TT * hp + TT], in_=bk),
                        waits=[("t", p.cnt["t"]), ("kv_st", p.cnt["kv_st"])], inc="s")
                    p.bank_mark(bi, "s")
                    rmx = rmxQ if base == 0 else rmxK
                    ssl = TT * (0 if base == 0 else 1)
                    p.op("vector", lambda e, dst=dst, hp=hp, ssl=ssl: e.tensor_tensor(
                        out=sqb[:, ssl:ssl + TT], in0=dst[:, TT * hp:TT * hp + TT],
                        in1=dst[:, TT * hp:TT * hp + TT], op=OP.mult),
                        waits=[("s", p.cnt["s"])], inc="v")
                    bi2 = rotg % 8; rotg += 1
                    p.op("tensor", lambda e, ssl=ssl, bk2=bank(bi2): e.matmul(
                        bk2[0:2, 0:TT], honesB[:], sqb[:, ssl:ssl + TT],
                        start=True, stop=True),
                        waits=[("v", p.cnt["v"])] + p.bank_wait(bi2), inc="t")
                    rsl = 10 + (rotg % 2)
                    p.op("vector", lambda e, bk2=bank(bi2), rsl=rsl: e.tensor_reduce(
                        out=bbt[0:2, rsl:rsl + 1], in_=bk2[0:2, 0:TT], axis=AX,
                        op=OP.max), waits=[("t", p.cnt["t"])], inc="v")
                    p.bank_mark(bi2, "v")
                    p.op("vector", lambda e, rmx=rmx, hp=hp, rsl=rsl: e.tensor_tensor(
                        out=rmx[0:2, hp:hp + 1], in0=rmx[0:2, hp:hp + 1],
                        in1=bbt[0:2, rsl:rsl + 1], op=OP.max),
                        waits=[("v", p.cnt["v"])], inc="v")
            QKE = p.cnt["s"]
            for s4 in range(4):
                for h4 in range(2):
                    bi = rotg % 8; rotg += 1
                    bk = bank(bi)
                    for c in range(NCH):
                        p.op("tensor", lambda e, c=c, s4=s4, h4=h4, bk=bk: e.matmul(
                            bk, h1T[:, TT * c + 128 * s4:TT * c + 128 * s4 + 128],
                            wqkvB[:, 3072 * c + 2048 + 512 * h4:
                                  3072 * c + 2048 + 512 * h4 + 512],
                            start=(c == 0), stop=(c == NCH - 1)),
                            waits=p.bank_wait(bi) if c == 0 else (),
                            inc=("t" if c == NCH - 1 else None))
                    p.op("scalar", lambda e, s4=s4, h4=h4, bk=bk: e.copy(
                        out=Vc[:, 1040 * s4 + 520 * h4:1040 * s4 + 520 * h4 + 520]
                            .rearrange("p (h x) -> p h x", h=NH // 2)[:, :, 0:64],
                        in_=bk.rearrange("p (h x) -> p h x", h=NH // 2)),
                        waits=[("t", p.cnt["t"]), ("kv_st", p.cnt["kv_st"])], inc="s")
                    p.bank_mark(bi, "s")
            PROD_S = p.cnt["s"]
            for j in range(4):
                p.dma("sync", ksp[4 * T + j].rearrange("p (h x) -> p h x", h=NHP),
                      Kc[:].rearrange("p (h k x) -> p k h x", h=NHP, k=4)[:, j],
                      "kv_st", waits=[("s", QKE)])
                p.dma("sync", vsp[4 * T + j], Vc[:, 1040 * j:1040 * j + 1040],
                      "kv_st", waits=[("s", PROD_S)])

            nkc = 4 * T + 4
            W0 = [("s", p.cnt["s"]), ("v", p.cnt["v"])]
            def _c(sw, k):
                return bbt[0:1, 12 * sw + k:12 * sw + k + 1]
            for sw in range(4):
        
                p.op("vector", lambda e, sw=sw: e.tensor_tensor(
                    out=bbt[0:2, 12 * sw + 0:12 * sw + 1],
                    in0=rmxQ[0:2, 2 * sw:2 * sw + 1],
                    in1=rmxQ[0:2, 2 * sw + 1:2 * sw + 2], op=OP.max),
                    waits=W0 if sw == 0 else ())
                p.op("vector", lambda e, sw=sw: e.tensor_tensor(
                    out=bbt[0:2, 12 * sw + 1:12 * sw + 2],
                    in0=rmxK[0:2, 2 * sw:2 * sw + 1],
                    in1=rmxK[0:2, 2 * sw + 1:2 * sw + 2], op=OP.max), inc="v")
            for sw in range(4):
                p.op("vector", lambda e, sw=sw: e.stream_shuffle(
                    bbt[0:32, 12 * sw + 2:12 * sw + 4],
                    bbt[0:32, 12 * sw + 0:12 * sw + 2], mask=[1] * 32),
                    waits=[("v", p.cnt["v"])] if sw == 0 else (), inc="v")
            for sw in range(4):
                p.op("vector", lambda e, sw=sw: e.tensor_tensor(
                    out=bbt[0:1, 12 * sw + 4:12 * sw + 6],
                    in0=bbt[0:1, 12 * sw + 0:12 * sw + 2],
                    in1=bbt[0:1, 12 * sw + 2:12 * sw + 4], op=OP.max),
                    waits=[("v", p.cnt["v"])] if sw == 0 else (), inc="v")
            for sw in range(4):
                p.op("vector", lambda e, sw=sw: e.tensor_tensor(
                    out=_c(sw, 6), in0=_c(sw, 4), in1=_c(sw, 5), op=OP.mult),
                    waits=[("v", p.cnt["v"])] if sw == 0 else (), inc="v")
            for sw in range(4):
                p.op("scalar", lambda e, sw=sw: e.activation(
                    out=_c(sw, 7), in_=_c(sw, 6), func=AF.Ln),
                    waits=[("v", p.cnt["v"])] if sw == 0 else (), inc="s")
            for sw in range(4):
                p.op("scalar", lambda e, sw=sw: e.activation(
                    out=_c(sw, 8), in_=_c(sw, 7), func=AF.Exp, scale=0.5),
                    waits=[("s", p.cnt["s"])] if sw == 0 else (), inc="s")
            for sw in range(4):
                p.op("vector", lambda e, sw=sw: e.tensor_scalar(
                    out=_c(sw, 9), in0=_c(sw, 8), scalar1=-0.125,
                    scalar2=85.0, op0=OP.mult, op1=OP.add),
                    waits=[("s", p.cnt["s"])] if sw == 0 else (), inc="v")
            for sw in range(4):
                p.op("vector", lambda e, sw=sw: e.tensor_scalar_min(
                    out=_c(sw, 10), in0=_c(sw, 9), scalar1=0.0),
                    waits=[("v", p.cnt["v"])] if sw == 0 else (), inc="v")
            NBMIN = p.cnt["v"]
            for b4 in range(4):
                for sw in range(4):
                    p.op("vector", lambda e, b4=b4, sw=sw: e.stream_shuffle(
                        negBc[32 * b4:32 * b4 + 32, sw:sw + 1],
                        bbt[0:32, 12 * sw + 10:12 * sw + 11], mask=[0] * 32),
                        waits=[("v", NBMIN)] if (b4 == 0 and sw == 0) else (),
                        inc=("v" if (b4 == 3 and sw == 3) else None))
            NB_V = p.cnt["v"]
            for sw in range(4):
                pairs = (2 * sw, 2 * sw + 1)
                pend = []

                def emit_pv(info):
                    islot = info["i"] % 4
                    pi = info["pi"]
                    for hb in range(2):
                        h = 2 * pairs[pi] + hb
                        p.op("tensor", lambda e, pi=pi, islot=islot, h=h, hb=hb,
                             info=info: e.matmul(
                                 PSB[0:65, 1024 * pi + 512 * hb:
                                     1024 * pi + 512 * hb + 512],
                                 info["vsrc"][:, 65 * h:65 * h + 65],
                                 Pb[:, 1024 * islot + 512 * hb:
                                    1024 * islot + 512 * hb + 512],
                                 start=info["start"], stop=info["stop"]),
                             waits=info["wait"] if hb == 0 else (),
                             inc=("t" if hb == 1 else None))
                    if info["load_idx"] is not None:
                        stream_consumed[info["load_idx"]] = p.cnt["t"]

                for kc in range(nkc):
                    diag = kc >= 4 * T
                    if not diag:
                        slot = kv_loads % NKV
                        wt = [("kv_st", p.cnt["kv_st"])]
                        if kv_loads >= NKV:
                            wt.append(("t", stream_consumed[kv_loads - NKV]))
                        p.dma("gpsimd", kvK[:, 1024 * slot:1024 * slot + 1024],
                              ksp[kc], "kv_ld", waits=wt)
                        p.dma("gpsimd", kvV[:, 1040 * slot:1040 * slot + 1040],
                              vsp[kc], "kv_ld")
                        this_load = p.cnt["kv_ld"]
                        kQ = kvK[:, 1024 * slot:1024 * slot + 1024]
                        vQ = kvV[:, 1040 * slot:1040 * slot + 1040]
                        ks = [kQ[:, 128 * pr:128 * pr + 128] for pr in pairs]
                        vs = [vQ, vQ]
                        kv_loads += 1
                    else:
                        j = kc - 4 * T
                        ks = [Kc[:, TT * pr + 128 * j:TT * pr + 128 * j + 128]
                              for pr in pairs]
                        vs = [Vc[:, 1040 * j:1040 * j + 1040]] * 2
                        this_load = None
                    for pi in range(2):
                        i = 2 * kc + pi
                        if len(pend) == 2:
                            emit_pv(pend.pop(0))
                        wt = p.bank_wait(2 * pi) + p.bank_wait(2 * pi + 1)
                        if i == 0:
                            wt.append(("s", PROD_S))
                        if pi == 0 and this_load is not None:
                            wt.append(("kv_ld", this_load))
                        pr = pairs[pi]
                        ksrc = ks[pi]
                        p.op("tensor", lambda e, pi=pi, ksrc=ksrc, pr=pr: e.matmul(
                            PSA[:, 1024 * pi:1024 * pi + 512], ksrc[0:64, :],
                            Qp[0:64, TT * pr:TT * pr + TT],
                            start=True, stop=True, tile_position=(0, 0)), waits=wt)
                        p.op("tensor", lambda e, pi=pi, ksrc=ksrc, pr=pr: e.matmul(
                            PSA[:, 1024 * pi + 512:1024 * pi + 1024], ksrc[64:128, :],
                            Qp[64:128, TT * pr:TT * pr + TT],
                            start=True, stop=True, tile_position=(64, 0)), inc="t")
                        SMM = p.cnt["t"]
                        p.op("scalar", lambda e, pi=pi, islot=i % 4, sw=sw:
                             e.activation(
                                 out=Pb[:, 1024 * islot:1024 * islot + 1024],
                                 in_=PSA[:, 1024 * pi:1024 * pi + 1024], func=AF.Exp,
                                 scale=SCALE, bias=negBc[:, sw:sw + 1]),
                             waits=[("t", SMM)] + ([("v", NB_V)] if i == 0 else []),
                             inc="s")
                        p.bank_mark(2 * pi, "s")
                        p.bank_mark(2 * pi + 1, "s")
                        if diag:
                            j = kc - 4 * T
                            p.op("vector", lambda e, islot=i % 4, j=j: e.tensor_tensor(
                                out=Pb[:, 1024 * islot:1024 * islot + 1024],
                                in0=Pb[:, 1024 * islot:1024 * islot + 1024],
                                in1=maskB[:, 1024 * j:1024 * j + 1024], op=OP.mult),
                                waits=[("s", p.cnt["s"])], inc="v")
                            pv_wait = [("v", p.cnt["v"])]
                        else:
                            pv_wait = [("s", p.cnt["s"])]
                        if kc == 0:
                            pv_wait = pv_wait + p.bank_wait(4 + 2 * pi) + \
                                      p.bank_wait(5 + 2 * pi)
                        pend.append(dict(i=i, pi=pi, vsrc=vs[pi], wait=pv_wait,
                                         start=(kc == 0), stop=(kc == nkc - 1),
                                         load_idx=(kv_loads - 1)
                                         if (not diag and pi == 1) else None))

                for pi in range(2):
                    emit_pv(pend.pop(0))
                    PVD = p.cnt["t"]
                    pr = pairs[pi]
                    p.op("scalar", lambda e, pi=pi, pr=pr: e.copy(
                        out=OnT[0:64, TT * pr:TT * pr + TT],
                        in_=PSB[0:64, 1024 * pi:1024 * pi + 512]),
                        waits=[("t", PVD)], inc="s")
                    p.op("scalar", lambda e, pi=pi: e.copy(
                        out=Ot[:, TT * pi:TT * pi + TT],
                        in_=PSB[0:64, 1024 * pi + 512:1024 * pi + 1024]),
                        waits=ot_guard.get(pi, []), inc="s")
                    SCP = p.cnt["s"]
                    p.dma("sync", OnT[64:128, TT * pr:TT * pr + TT],
                          Ot[:, TT * pi:TT * pi + TT], "sh_st",
                          waits=[("s", SCP)])
                    SHD = p.cnt["sh_st"]
                    ot_guard[pi] = [("sh_st", SHD)]
                    p.op("vector", lambda e, pi=pi: e.tensor_scalar(
                        out=stg[:, 1024 * pi:1024 * pi + 1024][64:65, :],
                        in0=PSB[64:65, 1024 * pi:1024 * pi + 1024],
                        scalar1=1e-37, scalar2=None, op0=OP.add),
                        waits=[("t", PVD)], inc="v")
                    p.banks[4 + 2 * pi] = [("s", SCP), ("v", p.cnt["v"])]
                    p.banks[5 + 2 * pi] = [("s", SCP), ("v", p.cnt["v"])]
                    p.op("vector", lambda e, pi=pi: e.reciprocal(
                        out=stg[:, 1024 * pi:1024 * pi + 1024][64:65, :],
                        in_=stg[:, 1024 * pi:1024 * pi + 1024][64:65, :]),
                        waits=[("v", p.cnt["v"])], inc="v")
                    for b4 in range(4):
                        p.op("vector", lambda e, pi=pi, b4=b4: e.stream_shuffle(
                            lb[32 * b4:32 * b4 + 32, 0:1024],
                            stg[:, 1024 * pi:1024 * pi + 1024][64:96, :],
                            mask=[0] * 32),
                            waits=[("v", p.cnt["v"])] if b4 == 0 else (),
                            inc=("v" if b4 == 3 else None))
                    p.op("vector", lambda e, pr=pr: e.tensor_tensor(
                        out=OnT[0:64, TT * pr:TT * pr + TT],
                        in0=OnT[0:64, TT * pr:TT * pr + TT],
                        in1=lb[0:64, 0:TT], op=OP.mult),
                        waits=[("s", SCP)])
                    p.op("vector", lambda e, pr=pr: e.tensor_tensor(
                        out=OnT[64:128, TT * pr:TT * pr + TT],
                        in0=OnT[64:128, TT * pr:TT * pr + TT],
                        in1=lb[64:128, TT:2 * TT], op=OP.mult),
                        waits=[("sh_st", SHD)], inc="v")

            ALLON = p.cnt["v"]
            for fc in range(NCH):
                bi = rotg % 8; rotg += 1
                bk = bank(bi)
                for hp in range(NHP):
                    p.op("tensor", lambda e, fc=fc, hp=hp, bk=bk: e.matmul(
                        bk, woutB[:, 1024 * hp + 128 * fc:1024 * hp + 128 * fc + 128],
                        OnT[:, TT * hp:TT * hp + TT],
                        start=(hp == 0), stop=(hp == NHP - 1)),
                        waits=([("v", ALLON)] + p.bank_wait(bi))
                        if hp == 0 else (),
                        inc=("t" if hp == NHP - 1 else None))
                p.op("vector", lambda e, bk=bk, fc=fc: e.scalar_tensor_tensor(
                    out=xio[:, TT * fc:TT * fc + TT], in0=bk,
                    scalar=boutF[:, fc:fc + 1],
                    in1=xio[:, TT * fc:TT * fc + TT], op0=OP.add, op1=OP.add),
                    waits=[("t", p.cnt["t"])], inc="v")
                p.bank_mark(bi, "v")
                p.dma("sync", xp[128 * fc:128 * fc + 128, TT * T:TT * T + TT],
                      xio[:, TT * fc:TT * fc + TT], "xp_st",
                      waits=[("v", p.cnt["v"])])
            if debug and T == 0:
                p.dma("gpsimd", dbg["d_h1T"], h1T[:], "y_st", waits=[("v", H1R)])
                p.dma("gpsimd", dbg["d_ont"], OnT[:], "y_st",
                      waits=[("v", ALLON)])
                p.dma("gpsimd", dbg["d_nb"], negBc[:], "y_st", waits=[("v", NB_V)])
                p.dma("gpsimd", dbg["d_bbt"], bbt[:], "y_st")
                p.dma("gpsimd", dbg["d_rmx"], rmxQ[:], "y_st")

        FIN = {k: p.cnt[k] for k in p.cnt}
        p.flush(blkA)
        blkA_es.close()
        esA.close()

        esB = es.enter_context(contextlib.ExitStack())
        def sbB(name, shape, dt):
            return esB.enter_context(nc.sbuf_tensor(name, shape, dt))

        h2T = sbB("h2T", [128, 2 * NCH * TT], BF16)
        wgB = sbB("wgB", [128, 2 * NCH * DI], BF16)
        wuB = sbB("wuB", [128, 2 * NCH * DI], BF16)
        wdB = sbB("wdB", [128, 2 * 4 * D], BF16)
        actB = sbB("actB", [128, 4 * TT], BF16)
        routed = sbB("routed", [128, 2 * NCH * TT], F32)
        cwb = sbB("cwb", [128, TT], F32)

        blkB_es = contextlib.ExitStack()
        blkB = blkB_es.enter_context(nc.Block())
        bw = ([(k, FIN[k]) for k in ("t", "s", "v")] +
              [(k, FIN[k]) for k in ("c_ld", "x_ld", "kv_ld", "kv_st",
                                     "xp_st", "sh_st", "y_st")])
        p.extra_first = {e: list(bw) for e in ENGINES}

        wslot_cnt = {}
        stg_guard = {}
        wchain = []
        wcast_done = {}
        for ST in range(0 if DEV_SKIP_B else NST):
            h2rv = {}
            for tt in range(2):
                T = 2 * ST + tt
                xw = [("y_st", p.cnt["y_st"]), ("t", p.cnt["t"]),
                      ("v", p.cnt["v"]), ("s", p.cnt["s"])]
                for c in range(NCH):
                    p.dma("sync", xio[:, TT * c:TT * c + TT],
                          xp[128 * c:128 * c + 128, TT * T:TT * T + TT], "x_ld",
                          waits=xw if c == 0 else ())
                XL = p.cnt["x_ld"]
                H2R, VMR = ln_tile(h2T, NCH * TT * tt, [("x_ld", XL)])
                h2rv[tt] = H2R
                for c in range(NCH):
                    p.op("tensor", lambda e, c=c: e.matmul(
                        bank(4)[0:8, :], gwF[:, 8 * c:8 * c + 8],
                        xio[:, TT * c:TT * c + TT],
                        start=(c == 0), stop=(c == NCH - 1)),
                        waits=([("v", H2R)] + p.bank_wait(4)) if c == 0 else (),
                        inc=("t" if c == NCH - 1 else None))
                p.op("scalar", lambda e: e.copy(out=glbuf[:], in_=bank(4)[0:8, :]),
                     waits=[("t", p.cnt["t"])], inc="s")
                p.bank_mark(4, "s")
                GLE = p.cnt["s"]
                for sub in range(4):
                    p.op("tensor", lambda e, sub=sub: e.matmul(
                        bank(5)[:, 0:8], glbuf[:, 128 * sub:128 * sub + 128], i8F[:],
                        start=True, stop=True),
                        waits=[("s", GLE), ("v", VMR)] + p.bank_wait(5))
                    p.op("tensor", lambda e, sub=sub: e.matmul(
                        bank(5)[:, 8:9], r_rstd[0:1, 128 * sub:128 * sub + 128],
                        onesF[0:1, 0:1], start=True, stop=True))
                    p.op("tensor", lambda e, sub=sub: e.matmul(
                        bank(5)[:, 9:10], r_mr[0:1, 128 * sub:128 * sub + 128],
                        onesF[0:1, 0:1], start=True, stop=True), inc="t")
                    p.op("scalar", lambda e, sub=sub: e.copy(
                        out=gtm[:, 10 * sub:10 * sub + 10], in_=bank(5)[:, 0:10]),
                        waits=[("t", p.cnt["t"])], inc="s")
                    p.bank_mark(5, "s")
                GTE = p.cnt["s"]

                o8 = lambda t_, sub: t_[:, 8 * sub:8 * sub + 8]
                o4_ = lambda t_, sub: t_[:, 4 * sub:4 * sub + 4]
                def rop(mk, inc=None, w=None):
                    for sub in range(4):
                        p.op("vector", mk(sub), waits=(w or []) if sub == 0 else [],
                             inc=(inc if sub == 3 else None))
                rop(lambda sub: (lambda e: e.tensor_scalar(
                    out=o8(rA_, sub), in0=gtm[:, 10 * sub:10 * sub + 8],
                    scalar1=gtm[:, 10 * sub + 8:10 * sub + 9], scalar2=None,
                    op0=OP.mult)), w=[("s", GTE)])
                rop(lambda sub: (lambda e: e.tensor_scalar(
                    out=o8(rB_, sub), in0=gcol[:],
                    scalar1=gtm[:, 10 * sub + 9:10 * sub + 10], scalar2=None,
                    op0=OP.mult)))
                rop(lambda sub: (lambda e: e.tensor_tensor(
                    out=o8(rL_, sub), in0=o8(rA_, sub), in1=o8(rB_, sub),
                    op=OP.subtract)), inc="v")
                p.op("scalar", lambda e: e.activation(out=rS_[:], in_=rL_[:],
                     func=AF.Sigmoid), waits=[("v", p.cnt["v"])], inc="s")
                SC = p.cnt["s"]
                def ropS(mk, inc=None):
                    for sub in range(4):
                        p.op("vector", mk(sub), waits=[("s", SC)] if sub == 0 else [],
                             inc=(inc if sub == 3 else None))
                ropS(lambda sub: (lambda e: e.tensor_tensor(
                    out=o8(rC_, sub), in0=o8(rS_, sub), in1=cbb[:], op=OP.add)))
                ropS(lambda sub: (lambda e: e.tensor_tensor(
                    out=o4_(rG4, sub), in0=rC_[:, 8 * sub:8 * sub + 8:2],
                    in1=rC_[:, 8 * sub + 1:8 * sub + 8:2], op=OP.add)))
                ropS(lambda sub: (lambda e: e.tensor_reduce(
                    out=rM_[:, 8 * sub:8 * sub + 1], in_=o4_(rG4, sub),
                    axis=AX, op=OP.max)))
                ropS(lambda sub: (lambda e: e.tensor_scalar(
                    out=o4_(rI_, sub), in0=o4_(rG4, sub),
                    scalar1=rM_[:, 8 * sub:8 * sub + 1], scalar2=None,
                    op0=OP.is_ge)))
                ropS(lambda sub: (lambda e: e.scalar_tensor_tensor(
                    out=o4_(rW_, sub), in0=o4_(rI_, sub), scalar=-BIG,
                    in1=o4_(rG4, sub), op0=OP.mult, op1=OP.add)))
                ropS(lambda sub: (lambda e: e.tensor_reduce(
                    out=rM_[:, 8 * sub + 1:8 * sub + 2], in_=o4_(rW_, sub),
                    axis=AX, op=OP.max)))
                ropS(lambda sub: (lambda e: e.tensor_scalar(
                    out=o4_(rI_, sub), in0=o4_(rG4, sub),
                    scalar1=rM_[:, 8 * sub + 1:8 * sub + 2], scalar2=None,
                    op0=OP.is_ge)))
                ropS(lambda sub: (lambda e: e.tensor_copy(
                    out=rSM[:, 8 * sub:8 * sub + 8:2], in_=o4_(rI_, sub))))
                ropS(lambda sub: (lambda e: e.tensor_copy(
                    out=rSM[:, 8 * sub + 1:8 * sub + 8:2], in_=o4_(rI_, sub))))
                ropS(lambda sub: (lambda e: e.tensor_tensor(
                    out=o8(rFC, sub), in0=o8(rSM, sub), in1=o8(rC_, sub),
                    op=OP.mult)))
                ropS(lambda sub: (lambda e: e.tensor_scalar(
                    out=o8(rSM, sub), in0=o8(rSM, sub), scalar1=1.0,
                    scalar2=None, op0=OP.subtract)))
                ropS(lambda sub: (lambda e: e.tensor_tensor(
                    out=o8(rB_, sub), in0=o8(rSM, sub), in1=idxE[:], op=OP.mult)))
                ropS(lambda sub: (lambda e: e.tensor_tensor(
                    out=o8(rFC, sub), in0=o8(rFC, sub), in1=o8(rB_, sub),
                    op=OP.add)))
                ropS(lambda sub: (lambda e: e.tensor_copy(
                    out=o8(rW_, sub), in_=o8(rFC, sub))))
                for rnd in range(3):
                    ropS(lambda sub: (lambda e: e.tensor_reduce(
                        out=rM_[:, 8 * sub + 2:8 * sub + 3], in_=o8(rW_, sub),
                        axis=AX, op=OP.max)))
                    ropS(lambda sub: (lambda e: e.tensor_scalar(
                        out=o8(rI_, sub), in0=o8(rW_, sub),
                        scalar1=rM_[:, 8 * sub + 2:8 * sub + 3], scalar2=None,
                        op0=OP.is_ge)))
                    ropS(lambda sub: (lambda e: e.scalar_tensor_tensor(
                        out=o8(rW_, sub), in0=o8(rI_, sub), scalar=-BIG,
                        in1=o8(rW_, sub), op0=OP.mult, op1=OP.add)))
                ropS(lambda sub: (lambda e: e.tensor_reduce(
                    out=rM_[:, 8 * sub + 3:8 * sub + 4], in_=o8(rW_, sub),
                    axis=AX, op=OP.max)))
                ropS(lambda sub: (lambda e: e.tensor_scalar(
                    out=o8(rI_, sub), in0=o8(rFC, sub),
                    scalar1=rM_[:, 8 * sub + 3:8 * sub + 4], scalar2=None,
                    op0=OP.is_ge)))
                ropS(lambda sub: (lambda e: e.tensor_tensor(
                    out=o8(rW_, sub), in0=o8(rI_, sub), in1=o8(rS_, sub),
                    op=OP.mult)))
                ropS(lambda sub: (lambda e: e.tensor_reduce(
                    out=rM_[:, 8 * sub + 4:8 * sub + 5], in_=o8(rW_, sub),
                    axis=AX, op=OP.add)))
                ropS(lambda sub: (lambda e: e.tensor_scalar(
                    out=rM_[:, 8 * sub + 5:8 * sub + 6],
                    in0=rM_[:, 8 * sub + 4:8 * sub + 5], scalar1=1e-20,
                    scalar2=None, op0=OP.add)))
                ropS(lambda sub: (lambda e: e.reciprocal(
                    out=rM_[:, 8 * sub + 6:8 * sub + 7],
                    in_=rM_[:, 8 * sub + 5:8 * sub + 6])))
                ropS(lambda sub: (lambda e: e.tensor_scalar(
                    out=rM_[:, 8 * sub + 7:8 * sub + 8],
                    in0=rM_[:, 8 * sub + 6:8 * sub + 7], scalar1=2.5,
                    scalar2=None, op0=OP.mult)))
                ropS(lambda sub: (lambda e: e.tensor_scalar(
                    out=o8(rCW, sub), in0=o8(rW_, sub),
                    scalar1=rM_[:, 8 * sub + 7:8 * sub + 8], scalar2=None,
                    op0=OP.mult)), inc="v")
                CWV = p.cnt["v"]
                for sub in range(4):
                    p.op("tensor", lambda e, sub=sub: e.matmul(
                        bank(5)[0:8, 0:128], rCW[:, 8 * sub:8 * sub + 8], i128F[:],
                        start=True, stop=True),
                        waits=[("v", CWV)] + p.bank_wait(5), inc="t")
                    p.op("scalar", lambda e, sub=sub, tt=tt: e.copy(
                        out=cwT[0:8, TT * tt + 128 * sub:TT * tt + 128 * sub + 128],
                        in_=bank(5)[0:8, 0:128]), waits=[("t", p.cnt["t"])], inc="s")
                    p.bank_mark(5, "s")
            CWS = p.cnt["s"]
            if debug and ST == 0:
                p.dma("gpsimd", dbg["d_h2T"], h2T[:, 0:NCH * TT], "y_st",
                      waits=[("v", p.cnt["v"])])
                p.dma("gpsimd", dbg["d_cwT"], cwT[:], "y_st", waits=[("s", CWS)])

            def queue_chain(gi):
                e_idx = ([NE] + list(range(NE)))[gi % 9]
                sl = gi % 2
                for step, (which, wt_dram, blk) in enumerate(
                        [(w_, d_, b_) for (w_, d_) in (("g", wg), ("u", wu),
                                                       ("d", wd))
                         for b_ in range(4)]):
                    dstb = {"g": wgB, "u": wuB, "d": wdB}[which]
                    if which != "d":
                        din_ap = wt_dram[e_idx, 256 * blk:256 * blk + 256, :] \
                            .rearrange("(k p) x -> p k x", k=2)
                        off = NCH * DI * sl + 1024 * blk
                    else:
                        din_ap = wt_dram[e_idx, 128 * blk:128 * blk + 128, :]
                        off = 4 * D * sl + D * blk
                    wchain.append((gi, sl, step, din_ap, dstb, off))

            def drain_step():
                if not wchain:
                    return
                gi, sl, step, din_ap, dstb, off = wchain.pop(0)
                s0 = 1024 * (step % 2)
                wt = list(stg_guard.get(step % 2, []))
                if step == 0 and wslot_cnt.get(sl) is not None:
                    wt.append(("t", wslot_cnt[sl]))
                dout_ap = stg[:, s0:s0 + 1024]
                if din_ap.shape != (128, 1024):
                    dout_ap = dout_ap.rearrange("p (k x) -> p k x", k=2)
                p.dma("gpsimd", dout_ap, din_ap, "w_ld", waits=wt)
                p.op("scalar", lambda e, dstb=dstb, off=off, s0=s0:
                     e.copy(out=dstb[:, off:off + 1024],
                            in_=stg[:, s0:s0 + 1024]),
                     waits=[("w_ld", p.cnt["w_ld"])], inc="s")
                stg_guard[step % 2] = [("s", p.cnt["s"])]
                wcast_done[gi] = p.cnt["s"]

            def drain_until(gi):
                while wchain and wchain[0][0] <= gi:
                    drain_step()

            for ei, e_idx in enumerate([NE] + list(range(NE))):
                gi = 9 * ST + ei
                sl = gi % 2
                if gi == 0:
                    queue_chain(0)
                drain_until(gi)
                WCAST_S = wcast_done[gi]
                if gi + 1 < 9 * NST:
                    queue_chain(gi + 1)

                for tt in range(2):
                    if e_idx != NE:
                        for b4 in range(4):
                            p.op("vector", lambda e, b4=b4, tt=tt, e_idx=e_idx:
                                 e.stream_shuffle(cwb[32 * b4:32 * b4 + 32, :],
                                                  cwT[0:32, TT * tt:TT * tt + TT],
                                                  mask=[e_idx] * 32),
                                 waits=[("s", CWS), ("t", p.cnt["t"])]
                                 if b4 == 0 else (),
                                 inc=("v" if b4 == 3 else None))
                    CWBV = p.cnt["v"]
                    mults = []
                    act_mark = {}
                    for ic in range(4):
                        gb = 2 * (ic % 2)
                        ub = gb + 1
                        for wsb, bki in ((wgB, gb), (wuB, ub)):
                            bk = bank(bki)
                            for hc in range(NCH):
                                p.op("tensor", lambda e, wsb=wsb, bk=bk, ic=ic,
                                     hc=hc, sl=sl, tt=tt: e.matmul(
                                         bk,
                                         wsb[:, NCH * DI * sl + DI * hc + 128 * ic:
                                             NCH * DI * sl + DI * hc + 128 * ic + 128],
                                         h2T[:, NCH * TT * tt + TT * hc:
                                             NCH * TT * tt + TT * hc + TT],
                                         start=(hc == 0), stop=(hc == NCH - 1)),
                                     waits=([("s", WCAST_S), ("v", h2rv[tt])] +
                                            p.bank_wait(bki))
                                     if hc == 0 else (),
                                     inc=("t" if hc == NCH - 1 else None))
                        stmp = (tA, tB, tC)[ic % 3]
                        p.op("scalar", lambda e, stmp=stmp, gb=gb: e.activation(
                            out=stmp[:], in_=bank(gb), func=AF.Silu),
                            waits=[("t", p.cnt["t"]), ("v", p.cnt["v"])], inc="s")
                        p.bank_mark(gb, "s")
                        drain_step()
                        mtmp = (tD, tE, tF)[ic % 3]
                        p.op("vector", lambda e, stmp=stmp, mtmp=mtmp, ub=ub:
                             e.tensor_tensor(out=mtmp[:], in0=stmp[:],
                                             in1=bank(ub), op=OP.mult),
                             waits=[("s", p.cnt["s"])], inc="v")
                        p.bank_mark(ub, "v")
                        drain_step()
                        mults.append((mtmp, ic))
                        if len(mults) == 2:
                            mt, mic = mults.pop(0)
                            if e_idx == NE:
                                p.op("vector", lambda e, mt=mt, mic=mic:
                                     e.tensor_copy(out=actB[:, TT * mic:TT * mic + TT],
                                                   in_=mt[:]), inc="v")
                            else:
                                p.op("vector", lambda e, mt=mt, mic=mic:
                                     e.tensor_tensor(out=actB[:, TT * mic:TT * mic + TT],
                                                     in0=mt[:], in1=cwb[:],
                                                     op=OP.mult),
                                     waits=[("v", CWBV)], inc="v")
                            act_mark[mic] = p.cnt["v"]
                    mt, mic = mults.pop(0)
                    if e_idx == NE:
                        p.op("vector", lambda e, mt=mt, mic=mic: e.tensor_copy(
                            out=actB[:, TT * mic:TT * mic + TT], in_=mt[:]), inc="v")
                    else:
                        p.op("vector", lambda e, mt=mt, mic=mic: e.tensor_tensor(
                            out=actB[:, TT * mic:TT * mic + TT], in0=mt[:],
                            in1=cwb[:], op=OP.mult), waits=[("v", CWBV)], inc="v")
                    act_mark[mic] = p.cnt["v"]
                    ACTV = p.cnt["v"]
                    for hc in range(NCH):
                        bi = 4 + hc % 4
                        bk = bank(bi)
                        for ic in range(4):
                            p.op("tensor", lambda e, hc=hc, ic=ic, sl=sl, bk=bk:
                                 e.matmul(bk,
                                          wdB[:, 4 * D * sl + D * ic + 128 * hc:
                                              4 * D * sl + D * ic + 128 * hc + 128],
                                          actB[:, TT * ic:TT * ic + TT],
                                          start=(ic == 0), stop=(ic == 3)),
                                 waits=([("v", act_mark[ic])] +
                                        (p.bank_wait(bi) if ic == 0 else [])),
                                 inc=("t" if ic == 3 else None))
                        wslot_cnt[sl] = p.cnt["t"]
                        dst = routed[:, NCH * TT * tt + TT * hc:
                                     NCH * TT * tt + TT * hc + TT]
                        if ei == 0:
                            p.op("vector", lambda e, dst=dst, bk=bk:
                                 e.tensor_copy(out=dst, in_=bk),
                                 waits=[("t", p.cnt["t"])], inc="v")
                        else:
                            p.op("vector", lambda e, dst=dst, bk=bk:
                                 e.tensor_tensor(out=dst, in0=dst, in1=bk,
                                                 op=OP.add),
                                 waits=[("t", p.cnt["t"])], inc="v")
                        p.bank_mark(bi, "v")
            ROUT = p.cnt["v"]
            for tt in range(2):
                T = 2 * ST + tt
                xw = [("v", ROUT), ("y_st", p.cnt["y_st"])]
                for c in range(NCH):
                    p.dma("sync", xio[:, TT * c:TT * c + TT],
                          xp[128 * c:128 * c + 128, TT * T:TT * T + TT], "x_ld",
                          waits=xw if c == 0 else ())
                XL = p.cnt["x_ld"]
                for c in range(NCH):
                    p.op("vector", lambda e, c=c, tt=tt: e.tensor_tensor(
                        out=xio[:, TT * c:TT * c + TT],
                        in0=xio[:, TT * c:TT * c + TT],
                        in1=routed[:, NCH * TT * tt + TT * c:
                                   NCH * TT * tt + TT * c + TT],
                        op=OP.add), waits=[("x_ld", XL)], inc="v")
                    p.dma("sync", yT[128 * c:128 * c + 128, TT * T:TT * T + TT],
                          xio[:, TT * c:TT * c + TT], "y_st",
                          waits=[("v", p.cnt["v"])])
        if p.cnt["y_st"]:
            p.op("sync", lambda e: e.nop(), waits=[("y_st", p.cnt["y_st"])])
        p.flush(blkB)
        blkB_es.close()
        esB.close()
    return nc


_QKV_PERM = None
def _qkv_perm():
    global _QKV_PERM
    if _QKV_PERM is None:
        perm = []
        for hp in range(NHP):
            for h in (2 * hp, 2 * hp + 1):
                perm.extend(range(h * 192, h * 192 + 64))
        for hp in range(NHP):
            for h in (2 * hp, 2 * hp + 1):
                perm.extend(range(h * 192 + 64, h * 192 + 128))
        for h in range(NH):
            perm.extend(range(h * 192 + 128, h * 192 + 192))
        _QKV_PERM = np.array(perm)
    return _QKV_PERM


_NC_CACHE = {}
def get_nc(debug=False):
    if debug not in _NC_CACHE:
        _NC_CACHE[debug] = build_nc(debug)
    return _NC_CACHE[debug]


def make_inputs(L0_attn_x, L0_attn_w_qkv, L0_attn_w_out, L0_attn_b_out,
                L0_ffn_gate_w, L0_ffn_correction_bias, L0_ffn_Wg, L0_ffn_Wu,
                L0_ffn_Wd, L0_ffn_Wg_s, L0_ffn_Wu_s, L0_ffn_Wd_s):
    f = np.float32
    wqkvT = np.ascontiguousarray(np.asarray(L0_attn_w_qkv, f).T[:, _qkv_perm()])
    woutT = np.ascontiguousarray(np.asarray(L0_attn_w_out, f).T)
    gwT = np.ascontiguousarray(np.asarray(L0_ffn_gate_w, f).T)
    bout = np.ascontiguousarray(np.asarray(L0_attn_b_out, f).reshape(NCH, 128).T)
    cbias = np.ascontiguousarray(np.asarray(L0_ffn_correction_bias, f).reshape(1, NE))
    idxeps = np.ascontiguousarray(
        np.tile((np.arange(1, NE + 1, dtype=f) * f(1e-30)).reshape(1, NE), (128, 1)))
    kk = np.arange(128).reshape(1, 128, 1)
    qq = np.arange(TT).reshape(1, 1, TT)
    jj = np.arange(4).reshape(4, 1, 1)
    mk = (128 * jj + kk <= qq).astype(f)
    mk = np.ascontiguousarray(np.tile(mk, (1, 1, 2)))
    onesf = np.ones((128, 128), f)
    hones = np.zeros((128, 2), f); hones[0:64, 0] = 1.0; hones[64:128, 1] = 1.0
    i8 = np.eye(8, dtype=f)
    i128 = np.eye(128, dtype=f)
    wg_ = np.ascontiguousarray(np.concatenate(
        [np.asarray(L0_ffn_Wg, f), np.asarray(L0_ffn_Wg_s, f)[None]], axis=0))
    wu_ = np.ascontiguousarray(np.concatenate(
        [np.asarray(L0_ffn_Wu, f), np.asarray(L0_ffn_Wu_s, f)[None]], axis=0))
    wd_ = np.ascontiguousarray(np.concatenate(
        [np.asarray(L0_ffn_Wd, f), np.asarray(L0_ffn_Wd_s, f)[None]], axis=0))
    shared = dict(wqkvT=wqkvT, woutT=woutT, gwT=gwT, bout=bout, cbias=cbias,
                  idxeps=idxeps, masks=mk, onesf=onesf, hones=hones, i8=i8,
                  i128=i128, wg=wg_, wu=wu_, wd=wd_)
    x = np.asarray(L0_attn_x, f)
    maps = []
    for core in range(NCORES):
        m = dict(shared)
        m["xT"] = np.ascontiguousarray(x[core].T)
        maps.append(m)
    return maps


def kernel(**inputs):
    maps = make_inputs(**inputs)
    nc = get_nc(debug=False)
    res = run_bass_kernel_spmd(nc, maps, core_ids=list(range(NCORES)))
    out = np.stack([np.ascontiguousarray(r["yT"].T) for r in res.results], axis=0)
    return out.astype(np.float32)


# version 15
# speedup vs baseline: 1.0345x; 1.0345x over previous
import contextlib
import numpy as np

import concourse.bass as bass
import concourse.mybir as mybir
from concourse.bass_utils import run_bass_kernel_spmd
from concourse.alu_op_type import AluOpType as OP

F32 = mybir.dt.float32
BF16 = mybir.dt.bfloat16
AF = mybir.ActivationFunctionType
AX = mybir.AxisListType.X

NCORES = 8
SEQ = 4096
D = 1024
NH = 16
NHP = 8
NT = 8
TT = 512
NCH = 8
NE = 8
DI = 512
NST = 4
SCALE = 0.125
EPS = 1e-5
BIG = 1.0e30
NKV = 3

ENGINES = ("sync", "gpsimd", "tensor", "vector", "scalar")


class Prog:

    def __init__(self, nc, es):
        self.nc = nc
        self.q = {e: [] for e in ENGINES}
        self.cnt = {}
        self.sems = {}
        self.es = es
        self.extra_first = {}
        self.banks = {}

    def sem(self, name):
        h = self.es.enter_context(self.nc.semaphore(name))
        self.sems[name] = h
        self.cnt[name] = 0
        return name

    def bank_wait(self, i):
        return list(self.banks.get(i, []))

    def bank_mark(self, i, semname):
        self.banks[i] = [(semname, self.cnt[semname])]

    def op(self, eng, fn, waits=(), inc=None, n=1):
        waits = list(waits)
        if eng in self.extra_first:
            waits += self.extra_first.pop(eng)
        for s, t in waits:
            assert t <= self.cnt[s], (eng, s, t, self.cnt[s])
        if inc is not None:
            self.cnt[inc] += n
        self.q[eng].append((fn, tuple(waits), inc, n))
        return self.cnt[inc] if inc is not None else None

    def dma(self, eng, out, in_, sem, waits=()):
        return self.op(eng, lambda e, o=out, i=in_: e.dma_start(out=o, in_=i),
                       waits, sem, 16)

    def flush(self, block):
        for name in ENGINES:
            ops = self.q[name]

            def body(h, ops=ops):
                for fn, waits, inc, n in ops:
                    for s, t in waits:
                        h.wait_ge(self.sems[s], t)
                    r = fn(h)
                    if inc is not None:
                        r.then_inc(self.sems[inc], n)

            getattr(block, name)(body)
        self.q = {e: [] for e in ENGINES}


DEV_NT = None
DEV_SKIP_B = False


def build_nc(debug=False):
    nc = bass.Bass("TRN2", target_bir_lowering=False)

    def din(name, shape, dt=F32):
        return nc.dram_tensor(name, shape, dt, kind="ExternalInput").ap()

    xT = din("xT", [D, SEQ])
    wqkvT = din("wqkvT", [D, 3 * D])
    woutT = din("woutT", [D, D])
    gwT = din("gwT", [D, NE])
    bout = din("bout", [128, NCH])
    cbias = din("cbias", [1, NE])
    idxeps = din("idxeps", [128, NE])
    masks = din("masks", [4, 128, 2 * TT])
    onesf = din("onesf", [128, 128])
    hones = din("hones", [128, 2])
    i8 = din("i8", [8, 8])
    i128 = din("i128", [128, 128])
    wg = din("wg", [NE + 1, D, DI])
    wu = din("wu", [NE + 1, D, DI])
    wd = din("wd", [NE + 1, DI, D])

    yT = nc.dram_tensor("yT", [D, SEQ], F32, kind="ExternalOutput").ap()
    xp = nc.dram_tensor("xp", [D, SEQ], F32, kind="Internal").ap()
    ksp = nc.dram_tensor("ksp", [SEQ // 128, 128, 1024], BF16, kind="Internal").ap()
    vsp = nc.dram_tensor("vsp", [SEQ // 128, 128, 1040], BF16, kind="Internal").ap()

    dbg = {}
    if debug:
        for name, shape, dt in [
            ("d_h1T", [128, NCH * TT], BF16), ("d_ont", [128, NHP * TT], BF16),
            ("d_h2T", [128, NCH * TT], BF16), ("d_cwT", [32, 2 * TT], F32),
            ("d_nb", [128, 4], F32), ("d_bbt", [32, 48], F32),
            ("d_rmx", [32, NHP], F32),
        ]:
            dbg[name] = nc.dram_tensor(name, shape, dt, kind="ExternalOutput").ap()

    es = contextlib.ExitStack()
    with es:
        def sb(name, shape, dt):
            return es.enter_context(nc.sbuf_tensor(name, shape, dt))

        xio = sb("xio", [128, NCH * TT], F32)
        onesF = sb("onesF", [128, 128], F32)
        onesB = sb("onesB", [128, 128], BF16)
        i8F = sb("i8F", [8, 8], F32)
        i128F = sb("i128F", [128, 128], F32)
        boutF = sb("boutF", [128, NCH], F32)
        crow = sb("crow", [32, 16], F32)
        gcol = sb("gcol", [128, NE], F32)
        cbb = sb("cbb", [128, NE], F32)
        idxE = sb("idxE", [128, NE], F32)
        gwF = sb("gwF", [128, NCH * NE], F32)
        stg = sb("stg", [128, 2 * 1024], F32)
        r_q = sb("r_q", [1, TT], F32)
        r_mean = sb("r_mean", [1, TT], F32)
        r_q2 = sb("r_q2", [1, TT], F32)
        r_v = sb("r_v", [1, TT], F32)
        r_ve = sb("r_ve", [1, TT], F32)
        r_ln = sb("r_ln", [1, TT], F32)
        r_rstd = sb("r_rstd", [1, TT], F32)
        r_mr = sb("r_mr", [1, TT], F32)
        tA = sb("tA", [128, TT], F32)
        tB = sb("tB", [128, TT], F32)
        tC = sb("tC", [128, TT], F32)
        tD = sb("tD", [128, TT], F32)
        tE = sb("tE", [128, TT], F32)
        tF = sb("tF", [128, TT], F32)
        xbb = sb("xbb", [128, 3 * TT], BF16)
        xsq = sb("xsq", [128, 3 * TT], BF16)
        maskB = sb("maskB", [128, 4 * 2 * TT], BF16)
        lb = sb("lb", [128, 2 * TT], F32)
        h1T = sb("h1T", [128, NCH * TT], BF16)
        honesB = sb("honesB", [128, 2], BF16)
        rmxQ = sb("rmxQ", [32, NHP], F32)
        rmxK = sb("rmxK", [32, NHP], F32)
        sqb = sb("sqb", [128, 2 * TT], BF16)
        bbt = sb("bbt", [32, 48], F32)
        negBc = sb("negBc", [128, 4], F32)
        gtm = sb("gtm", [128, 4 * 10], F32)
        def rt(name):
            return sb(name, [128, 4 * NE], F32)
        rA_, rB_, rL_, rS_, rC_ = rt("rtA"), rt("rtB"), rt("rtL"), rt("rtS"), rt("rtC")
        rG4 = sb("rG4", [128, 4 * 4], F32)
        rM_ = sb("rM_", [128, 4 * 8], F32)
        rI_, rW_, rSM, rFC, rCW = rt("rtI"), rt("rtW"), rt("rtSM"), rt("rtFC"), rt("rtCW")
        glbuf = sb("glbuf", [8, TT], F32)
        cwT = sb("cwT", [32, 2 * TT], F32)

        PSA = es.enter_context(nc.psum_tensor("psA", [128, 2048], F32))
        PSB = es.enter_context(nc.psum_tensor("psB", [128, 2048], F32))
        PS = [PSA[:, 0:1024], PSA[:, 1024:2048], PSB[:, 0:1024], PSB[:, 1024:2048]]
        def bank(i):
            return (PSA if i < 4 else PSB)[:, 512 * (i % 4):512 * (i % 4) + 512]

        p = Prog(nc, es)
        for s in ("c_ld", "x_ld", "kv_ld", "w_ld", "kv_st", "xp_st", "sh_st",
                  "y_st", "t", "s", "v", "g"):
            p.sem(s)

        esA = es.enter_context(contextlib.ExitStack())
        def sbA(name, shape, dt):
            return esA.enter_context(nc.sbuf_tensor(name, shape, dt))

        wqkvB = sbA("wqkvB", [128, NCH * 3072], BF16)
        woutB = sbA("woutB", [128, NCH * 1024], BF16)
        Qp = sbA("Qp", [128, NHP * TT], BF16)
        Kc = sbA("Kc", [128, NHP * TT], BF16)
        Vc = sbA("Vc", [128, 4 * 1040], BF16)
        kvK = sbA("kvK", [128, NKV * 1024], BF16)
        kvV = sbA("kvV", [128, NKV * 1040], BF16)
        Pb = sbA("Pb", [128, 4 * 1024], BF16)
        OnT = sbA("OnT", [128, NHP * TT], BF16)
        Ot = sbA("Ot", [64, 2 * TT], BF16)

        blkA_es = contextlib.ExitStack()
        blkA = blkA_es.enter_context(nc.Block())

        for dst, src in [(onesF[:], onesf), (i8F[:], i8), (i128F[:], i128),
                         (boutF[:], bout), (idxE[:], idxeps),
                         (crow[0:1, 8:16], cbias)]:
            p.dma("sync", dst, src, "c_ld")
        for c in range(NCH):
            p.dma("sync", gwF[:, 8 * c:8 * c + 8], gwT[128 * c:128 * c + 128, :], "c_ld")
        p.dma("sync", stg[:, 0:2], hones, "c_ld")
        p.op("scalar", lambda e: e.copy(out=onesB[:], in_=onesF[:]),
             waits=[("c_ld", 16)], inc="s")
        p.op("scalar", lambda e: e.copy(out=honesB[:], in_=stg[:, 0:2]),
             waits=[("c_ld", p.cnt["c_ld"])], inc="s")
        ONES_S = p.cnt["s"]
        p.op("vector", lambda e: e.memset(rmxQ[:], 0.0), inc="v")
        p.op("vector", lambda e: e.memset(rmxK[:], 0.0), inc="v")
        for c in range(NCH):
            p.op("tensor", lambda e, c=c: e.matmul(
                bank(0)[0:1, 0:8], onesF[:, 0:1], gwF[:, 8 * c:8 * c + 8],
                start=(c == 0), stop=(c == NCH - 1)),
                waits=[("c_ld", 16 * (7 + NCH))], inc="t")
        p.op("scalar", lambda e: e.copy(out=crow[0:1, 0:8], in_=bank(0)[0:1, 0:8]),
             waits=[("t", p.cnt["t"])], inc="s")
        p.bank_mark(0, "s")
        SROW = p.cnt["s"]
        for b in range(4):
            p.op("vector", lambda e, b=b: e.stream_shuffle(
                gcol[32 * b:32 * b + 32, :], crow[:, 0:8], mask=[0] * 32),
                waits=[("s", SROW), ("c_ld", 16 * 6)])
            p.op("vector", lambda e, b=b: e.stream_shuffle(
                cbb[32 * b:32 * b + 32, :], crow[:, 8:16], mask=[0] * 32),
                waits=[("s", SROW)], inc="v")
        p.op("vector", lambda e: e.memset(
            Vc[:].rearrange("p (k h x) -> p k h x", k=4, h=NH)[:, :, :, 64:65], 1.0),
            inc="v")

        def ln_tile(out_h1, out_base, xwaits):
            xwaits = list(xwaits)
            t_entry = p.cnt["t"]
            slot_guard = {i: [("t", t_entry)] for i in range(3)}
            for c in range(NCH):
                sl = (c % 3) * TT
                xa = xio[:, TT * c:TT * c + TT]
                w0 = (xwaits if c == 0 else []) + slot_guard.get(c % 3, [])
                p.op("scalar", lambda e, xa=xa, sl=sl:
                     e.copy(out=xbb[:, sl:sl + TT], in_=xa), waits=w0, inc="s")
                p.op("vector", lambda e, xa=xa, sl=sl:
                     e.tensor_tensor(out=xsq[:, sl:sl + TT], in0=xa, in1=xa,
                                     op=OP.mult), waits=w0, inc="v")
                p.op("tensor", lambda e, c=c, sl=sl: e.matmul(
                    bank(0)[0:1, 0:TT], onesB[:, 0:1], xbb[:, sl:sl + TT],
                    start=(c == 0), stop=(c == NCH - 1)),
                    waits=[("s", p.cnt["s"])] + (p.bank_wait(0) if c == 0 else []))
                p.op("tensor", lambda e, c=c, sl=sl: e.matmul(
                    bank(1)[0:1, 0:TT], onesB[:, 0:1], xsq[:, sl:sl + TT],
                    start=(c == 0), stop=(c == NCH - 1)),
                    waits=[("v", p.cnt["v"])] + (p.bank_wait(1) if c == 0 else []),
                    inc="t")
                slot_guard[c % 3] = [("t", p.cnt["t"])]
            TSUM = p.cnt["t"]
            p.op("scalar", lambda e: e.activation(out=r_q[:], in_=bank(0)[0:1, 0:TT],
                 func=AF.Copy, scale=1.0 / 32.0), waits=[("t", TSUM)], inc="s")
            p.op("scalar", lambda e: e.activation(out=r_mean[:], in_=bank(0)[0:1, 0:TT],
                 func=AF.Copy, scale=1.0 / 1024.0), inc="s")
            p.bank_mark(0, "s")
            p.op("vector", lambda e: e.tensor_tensor(out=r_q2[:], in0=r_q[:],
                 in1=r_q[:], op=OP.mult), waits=[("s", p.cnt["s"])], inc="v")
            p.op("vector", lambda e: e.tensor_tensor(out=r_v[:], in0=bank(1)[0:1, 0:TT],
                 in1=r_q2[:], op=OP.subtract), waits=[("v", p.cnt["v"]),
                 ("t", TSUM)], inc="v")
            p.bank_mark(1, "v")
            p.op("vector", lambda e: e.tensor_scalar(out=r_ve[:], in0=r_v[:],
                 scalar1=1.0 / 1024.0, scalar2=EPS, op0=OP.mult, op1=OP.add),
                 waits=[("v", p.cnt["v"])], inc="v")
            p.op("scalar", lambda e: e.activation(out=r_ln[:], in_=r_ve[:],
                 func=AF.Ln), waits=[("v", p.cnt["v"])], inc="s")
            p.op("scalar", lambda e: e.activation(out=r_rstd[:], in_=r_ln[:],
                 func=AF.Exp, scale=-0.5), waits=[("s", p.cnt["s"])], inc="s")
            SRS = p.cnt["s"]
            p.op("vector", lambda e: e.tensor_tensor(out=r_mr[:], in0=r_mean[:],
                 in1=r_rstd[:], op=OP.mult), waits=[("s", SRS)], inc="v")
            VMR = p.cnt["v"]
            p.op("tensor", lambda e: e.matmul(bank(2)[:], onesF[0:1, :],
                 r_mean[:], start=True, stop=True),
                 waits=[("s", SRS)] + p.bank_wait(2))
            p.op("tensor", lambda e: e.matmul(bank(3)[:], onesF[0:1, :],
                 r_rstd[:], start=True, stop=True),
                 waits=p.bank_wait(3), inc="t")
            TBC = p.cnt["t"]
            pend = []
            for c in range(NCH):
                xa = xio[:, TT * c:TT * c + TT]
                tmp = (tA, tB, tC)[c % 3]
                p.op("vector", lambda e, xa=xa, tmp=tmp: e.tensor_tensor(
                    out=tmp[:], in0=xa, in1=bank(2)[:], op=OP.subtract),
                    waits=[("t", TBC)], inc="v")
                pend.append((tmp, c))
                if len(pend) == 2:
                    tm, cc = pend.pop(0)
                    p.op("vector", lambda e, tm=tm, cc=cc: e.tensor_tensor(
                        out=out_h1[:, out_base + TT * cc:out_base + TT * cc + TT],
                        in0=tm[:], in1=bank(3)[:], op=OP.mult), inc="v")
            tm, cc = pend.pop(0)
            p.op("vector", lambda e, tm=tm, cc=cc: e.tensor_tensor(
                out=out_h1[:, out_base + TT * cc:out_base + TT * cc + TT],
                in0=tm[:], in1=bank(3)[:], op=OP.mult), inc="v")
            p.bank_mark(2, "v")
            p.bank_mark(3, "v")
            return p.cnt["v"], VMR

        for c in range(NCH):
            p.dma("sync", xio[:, TT * c:TT * c + TT],
                  xT[128 * c:128 * c + 128, 0:TT], "x_ld")
        XL0 = p.cnt["x_ld"]
        H1R_0, _ = ln_tile(h1T, 0, [("x_ld", XL0), ("s", ONES_S)])

        cast_hist = []
        stage_jobs = []
        for j in range(4):
            stage_jobs.append((maskB[:, 1024 * j:1024 * j + 1024], masks[j]))
        for c in range(NCH):
            for h in range(3):
                stage_jobs.append((
                    wqkvB[:, 3072 * c + 1024 * h:3072 * c + 1024 * h + 1024],
                    wqkvT[128 * c:128 * c + 128, 1024 * h:1024 * h + 1024]))
        for c in range(NCH):
            stage_jobs.append((woutB[:, 1024 * c:1024 * c + 1024],
                               woutT[128 * c:128 * c + 128, :]))
        for k, (dst, hsrc) in enumerate(stage_jobs):
            sl = 1024 * (k % 2)
            wt = [("s", cast_hist[k - 2])] if k >= 2 else [("s", ONES_S)]
            q = "sync" if k % 2 == 0 else "gpsimd"
            ld = p.dma(q, stg[:, sl:sl + 1024], hsrc, "c_ld", waits=wt)
            p.op("scalar", lambda e, dst=dst, sl=sl:
                 e.copy(out=dst, in_=stg[:, sl:sl + 1024]),
                 waits=[("c_ld", p.cnt["c_ld"])], inc="s")
            cast_hist.append(p.cnt["s"])
        CAST_DONE = p.cnt["s"]

        kv_loads = 0
        stream_consumed = {}
        ot_guard = {}
        rotg = 0

        for T in range(DEV_NT or NT):
            if T == 0:
                H1R = H1R_0
            else:
                xw = [("xp_st", p.cnt["xp_st"])]
                for c in range(NCH):
                    p.dma("sync", xio[:, TT * c:TT * c + TT],
                          xT[128 * c:128 * c + 128, TT * T:TT * T + TT], "x_ld",
                          waits=xw if c == 0 else ())
                XL = p.cnt["x_ld"]
                H1R, _ = ln_tile(h1T, 0, [("x_ld", XL)])

            for hp in range(NHP):
                for base, dst in ((0, Qp), (1024, Kc)):
                    bi = rotg % 8; rotg += 1
                    bk = bank(bi)
                    for c in range(NCH):
                        p.op("tensor", lambda e, c=c, hp=hp, base=base, bk=bk: e.matmul(
                            bk, wqkvB[:, 3072 * c + base + 128 * hp:
                                      3072 * c + base + 128 * hp + 128],
                            h1T[:, TT * c:TT * c + TT],
                            start=(c == 0), stop=(c == NCH - 1)),
                            waits=([("v", H1R)] + p.bank_wait(bi) +
                                   ([("s", CAST_DONE)] if (T == 0 and hp == 0)
                                    else [])) if c == 0 else (),
                            inc=("t" if c == NCH - 1 else None))
                    p.op("scalar", lambda e, dst=dst, hp=hp, bk=bk: e.copy(
                        out=dst[:, TT * hp:TT * hp + TT], in_=bk),
                        waits=[("t", p.cnt["t"]), ("kv_st", p.cnt["kv_st"])], inc="s")
                    p.bank_mark(bi, "s")
                    rmx = rmxQ if base == 0 else rmxK
                    ssl = TT * (0 if base == 0 else 1)
                    p.op("vector", lambda e, dst=dst, hp=hp, ssl=ssl: e.tensor_tensor(
                        out=sqb[:, ssl:ssl + TT], in0=dst[:, TT * hp:TT * hp + TT],
                        in1=dst[:, TT * hp:TT * hp + TT], op=OP.mult),
                        waits=[("s", p.cnt["s"])], inc="v")
                    bi2 = rotg % 8; rotg += 1
                    p.op("tensor", lambda e, ssl=ssl, bk2=bank(bi2): e.matmul(
                        bk2[0:2, 0:TT], honesB[:], sqb[:, ssl:ssl + TT],
                        start=True, stop=True),
                        waits=[("v", p.cnt["v"])] + p.bank_wait(bi2), inc="t")
                    rsl = 10 + (rotg % 2)
                    p.op("vector", lambda e, bk2=bank(bi2), rsl=rsl: e.tensor_reduce(
                        out=bbt[0:2, rsl:rsl + 1], in_=bk2[0:2, 0:TT], axis=AX,
                        op=OP.max), waits=[("t", p.cnt["t"])], inc="v")
                    p.bank_mark(bi2, "v")
                    p.op("vector", lambda e, rmx=rmx, hp=hp, rsl=rsl: e.tensor_tensor(
                        out=rmx[0:2, hp:hp + 1], in0=rmx[0:2, hp:hp + 1],
                        in1=bbt[0:2, rsl:rsl + 1], op=OP.max),
                        waits=[("v", p.cnt["v"])], inc="v")
            QKE = p.cnt["s"]
            for s4 in range(4):
                for h4 in range(2):
                    bi = rotg % 8; rotg += 1
                    bk = bank(bi)
                    for c in range(NCH):
                        p.op("tensor", lambda e, c=c, s4=s4, h4=h4, bk=bk: e.matmul(
                            bk, h1T[:, TT * c + 128 * s4:TT * c + 128 * s4 + 128],
                            wqkvB[:, 3072 * c + 2048 + 512 * h4:
                                  3072 * c + 2048 + 512 * h4 + 512],
                            start=(c == 0), stop=(c == NCH - 1)),
                            waits=p.bank_wait(bi) if c == 0 else (),
                            inc=("t" if c == NCH - 1 else None))
                    p.op("scalar", lambda e, s4=s4, h4=h4, bk=bk: e.copy(
                        out=Vc[:, 1040 * s4 + 520 * h4:1040 * s4 + 520 * h4 + 520]
                            .rearrange("p (h x) -> p h x", h=NH // 2)[:, :, 0:64],
                        in_=bk.rearrange("p (h x) -> p h x", h=NH // 2)),
                        waits=[("t", p.cnt["t"]), ("kv_st", p.cnt["kv_st"])], inc="s")
                    p.bank_mark(bi, "s")
            PROD_S = p.cnt["s"]
            for j in range(4):
                p.dma("sync", ksp[4 * T + j].rearrange("p (h x) -> p h x", h=NHP),
                      Kc[:].rearrange("p (h k x) -> p k h x", h=NHP, k=4)[:, j],
                      "kv_st", waits=[("s", QKE)])
                p.dma("sync", vsp[4 * T + j], Vc[:, 1040 * j:1040 * j + 1040],
                      "kv_st", waits=[("s", PROD_S)])

            nkc = 4 * T + 4
            W0 = [("s", p.cnt["s"]), ("v", p.cnt["v"])]
            def _c(sw, k):
                return bbt[0:1, 12 * sw + k:12 * sw + k + 1]
            for sw in range(4):
        
                p.op("vector", lambda e, sw=sw: e.tensor_tensor(
                    out=bbt[0:2, 12 * sw + 0:12 * sw + 1],
                    in0=rmxQ[0:2, 2 * sw:2 * sw + 1],
                    in1=rmxQ[0:2, 2 * sw + 1:2 * sw + 2], op=OP.max),
                    waits=W0 if sw == 0 else ())
                p.op("vector", lambda e, sw=sw: e.tensor_tensor(
                    out=bbt[0:2, 12 * sw + 1:12 * sw + 2],
                    in0=rmxK[0:2, 2 * sw:2 * sw + 1],
                    in1=rmxK[0:2, 2 * sw + 1:2 * sw + 2], op=OP.max), inc="v")
            for sw in range(4):
                p.op("vector", lambda e, sw=sw: e.stream_shuffle(
                    bbt[0:32, 12 * sw + 2:12 * sw + 4],
                    bbt[0:32, 12 * sw + 0:12 * sw + 2], mask=[1] * 32),
                    waits=[("v", p.cnt["v"])] if sw == 0 else (), inc="v")
            for sw in range(4):
                p.op("vector", lambda e, sw=sw: e.tensor_tensor(
                    out=bbt[0:1, 12 * sw + 4:12 * sw + 6],
                    in0=bbt[0:1, 12 * sw + 0:12 * sw + 2],
                    in1=bbt[0:1, 12 * sw + 2:12 * sw + 4], op=OP.max),
                    waits=[("v", p.cnt["v"])] if sw == 0 else (), inc="v")
            for sw in range(4):
                p.op("vector", lambda e, sw=sw: e.tensor_tensor(
                    out=_c(sw, 6), in0=_c(sw, 4), in1=_c(sw, 5), op=OP.mult),
                    waits=[("v", p.cnt["v"])] if sw == 0 else (), inc="v")
            for sw in range(4):
                p.op("scalar", lambda e, sw=sw: e.activation(
                    out=_c(sw, 7), in_=_c(sw, 6), func=AF.Ln),
                    waits=[("v", p.cnt["v"])] if sw == 0 else (), inc="s")
            for sw in range(4):
                p.op("scalar", lambda e, sw=sw: e.activation(
                    out=_c(sw, 8), in_=_c(sw, 7), func=AF.Exp, scale=0.5),
                    waits=[("s", p.cnt["s"])] if sw == 0 else (), inc="s")
            for sw in range(4):
                p.op("vector", lambda e, sw=sw: e.tensor_scalar(
                    out=_c(sw, 9), in0=_c(sw, 8), scalar1=-0.125,
                    scalar2=85.0, op0=OP.mult, op1=OP.add),
                    waits=[("s", p.cnt["s"])] if sw == 0 else (), inc="v")
            for sw in range(4):
                p.op("vector", lambda e, sw=sw: e.tensor_scalar_min(
                    out=_c(sw, 10), in0=_c(sw, 9), scalar1=0.0),
                    waits=[("v", p.cnt["v"])] if sw == 0 else (), inc="v")
            NBMIN = p.cnt["v"]
            for b4 in range(4):
                for sw in range(4):
                    p.op("vector", lambda e, b4=b4, sw=sw: e.stream_shuffle(
                        negBc[32 * b4:32 * b4 + 32, sw:sw + 1],
                        bbt[0:32, 12 * sw + 10:12 * sw + 11], mask=[0] * 32),
                        waits=[("v", NBMIN)] if (b4 == 0 and sw == 0) else (),
                        inc=("v" if (b4 == 3 and sw == 3) else None))
            NB_V = p.cnt["v"]
            for sw in range(4):
                pairs = (2 * sw, 2 * sw + 1)
                pend = []

                def emit_pv(info):
                    islot = info["i"] % 4
                    pi = info["pi"]
                    for hb in range(2):
                        h = 2 * pairs[pi] + hb
                        p.op("tensor", lambda e, pi=pi, islot=islot, h=h, hb=hb,
                             info=info: e.matmul(
                                 PSB[0:65, 1024 * pi + 512 * hb:
                                     1024 * pi + 512 * hb + 512],
                                 info["vsrc"][:, 65 * h:65 * h + 65],
                                 Pb[:, 1024 * islot + 512 * hb:
                                    1024 * islot + 512 * hb + 512],
                                 start=info["start"], stop=info["stop"]),
                             waits=info["wait"] if hb == 0 else (),
                             inc=("t" if hb == 1 else None))
                    if info["load_idx"] is not None:
                        stream_consumed[info["load_idx"]] = p.cnt["t"]

                for kc in range(nkc):
                    diag = kc >= 4 * T
                    if not diag:
                        slot = kv_loads % NKV
                        wt = [("kv_st", p.cnt["kv_st"])]
                        if kv_loads >= NKV:
                            wt.append(("t", stream_consumed[kv_loads - NKV]))
                        p.dma("gpsimd", kvK[:, 1024 * slot:1024 * slot + 1024],
                              ksp[kc], "kv_ld", waits=wt)
                        p.dma("gpsimd", kvV[:, 1040 * slot:1040 * slot + 1040],
                              vsp[kc], "kv_ld")
                        this_load = p.cnt["kv_ld"]
                        kQ = kvK[:, 1024 * slot:1024 * slot + 1024]
                        vQ = kvV[:, 1040 * slot:1040 * slot + 1040]
                        ks = [kQ[:, 128 * pr:128 * pr + 128] for pr in pairs]
                        vs = [vQ, vQ]
                        kv_loads += 1
                    else:
                        j = kc - 4 * T
                        ks = [Kc[:, TT * pr + 128 * j:TT * pr + 128 * j + 128]
                              for pr in pairs]
                        vs = [Vc[:, 1040 * j:1040 * j + 1040]] * 2
                        this_load = None
                    for pi in range(2):
                        i = 2 * kc + pi
                        if len(pend) == 2:
                            emit_pv(pend.pop(0))
                        wt = p.bank_wait(2 * pi) + p.bank_wait(2 * pi + 1)
                        if i == 0:
                            wt.append(("s", PROD_S))
                        if pi == 0 and this_load is not None:
                            wt.append(("kv_ld", this_load))
                        pr = pairs[pi]
                        ksrc = ks[pi]
                        p.op("tensor", lambda e, pi=pi, ksrc=ksrc, pr=pr: e.matmul(
                            PSA[:, 1024 * pi:1024 * pi + 512], ksrc[0:64, :],
                            Qp[0:64, TT * pr:TT * pr + TT],
                            start=True, stop=True, tile_position=(0, 0)), waits=wt)
                        p.op("tensor", lambda e, pi=pi, ksrc=ksrc, pr=pr: e.matmul(
                            PSA[:, 1024 * pi + 512:1024 * pi + 1024], ksrc[64:128, :],
                            Qp[64:128, TT * pr:TT * pr + TT],
                            start=True, stop=True, tile_position=(64, 0)), inc="t")
                        SMM = p.cnt["t"]
                        p.op("scalar", lambda e, pi=pi, islot=i % 4, sw=sw:
                             e.activation(
                                 out=Pb[:, 1024 * islot:1024 * islot + 1024],
                                 in_=PSA[:, 1024 * pi:1024 * pi + 1024], func=AF.Exp,
                                 scale=SCALE, bias=negBc[:, sw:sw + 1]),
                             waits=[("t", SMM)] + ([("v", NB_V)] if i == 0 else []),
                             inc="s")
                        p.bank_mark(2 * pi, "s")
                        p.bank_mark(2 * pi + 1, "s")
                        if diag:
                            j = kc - 4 * T
                            p.op("vector", lambda e, islot=i % 4, j=j: e.tensor_tensor(
                                out=Pb[:, 1024 * islot:1024 * islot + 1024],
                                in0=Pb[:, 1024 * islot:1024 * islot + 1024],
                                in1=maskB[:, 1024 * j:1024 * j + 1024], op=OP.mult),
                                waits=[("s", p.cnt["s"])], inc="v")
                            pv_wait = [("v", p.cnt["v"])]
                        else:
                            pv_wait = [("s", p.cnt["s"])]
                        if kc == 0:
                            pv_wait = pv_wait + p.bank_wait(4 + 2 * pi) + \
                                      p.bank_wait(5 + 2 * pi)
                        pend.append(dict(i=i, pi=pi, vsrc=vs[pi], wait=pv_wait,
                                         start=(kc == 0), stop=(kc == nkc - 1),
                                         load_idx=(kv_loads - 1)
                                         if (not diag and pi == 1) else None))

                for pi in range(2):
                    emit_pv(pend.pop(0))
                    PVD = p.cnt["t"]
                    pr = pairs[pi]
                    p.op("scalar", lambda e, pi=pi, pr=pr: e.copy(
                        out=OnT[0:64, TT * pr:TT * pr + TT],
                        in_=PSB[0:64, 1024 * pi:1024 * pi + 512]),
                        waits=[("t", PVD)], inc="s")
                    p.op("scalar", lambda e, pi=pi: e.copy(
                        out=Ot[:, TT * pi:TT * pi + TT],
                        in_=PSB[0:64, 1024 * pi + 512:1024 * pi + 1024]),
                        waits=ot_guard.get(pi, []), inc="s")
                    SCP = p.cnt["s"]
                    p.dma("sync", OnT[64:128, TT * pr:TT * pr + TT],
                          Ot[:, TT * pi:TT * pi + TT], "sh_st",
                          waits=[("s", SCP)])
                    SHD = p.cnt["sh_st"]
                    ot_guard[pi] = [("sh_st", SHD)]
                    p.op("vector", lambda e, pi=pi: e.tensor_scalar(
                        out=stg[:, 1024 * pi:1024 * pi + 1024][64:65, :],
                        in0=PSB[64:65, 1024 * pi:1024 * pi + 1024],
                        scalar1=1e-37, scalar2=None, op0=OP.add),
                        waits=[("t", PVD)], inc="v")
                    p.banks[4 + 2 * pi] = [("s", SCP), ("v", p.cnt["v"])]
                    p.banks[5 + 2 * pi] = [("s", SCP), ("v", p.cnt["v"])]
                    p.op("vector", lambda e, pi=pi: e.reciprocal(
                        out=stg[:, 1024 * pi:1024 * pi + 1024][64:65, :],
                        in_=stg[:, 1024 * pi:1024 * pi + 1024][64:65, :]),
                        waits=[("v", p.cnt["v"])], inc="v")
                    for b4 in range(4):
                        p.op("vector", lambda e, pi=pi, b4=b4: e.stream_shuffle(
                            lb[32 * b4:32 * b4 + 32, 0:1024],
                            stg[:, 1024 * pi:1024 * pi + 1024][64:96, :],
                            mask=[0] * 32),
                            waits=[("v", p.cnt["v"])] if b4 == 0 else (),
                            inc=("v" if b4 == 3 else None))
                    p.op("vector", lambda e, pr=pr: e.tensor_tensor(
                        out=OnT[0:64, TT * pr:TT * pr + TT],
                        in0=OnT[0:64, TT * pr:TT * pr + TT],
                        in1=lb[0:64, 0:TT], op=OP.mult),
                        waits=[("s", SCP)])
                    p.op("vector", lambda e, pr=pr: e.tensor_tensor(
                        out=OnT[64:128, TT * pr:TT * pr + TT],
                        in0=OnT[64:128, TT * pr:TT * pr + TT],
                        in1=lb[64:128, TT:2 * TT], op=OP.mult),
                        waits=[("sh_st", SHD)], inc="v")

            ALLON = p.cnt["v"]
            for fc in range(NCH):
                bi = rotg % 8; rotg += 1
                bk = bank(bi)
                for hp in range(NHP):
                    p.op("tensor", lambda e, fc=fc, hp=hp, bk=bk: e.matmul(
                        bk, woutB[:, 1024 * hp + 128 * fc:1024 * hp + 128 * fc + 128],
                        OnT[:, TT * hp:TT * hp + TT],
                        start=(hp == 0), stop=(hp == NHP - 1)),
                        waits=([("v", ALLON)] + p.bank_wait(bi))
                        if hp == 0 else (),
                        inc=("t" if hp == NHP - 1 else None))
                p.op("vector", lambda e, bk=bk, fc=fc: e.scalar_tensor_tensor(
                    out=xio[:, TT * fc:TT * fc + TT], in0=bk,
                    scalar=boutF[:, fc:fc + 1],
                    in1=xio[:, TT * fc:TT * fc + TT], op0=OP.add, op1=OP.add),
                    waits=[("t", p.cnt["t"])], inc="v")
                p.bank_mark(bi, "v")
                p.dma("sync", xp[128 * fc:128 * fc + 128, TT * T:TT * T + TT],
                      xio[:, TT * fc:TT * fc + TT], "xp_st",
                      waits=[("v", p.cnt["v"])])
            if debug and T == 0:
                p.dma("gpsimd", dbg["d_h1T"], h1T[:], "y_st", waits=[("v", H1R)])
                p.dma("gpsimd", dbg["d_ont"], OnT[:], "y_st",
                      waits=[("v", ALLON)])
                p.dma("gpsimd", dbg["d_nb"], negBc[:], "y_st", waits=[("v", NB_V)])
                p.dma("gpsimd", dbg["d_bbt"], bbt[:], "y_st")
                p.dma("gpsimd", dbg["d_rmx"], rmxQ[:], "y_st")

        FIN = {k: p.cnt[k] for k in p.cnt}
        p.flush(blkA)
        blkA_es.close()
        esA.close()

        esB = es.enter_context(contextlib.ExitStack())
        def sbB(name, shape, dt):
            return esB.enter_context(nc.sbuf_tensor(name, shape, dt))

        h2T = sbB("h2T", [128, 2 * NCH * TT], BF16)
        wgB = sbB("wgB", [128, 2 * NCH * DI], BF16)
        wuB = sbB("wuB", [128, 2 * NCH * DI], BF16)
        wdB = sbB("wdB", [128, 2 * 4 * D], BF16)
        actB = sbB("actB", [128, 4 * TT], BF16)
        routed = sbB("routed", [128, 2 * NCH * TT], F32)
        cwb = sbB("cwb", [128, TT], F32)

        blkB_es = contextlib.ExitStack()
        blkB = blkB_es.enter_context(nc.Block())
        bw = ([(k, FIN[k]) for k in ("t", "s", "v")] +
              [(k, FIN[k]) for k in ("c_ld", "x_ld", "kv_ld", "kv_st",
                                     "xp_st", "sh_st", "y_st")])
        p.extra_first = {e: list(bw) for e in ENGINES}

        wslot_cnt = {}
        stg_guard = {}
        wchain = []
        wcast_done = {}
        for ST in range(0 if DEV_SKIP_B else NST):
            h2rv = {}
            for tt in range(2):
                T = 2 * ST + tt
                xw = [("y_st", p.cnt["y_st"]), ("t", p.cnt["t"]),
                      ("v", p.cnt["v"]), ("s", p.cnt["s"])]
                for c in range(NCH):
                    p.dma("sync", xio[:, TT * c:TT * c + TT],
                          xp[128 * c:128 * c + 128, TT * T:TT * T + TT], "x_ld",
                          waits=xw if c == 0 else ())
                XL = p.cnt["x_ld"]
                H2R, VMR = ln_tile(h2T, NCH * TT * tt, [("x_ld", XL)])
                h2rv[tt] = H2R
                for c in range(NCH):
                    p.op("tensor", lambda e, c=c: e.matmul(
                        bank(4)[0:8, :], gwF[:, 8 * c:8 * c + 8],
                        xio[:, TT * c:TT * c + TT],
                        start=(c == 0), stop=(c == NCH - 1)),
                        waits=([("v", H2R)] + p.bank_wait(4)) if c == 0 else (),
                        inc=("t" if c == NCH - 1 else None))
                p.op("scalar", lambda e: e.copy(out=glbuf[:], in_=bank(4)[0:8, :]),
                     waits=[("t", p.cnt["t"])], inc="s")
                p.bank_mark(4, "s")
                GLE = p.cnt["s"]
                for sub in range(4):
                    p.op("tensor", lambda e, sub=sub: e.matmul(
                        bank(5)[:, 0:8], glbuf[:, 128 * sub:128 * sub + 128], i8F[:],
                        start=True, stop=True),
                        waits=[("s", GLE), ("v", VMR)] + p.bank_wait(5))
                    p.op("tensor", lambda e, sub=sub: e.matmul(
                        bank(5)[:, 8:9], r_rstd[0:1, 128 * sub:128 * sub + 128],
                        onesF[0:1, 0:1], start=True, stop=True))
                    p.op("tensor", lambda e, sub=sub: e.matmul(
                        bank(5)[:, 9:10], r_mr[0:1, 128 * sub:128 * sub + 128],
                        onesF[0:1, 0:1], start=True, stop=True), inc="t")
                    p.op("scalar", lambda e, sub=sub: e.copy(
                        out=gtm[:, 10 * sub:10 * sub + 10], in_=bank(5)[:, 0:10]),
                        waits=[("t", p.cnt["t"])], inc="s")
                    p.bank_mark(5, "s")
                GTE = p.cnt["s"]

                o8 = lambda t_, sub: t_[:, 8 * sub:8 * sub + 8]
                o4_ = lambda t_, sub: t_[:, 4 * sub:4 * sub + 4]
                def rop(mk, inc=None, w=None):
                    for sub in range(4):
                        p.op("vector", mk(sub), waits=(w or []) if sub == 0 else [],
                             inc=(inc if sub == 3 else None))
                rop(lambda sub: (lambda e: e.tensor_scalar(
                    out=o8(rA_, sub), in0=gtm[:, 10 * sub:10 * sub + 8],
                    scalar1=gtm[:, 10 * sub + 8:10 * sub + 9], scalar2=None,
                    op0=OP.mult)), w=[("s", GTE)])
                rop(lambda sub: (lambda e: e.tensor_scalar(
                    out=o8(rB_, sub), in0=gcol[:],
                    scalar1=gtm[:, 10 * sub + 9:10 * sub + 10], scalar2=None,
                    op0=OP.mult)))
                rop(lambda sub: (lambda e: e.tensor_tensor(
                    out=o8(rL_, sub), in0=o8(rA_, sub), in1=o8(rB_, sub),
                    op=OP.subtract)), inc="v")
                p.op("scalar", lambda e: e.activation(out=rS_[:], in_=rL_[:],
                     func=AF.Sigmoid), waits=[("v", p.cnt["v"])], inc="s")
                SC = p.cnt["s"]
                def ropS(mk, inc=None):
                    for sub in range(4):
                        p.op("vector", mk(sub), waits=[("s", SC)] if sub == 0 else [],
                             inc=(inc if sub == 3 else None))
                ropS(lambda sub: (lambda e: e.tensor_tensor(
                    out=o8(rC_, sub), in0=o8(rS_, sub), in1=cbb[:], op=OP.add)))
                ropS(lambda sub: (lambda e: e.tensor_tensor(
                    out=o4_(rG4, sub), in0=rC_[:, 8 * sub:8 * sub + 8:2],
                    in1=rC_[:, 8 * sub + 1:8 * sub + 8:2], op=OP.add)))
                ropS(lambda sub: (lambda e: e.tensor_reduce(
                    out=rM_[:, 8 * sub:8 * sub + 1], in_=o4_(rG4, sub),
                    axis=AX, op=OP.max)))
                ropS(lambda sub: (lambda e: e.tensor_scalar(
                    out=o4_(rI_, sub), in0=o4_(rG4, sub),
                    scalar1=rM_[:, 8 * sub:8 * sub + 1], scalar2=None,
                    op0=OP.is_ge)))
                ropS(lambda sub: (lambda e: e.scalar_tensor_tensor(
                    out=o4_(rW_, sub), in0=o4_(rI_, sub), scalar=-BIG,
                    in1=o4_(rG4, sub), op0=OP.mult, op1=OP.add)))
                ropS(lambda sub: (lambda e: e.tensor_reduce(
                    out=rM_[:, 8 * sub + 1:8 * sub + 2], in_=o4_(rW_, sub),
                    axis=AX, op=OP.max)))
                ropS(lambda sub: (lambda e: e.tensor_scalar(
                    out=o4_(rI_, sub), in0=o4_(rG4, sub),
                    scalar1=rM_[:, 8 * sub + 1:8 * sub + 2], scalar2=None,
                    op0=OP.is_ge)))
                ropS(lambda sub: (lambda e: e.tensor_copy(
                    out=rSM[:, 8 * sub:8 * sub + 8:2], in_=o4_(rI_, sub))))
                ropS(lambda sub: (lambda e: e.tensor_copy(
                    out=rSM[:, 8 * sub + 1:8 * sub + 8:2], in_=o4_(rI_, sub))))
                ropS(lambda sub: (lambda e: e.tensor_tensor(
                    out=o8(rFC, sub), in0=o8(rSM, sub), in1=o8(rC_, sub),
                    op=OP.mult)))
                ropS(lambda sub: (lambda e: e.tensor_scalar(
                    out=o8(rSM, sub), in0=o8(rSM, sub), scalar1=1.0,
                    scalar2=None, op0=OP.subtract)))
                ropS(lambda sub: (lambda e: e.tensor_tensor(
                    out=o8(rB_, sub), in0=o8(rSM, sub), in1=idxE[:], op=OP.mult)))
                ropS(lambda sub: (lambda e: e.tensor_tensor(
                    out=o8(rFC, sub), in0=o8(rFC, sub), in1=o8(rB_, sub),
                    op=OP.add)))
                ropS(lambda sub: (lambda e: e.tensor_copy(
                    out=o8(rW_, sub), in_=o8(rFC, sub))))
                for rnd in range(3):
                    ropS(lambda sub: (lambda e: e.tensor_reduce(
                        out=rM_[:, 8 * sub + 2:8 * sub + 3], in_=o8(rW_, sub),
                        axis=AX, op=OP.max)))
                    ropS(lambda sub: (lambda e: e.tensor_scalar(
                        out=o8(rI_, sub), in0=o8(rW_, sub),
                        scalar1=rM_[:, 8 * sub + 2:8 * sub + 3], scalar2=None,
                        op0=OP.is_ge)))
                    ropS(lambda sub: (lambda e: e.scalar_tensor_tensor(
                        out=o8(rW_, sub), in0=o8(rI_, sub), scalar=-BIG,
                        in1=o8(rW_, sub), op0=OP.mult, op1=OP.add)))
                ropS(lambda sub: (lambda e: e.tensor_reduce(
                    out=rM_[:, 8 * sub + 3:8 * sub + 4], in_=o8(rW_, sub),
                    axis=AX, op=OP.max)))
                ropS(lambda sub: (lambda e: e.tensor_scalar(
                    out=o8(rI_, sub), in0=o8(rFC, sub),
                    scalar1=rM_[:, 8 * sub + 3:8 * sub + 4], scalar2=None,
                    op0=OP.is_ge)))
                ropS(lambda sub: (lambda e: e.tensor_tensor(
                    out=o8(rW_, sub), in0=o8(rI_, sub), in1=o8(rS_, sub),
                    op=OP.mult)))
                ropS(lambda sub: (lambda e: e.tensor_reduce(
                    out=rM_[:, 8 * sub + 4:8 * sub + 5], in_=o8(rW_, sub),
                    axis=AX, op=OP.add)))
                ropS(lambda sub: (lambda e: e.tensor_scalar(
                    out=rM_[:, 8 * sub + 5:8 * sub + 6],
                    in0=rM_[:, 8 * sub + 4:8 * sub + 5], scalar1=1e-20,
                    scalar2=None, op0=OP.add)))
                ropS(lambda sub: (lambda e: e.reciprocal(
                    out=rM_[:, 8 * sub + 6:8 * sub + 7],
                    in_=rM_[:, 8 * sub + 5:8 * sub + 6])))
                ropS(lambda sub: (lambda e: e.tensor_scalar(
                    out=rM_[:, 8 * sub + 7:8 * sub + 8],
                    in0=rM_[:, 8 * sub + 6:8 * sub + 7], scalar1=2.5,
                    scalar2=None, op0=OP.mult)))
                ropS(lambda sub: (lambda e: e.tensor_scalar(
                    out=o8(rCW, sub), in0=o8(rW_, sub),
                    scalar1=rM_[:, 8 * sub + 7:8 * sub + 8], scalar2=None,
                    op0=OP.mult)), inc="v")
                CWV = p.cnt["v"]
                for sub in range(4):
                    p.op("tensor", lambda e, sub=sub: e.matmul(
                        bank(5)[0:8, 0:128], rCW[:, 8 * sub:8 * sub + 8], i128F[:],
                        start=True, stop=True),
                        waits=[("v", CWV)] + p.bank_wait(5), inc="t")
                    p.op("scalar", lambda e, sub=sub, tt=tt: e.copy(
                        out=cwT[0:8, TT * tt + 128 * sub:TT * tt + 128 * sub + 128],
                        in_=bank(5)[0:8, 0:128]), waits=[("t", p.cnt["t"])], inc="s")
                    p.bank_mark(5, "s")
            CWS = p.cnt["s"]
            if debug and ST == 0:
                p.dma("gpsimd", dbg["d_h2T"], h2T[:, 0:NCH * TT], "y_st",
                      waits=[("v", p.cnt["v"])])
                p.dma("gpsimd", dbg["d_cwT"], cwT[:], "y_st", waits=[("s", CWS)])

            def queue_chain(gi):
                e_idx = ([NE] + list(range(NE)))[gi % 9]
                sl = gi % 2
                for step, (which, wt_dram, blk) in enumerate(
                        [(w_, d_, b_) for (w_, d_) in (("g", wg), ("u", wu),
                                                       ("d", wd))
                         for b_ in range(4)]):
                    dstb = {"g": wgB, "u": wuB, "d": wdB}[which]
                    if which != "d":
                        din_ap = wt_dram[e_idx, 256 * blk:256 * blk + 256, :] \
                            .rearrange("(k p) x -> p k x", k=2)
                        off = NCH * DI * sl + 1024 * blk
                    else:
                        din_ap = wt_dram[e_idx, 128 * blk:128 * blk + 128, :]
                        off = 4 * D * sl + D * blk
                    wchain.append((gi, sl, step, din_ap, dstb, off))

            def drain_step():
                if not wchain:
                    return
                gi, sl, step, din_ap, dstb, off = wchain.pop(0)
                s0 = 1024 * (step % 2)
                wt = list(stg_guard.get(step % 2, []))
                if step == 0 and wslot_cnt.get(sl) is not None:
                    wt.append(("t", wslot_cnt[sl]))
                dout_ap = stg[:, s0:s0 + 1024]
                if din_ap.shape != (128, 1024):
                    dout_ap = dout_ap.rearrange("p (k x) -> p k x", k=2)
                p.dma("gpsimd", dout_ap, din_ap, "w_ld", waits=wt)
                p.op("scalar", lambda e, dstb=dstb, off=off, s0=s0:
                     e.copy(out=dstb[:, off:off + 1024],
                            in_=stg[:, s0:s0 + 1024]),
                     waits=[("w_ld", p.cnt["w_ld"])], inc="s")
                stg_guard[step % 2] = [("s", p.cnt["s"])]
                wcast_done[gi] = p.cnt["s"]

            def drain_until(gi):
                while wchain and wchain[0][0] <= gi:
                    drain_step()

            for ei, e_idx in enumerate([NE] + list(range(NE))):
                gi = 9 * ST + ei
                sl = gi % 2
                if gi == 0:
                    queue_chain(0)
                drain_until(gi)
                WCAST_S = wcast_done[gi]
                if gi + 1 < 9 * NST:
                    queue_chain(gi + 1)

                for tt in range(2):
                    if e_idx != NE:
                        for b4 in range(4):
                            p.op("vector", lambda e, b4=b4, tt=tt, e_idx=e_idx:
                                 e.stream_shuffle(cwb[32 * b4:32 * b4 + 32, :],
                                                  cwT[0:32, TT * tt:TT * tt + TT],
                                                  mask=[e_idx] * 32),
                                 waits=[("s", CWS), ("t", p.cnt["t"])]
                                 if b4 == 0 else (),
                                 inc=("v" if b4 == 3 else None))
                    CWBV = p.cnt["v"]
                    mults = []
                    act_mark = {}
                    for ic in range(4):
                        gb = 2 * (ic % 2)
                        ub = gb + 1
                        for wsb, bki in ((wgB, gb), (wuB, ub)):
                            bk = bank(bki)
                            for hc in range(NCH):
                                p.op("tensor", lambda e, wsb=wsb, bk=bk, ic=ic,
                                     hc=hc, sl=sl, tt=tt: e.matmul(
                                         bk,
                                         wsb[:, NCH * DI * sl + DI * hc + 128 * ic:
                                             NCH * DI * sl + DI * hc + 128 * ic + 128],
                                         h2T[:, NCH * TT * tt + TT * hc:
                                             NCH * TT * tt + TT * hc + TT],
                                         start=(hc == 0), stop=(hc == NCH - 1)),
                                     waits=([("s", WCAST_S), ("v", h2rv[tt])] +
                                            p.bank_wait(bki))
                                     if hc == 0 else (),
                                     inc=("t" if hc == NCH - 1 else None))
                        stmp = (tA, tB, tC)[ic % 3]
                        p.op("scalar", lambda e, stmp=stmp, gb=gb: e.activation(
                            out=stmp[:], in_=bank(gb), func=AF.Silu),
                            waits=[("t", p.cnt["t"]), ("v", p.cnt["v"])], inc="s")
                        p.bank_mark(gb, "s")
                        drain_step()
                        mtmp = (tD, tE, tF)[ic % 3]
                        p.op("vector", lambda e, stmp=stmp, mtmp=mtmp, ub=ub:
                             e.tensor_tensor(out=mtmp[:], in0=stmp[:],
                                             in1=bank(ub), op=OP.mult),
                             waits=[("s", p.cnt["s"])], inc="v")
                        p.bank_mark(ub, "v")
                        drain_step()
                        mults.append((mtmp, ic))
                        if len(mults) == 2:
                            mt, mic = mults.pop(0)
                            if e_idx == NE:
                                p.op("vector", lambda e, mt=mt, mic=mic:
                                     e.tensor_copy(out=actB[:, TT * mic:TT * mic + TT],
                                                   in_=mt[:]), inc="v")
                            else:
                                p.op("vector", lambda e, mt=mt, mic=mic:
                                     e.tensor_tensor(out=actB[:, TT * mic:TT * mic + TT],
                                                     in0=mt[:], in1=cwb[:],
                                                     op=OP.mult),
                                     waits=[("v", CWBV)], inc="v")
                            act_mark[mic] = p.cnt["v"]
                    mt, mic = mults.pop(0)
                    if e_idx == NE:
                        p.op("vector", lambda e, mt=mt, mic=mic: e.tensor_copy(
                            out=actB[:, TT * mic:TT * mic + TT], in_=mt[:]), inc="v")
                    else:
                        p.op("vector", lambda e, mt=mt, mic=mic: e.tensor_tensor(
                            out=actB[:, TT * mic:TT * mic + TT], in0=mt[:],
                            in1=cwb[:], op=OP.mult), waits=[("v", CWBV)], inc="v")
                    act_mark[mic] = p.cnt["v"]
                    ACTV = p.cnt["v"]
                    for hc in range(NCH):
                        bi = 4 + hc % 4
                        bk = bank(bi)
                        for ic in range(4):
                            p.op("tensor", lambda e, hc=hc, ic=ic, sl=sl, bk=bk:
                                 e.matmul(bk,
                                          wdB[:, 4 * D * sl + D * ic + 128 * hc:
                                              4 * D * sl + D * ic + 128 * hc + 128],
                                          actB[:, TT * ic:TT * ic + TT],
                                          start=(ic == 0), stop=(ic == 3)),
                                 waits=([("v", act_mark[ic])] +
                                        (p.bank_wait(bi) if ic == 0 else [])),
                                 inc=("t" if ic == 3 else None))
                        wslot_cnt[sl] = p.cnt["t"]
                        dst = routed[:, NCH * TT * tt + TT * hc:
                                     NCH * TT * tt + TT * hc + TT]
                        if ei == 0:
                            p.op("vector", lambda e, dst=dst, bk=bk:
                                 e.tensor_copy(out=dst, in_=bk),
                                 waits=[("t", p.cnt["t"])], inc="v")
                        else:
                            p.op("vector", lambda e, dst=dst, bk=bk:
                                 e.tensor_tensor(out=dst, in0=dst, in1=bk,
                                                 op=OP.add),
                                 waits=[("t", p.cnt["t"])], inc="v")
                        p.bank_mark(bi, "v")
            ROUT = p.cnt["v"]
            for tt in range(2):
                T = 2 * ST + tt
                xw = [("v", ROUT), ("y_st", p.cnt["y_st"])]
                for c in range(NCH):
                    p.dma("sync", xio[:, TT * c:TT * c + TT],
                          xp[128 * c:128 * c + 128, TT * T:TT * T + TT], "x_ld",
                          waits=xw if c == 0 else ())
                XL = p.cnt["x_ld"]
                for c in range(NCH):
                    p.op("vector", lambda e, c=c, tt=tt: e.tensor_tensor(
                        out=xio[:, TT * c:TT * c + TT],
                        in0=xio[:, TT * c:TT * c + TT],
                        in1=routed[:, NCH * TT * tt + TT * c:
                                   NCH * TT * tt + TT * c + TT],
                        op=OP.add), waits=[("x_ld", XL)], inc="v")
                    p.dma("sync", yT[128 * c:128 * c + 128, TT * T:TT * T + TT],
                          xio[:, TT * c:TT * c + TT], "y_st",
                          waits=[("v", p.cnt["v"])])
        if p.cnt["y_st"]:
            p.op("sync", lambda e: e.nop(), waits=[("y_st", p.cnt["y_st"])])
        p.flush(blkB)
        blkB_es.close()
        esB.close()
    return nc


_QKV_PERM = None
def _qkv_perm():
    global _QKV_PERM
    if _QKV_PERM is None:
        perm = []
        for hp in range(NHP):
            for h in (2 * hp, 2 * hp + 1):
                perm.extend(range(h * 192, h * 192 + 64))
        for hp in range(NHP):
            for h in (2 * hp, 2 * hp + 1):
                perm.extend(range(h * 192 + 64, h * 192 + 128))
        for h in range(NH):
            perm.extend(range(h * 192 + 128, h * 192 + 192))
        _QKV_PERM = np.array(perm)
    return _QKV_PERM


_NC_CACHE = {}
def get_nc(debug=False):
    if debug not in _NC_CACHE:
        _NC_CACHE[debug] = build_nc(debug)
    return _NC_CACHE[debug]


def make_inputs(L0_attn_x, L0_attn_w_qkv, L0_attn_w_out, L0_attn_b_out,
                L0_ffn_gate_w, L0_ffn_correction_bias, L0_ffn_Wg, L0_ffn_Wu,
                L0_ffn_Wd, L0_ffn_Wg_s, L0_ffn_Wu_s, L0_ffn_Wd_s):
    f = np.float32
    wqkvT = np.ascontiguousarray(np.asarray(L0_attn_w_qkv, f).T[:, _qkv_perm()])
    woutT = np.ascontiguousarray(np.asarray(L0_attn_w_out, f).T)
    gwT = np.ascontiguousarray(np.asarray(L0_ffn_gate_w, f).T)
    bout = np.ascontiguousarray(np.asarray(L0_attn_b_out, f).reshape(NCH, 128).T)
    cbias = np.ascontiguousarray(np.asarray(L0_ffn_correction_bias, f).reshape(1, NE))
    idxeps = np.ascontiguousarray(
        np.tile((np.arange(1, NE + 1, dtype=f) * f(1e-30)).reshape(1, NE), (128, 1)))
    kk = np.arange(128).reshape(1, 128, 1)
    qq = np.arange(TT).reshape(1, 1, TT)
    jj = np.arange(4).reshape(4, 1, 1)
    mk = (128 * jj + kk <= qq).astype(f)
    mk = np.ascontiguousarray(np.tile(mk, (1, 1, 2)))
    onesf = np.ones((128, 128), f)
    hones = np.zeros((128, 2), f); hones[0:64, 0] = 1.0; hones[64:128, 1] = 1.0
    i8 = np.eye(8, dtype=f)
    i128 = np.eye(128, dtype=f)
    wg_ = np.ascontiguousarray(np.concatenate(
        [np.asarray(L0_ffn_Wg, f), np.asarray(L0_ffn_Wg_s, f)[None]], axis=0))
    wu_ = np.ascontiguousarray(np.concatenate(
        [np.asarray(L0_ffn_Wu, f), np.asarray(L0_ffn_Wu_s, f)[None]], axis=0))
    wd_ = np.ascontiguousarray(np.concatenate(
        [np.asarray(L0_ffn_Wd, f), np.asarray(L0_ffn_Wd_s, f)[None]], axis=0))
    shared = dict(wqkvT=wqkvT, woutT=woutT, gwT=gwT, bout=bout, cbias=cbias,
                  idxeps=idxeps, masks=mk, onesf=onesf, hones=hones, i8=i8,
                  i128=i128, wg=wg_, wu=wu_, wd=wd_)
    x = np.asarray(L0_attn_x, f)
    maps = []
    for core in range(NCORES):
        m = dict(shared)
        m["xT"] = np.ascontiguousarray(x[core].T)
        maps.append(m)
    return maps


def kernel(**inputs):
    maps = make_inputs(**inputs)
    nc = get_nc(debug=False)
    res = run_bass_kernel_spmd(nc, maps, core_ids=list(range(NCORES)))
    out = np.stack([np.ascontiguousarray(r["yT"].T) for r in res.results], axis=0)
    return out.astype(np.float32)


# version 17
# speedup vs baseline: 1.0707x; 1.0350x over previous
import contextlib
import numpy as np

import concourse.bass as bass
import concourse.mybir as mybir
from concourse.bass_utils import run_bass_kernel_spmd
from concourse.alu_op_type import AluOpType as OP

F32 = mybir.dt.float32
BF16 = mybir.dt.bfloat16
AF = mybir.ActivationFunctionType
AX = mybir.AxisListType.X

NCORES = 8
SEQ = 4096
D = 1024
NH = 16
NHP = 8
NT = 8
TT = 512
NCH = 8
NE = 8
DI = 512
NST = 4
SCALE = 0.125
EPS = 1e-5
BIG = 1.0e30
NKV = 3

ENGINES = ("sync", "gpsimd", "tensor", "vector", "scalar")


class Prog:

    def __init__(self, nc, es):
        self.nc = nc
        self.q = {e: [] for e in ENGINES}
        self.cnt = {}
        self.sems = {}
        self.es = es
        self.extra_first = {}
        self.banks = {}

    def sem(self, name):
        h = self.es.enter_context(self.nc.semaphore(name))
        self.sems[name] = h
        self.cnt[name] = 0
        return name

    def bank_wait(self, i):
        return list(self.banks.get(i, []))

    def bank_mark(self, i, semname):
        self.banks[i] = [(semname, self.cnt[semname])]

    def op(self, eng, fn, waits=(), inc=None, n=1):
        waits = list(waits)
        if eng in self.extra_first:
            waits += self.extra_first.pop(eng)
        for s, t in waits:
            assert t <= self.cnt[s], (eng, s, t, self.cnt[s])
        if inc is not None:
            self.cnt[inc] += n
        self.q[eng].append((fn, tuple(waits), inc, n))
        return self.cnt[inc] if inc is not None else None

    def dma(self, eng, out, in_, sem, waits=()):
        return self.op(eng, lambda e, o=out, i=in_: e.dma_start(out=o, in_=i),
                       waits, sem, 16)

    def flush(self, block):
        for name in ENGINES:
            ops = self.q[name]

            def body(h, ops=ops):
                for fn, waits, inc, n in ops:
                    for s, t in waits:
                        h.wait_ge(self.sems[s], t)
                    r = fn(h)
                    if inc is not None:
                        r.then_inc(self.sems[inc], n)

            getattr(block, name)(body)
        self.q = {e: [] for e in ENGINES}


DEV_NT = None
DEV_SKIP_B = False


def build_nc(debug=False):
    nc = bass.Bass("TRN2", target_bir_lowering=False)

    def din(name, shape, dt=F32):
        return nc.dram_tensor(name, shape, dt, kind="ExternalInput").ap()

    xT = din("xT", [D, SEQ])
    wqkvT = din("wqkvT", [D, 3 * D])
    woutT = din("woutT", [D, D])
    gwT = din("gwT", [D, NE])
    bout = din("bout", [128, NCH])
    cbias = din("cbias", [1, NE])
    idxeps = din("idxeps", [128, NE])
    masks = din("masks", [4, 128, 2 * TT])
    onesf = din("onesf", [128, 128])
    hones = din("hones", [128, 2])
    i8 = din("i8", [8, 8])
    i128 = din("i128", [128, 128])
    wg = din("wg", [NE + 1, D, DI])
    wu = din("wu", [NE + 1, D, DI])
    wd = din("wd", [NE + 1, DI, D])

    yT = nc.dram_tensor("yT", [D, SEQ], F32, kind="ExternalOutput").ap()
    xp = nc.dram_tensor("xp", [D, SEQ], F32, kind="Internal").ap()
    ksp = nc.dram_tensor("ksp", [SEQ // 128, 128, 1024], BF16, kind="Internal").ap()
    vsp = nc.dram_tensor("vsp", [SEQ // 128, 128, 1040], BF16, kind="Internal").ap()

    dbg = {}
    if debug:
        for name, shape, dt in [
            ("d_h1T", [128, NCH * TT], BF16), ("d_ont", [128, NHP * TT], BF16),
            ("d_h2T", [128, NCH * TT], BF16), ("d_cwT", [32, 2 * TT], F32),
            ("d_nb", [128, 4], F32), ("d_bbt", [32, 48], F32),
            ("d_rmx", [32, NHP], F32),
        ]:
            dbg[name] = nc.dram_tensor(name, shape, dt, kind="ExternalOutput").ap()

    es = contextlib.ExitStack()
    with es:
        def sb(name, shape, dt):
            return es.enter_context(nc.sbuf_tensor(name, shape, dt))

        xio = sb("xio", [128, NCH * TT], F32)
        xn = sb("xn", [128, NCH * TT], F32)
        onesF = sb("onesF", [128, 128], F32)
        onesB = sb("onesB", [128, 128], BF16)
        i8F = sb("i8F", [8, 8], F32)
        i128F = sb("i128F", [128, 128], F32)
        boutF = sb("boutF", [128, NCH], F32)
        crow = sb("crow", [32, 16], F32)
        gcol = sb("gcol", [128, NE], F32)
        cbb = sb("cbb", [128, NE], F32)
        idxE = sb("idxE", [128, NE], F32)
        gwF = sb("gwF", [128, NCH * NE], F32)
        r_q = sb("r_q", [1, TT], F32)
        r_mean = sb("r_mean", [1, TT], F32)
        r_q2 = sb("r_q2", [1, TT], F32)
        r_v = sb("r_v", [1, TT], F32)
        r_ve = sb("r_ve", [1, TT], F32)
        r_ln = sb("r_ln", [1, TT], F32)
        r_rstd = sb("r_rstd", [1, TT], F32)
        r_mr = sb("r_mr", [1, TT], F32)
        tA = sb("tA", [128, TT], F32)
        tB = sb("tB", [128, TT], F32)
        tC = sb("tC", [128, TT], F32)
        xbb = sb("xbb", [128, 3 * TT], BF16)
        xsq = sb("xsq", [128, 3 * TT], BF16)
        maskB = sb("maskB", [128, 4 * 2 * TT], BF16)
        lb = sb("lb", [128, 2 * TT], F32)
        h1T = sb("h1T", [128, NCH * TT], BF16)
        honesB = sb("honesB", [128, 2], BF16)
        rmxQ = sb("rmxQ", [32, NHP], F32)
        rmxK = sb("rmxK", [32, NHP], F32)
        bbt = sb("bbt", [32, 48], F32)
        negBc = sb("negBc", [128, 4], F32)
        gtm = sb("gtm", [128, 4 * 10], F32)
        def rt(name):
            return sb(name, [128, 4 * NE], F32)
        rA_, rB_, rL_, rS_, rC_ = rt("rtA"), rt("rtB"), rt("rtL"), rt("rtS"), rt("rtC")
        rG4 = sb("rG4", [128, 4 * 4], F32)
        rM_ = sb("rM_", [128, 4 * 8], F32)
        rI_, rW_, rSM, rFC, rCW = rt("rtI"), rt("rtW"), rt("rtSM"), rt("rtFC"), rt("rtCW")
        glbuf = sb("glbuf", [8, TT], F32)
        cwT = sb("cwT", [32, 2 * TT], F32)

        PSA = es.enter_context(nc.psum_tensor("psA", [128, 2048], F32))
        PSB = es.enter_context(nc.psum_tensor("psB", [128, 2048], F32))
        PS = [PSA[:, 0:1024], PSA[:, 1024:2048], PSB[:, 0:1024], PSB[:, 1024:2048]]
        def bank(i):
            return (PSA if i < 4 else PSB)[:, 512 * (i % 4):512 * (i % 4) + 512]

        p = Prog(nc, es)
        for s in ("c_ld", "x_ld", "kv_ld", "w_ld", "kv_st", "xp_st", "sh_st",
                  "y_st", "t", "s", "v", "g"):
            p.sem(s)

        esA = es.enter_context(contextlib.ExitStack())
        def sbA(name, shape, dt):
            return esA.enter_context(nc.sbuf_tensor(name, shape, dt))

        wqkvB = sbA("wqkvB", [128, NCH * 3072], BF16)
        woutB = sbA("woutB", [128, NCH * 1024], BF16)
        Qp = sbA("Qp", [128, NHP * TT], BF16)
        Kc = sbA("Kc", [128, NHP * TT], BF16)
        Vc = sbA("Vc", [128, 4 * 1040], BF16)
        kvK = sbA("kvK", [128, NKV * 1024], BF16)
        kvV = sbA("kvV", [128, NKV * 1040], BF16)
        Pb = sbA("Pb", [128, 4 * 1024], BF16)
        OnT = sbA("OnT", [128, NHP * TT], BF16)
        Ot = sbA("Ot", [64, 2 * TT], BF16)

        blkA_es = contextlib.ExitStack()
        blkA = blkA_es.enter_context(nc.Block())

        for dst, src in [(onesF[:], onesf), (i8F[:], i8), (i128F[:], i128),
                         (boutF[:], bout), (idxE[:], idxeps),
                         (crow[0:1, 8:16], cbias)]:
            p.dma("sync", dst, src, "c_ld")
        for c in range(NCH):
            p.dma("sync", gwF[:, 8 * c:8 * c + 8], gwT[128 * c:128 * c + 128, :], "c_ld")
        p.dma("sync", xn[:, 0:2], hones, "c_ld")
        p.op("scalar", lambda e: e.copy(out=onesB[:], in_=onesF[:]),
             waits=[("c_ld", 16)], inc="s")
        p.op("scalar", lambda e: e.copy(out=honesB[:], in_=xn[:, 0:2]),
             waits=[("c_ld", p.cnt["c_ld"])], inc="s")
        ONES_S = p.cnt["s"]
        p.op("vector", lambda e: e.memset(rmxQ[:], 0.0), inc="v")
        p.op("vector", lambda e: e.memset(rmxK[:], 0.0), inc="v")
        for c in range(NCH):
            p.op("tensor", lambda e, c=c: e.matmul(
                bank(0)[0:1, 0:8], onesF[:, 0:1], gwF[:, 8 * c:8 * c + 8],
                start=(c == 0), stop=(c == NCH - 1)),
                waits=[("c_ld", 16 * (7 + NCH))], inc="t")
        p.op("scalar", lambda e: e.copy(out=crow[0:1, 0:8], in_=bank(0)[0:1, 0:8]),
             waits=[("t", p.cnt["t"])], inc="s")
        p.bank_mark(0, "s")
        SROW = p.cnt["s"]
        for b in range(4):
            p.op("vector", lambda e, b=b: e.stream_shuffle(
                gcol[32 * b:32 * b + 32, :], crow[:, 0:8], mask=[0] * 32),
                waits=[("s", SROW), ("c_ld", 16 * 6)])
            p.op("vector", lambda e, b=b: e.stream_shuffle(
                cbb[32 * b:32 * b + 32, :], crow[:, 8:16], mask=[0] * 32),
                waits=[("s", SROW)], inc="v")
        p.op("vector", lambda e: e.memset(
            Vc[:].rearrange("p (k h x) -> p k h x", k=4, h=NH)[:, :, :, 64:65], 1.0),
            inc="v")

        def ln_tile(out_h1, out_base, xwaits, xsrc=None):
            xwaits = list(xwaits)
            xsrc = xio if xsrc is None else xsrc
            t_entry = p.cnt["t"]
            slot_guard = {i: [("t", t_entry)] for i in range(3)}
            for c in range(NCH):
                sl = (c % 3) * TT
                xa = xsrc[:, TT * c:TT * c + TT]
                w0 = (xwaits if c == 0 else []) + slot_guard.get(c % 3, [])
                p.op("scalar", lambda e, xa=xa, sl=sl:
                     e.copy(out=xbb[:, sl:sl + TT], in_=xa), waits=w0, inc="s")
                p.op("vector", lambda e, xa=xa, sl=sl:
                     e.tensor_tensor(out=xsq[:, sl:sl + TT], in0=xa, in1=xa,
                                     op=OP.mult), waits=w0, inc="v")
                p.op("tensor", lambda e, c=c, sl=sl: e.matmul(
                    bank(0)[0:1, 0:TT], onesB[:, 0:1], xbb[:, sl:sl + TT],
                    start=(c == 0), stop=(c == NCH - 1)),
                    waits=[("s", p.cnt["s"])] + (p.bank_wait(0) if c == 0 else []))
                p.op("tensor", lambda e, c=c, sl=sl: e.matmul(
                    bank(1)[0:1, 0:TT], onesB[:, 0:1], xsq[:, sl:sl + TT],
                    start=(c == 0), stop=(c == NCH - 1)),
                    waits=[("v", p.cnt["v"])] + (p.bank_wait(1) if c == 0 else []),
                    inc="t")
                slot_guard[c % 3] = [("t", p.cnt["t"])]
            TSUM = p.cnt["t"]
            p.op("scalar", lambda e: e.activation(out=r_q[:], in_=bank(0)[0:1, 0:TT],
                 func=AF.Copy, scale=1.0 / 32.0), waits=[("t", TSUM)], inc="s")
            p.op("scalar", lambda e: e.activation(out=r_mean[:], in_=bank(0)[0:1, 0:TT],
                 func=AF.Copy, scale=1.0 / 1024.0), inc="s")
            p.bank_mark(0, "s")
            p.op("vector", lambda e: e.tensor_tensor(out=r_q2[:], in0=r_q[:],
                 in1=r_q[:], op=OP.mult), waits=[("s", p.cnt["s"])], inc="v")
            p.op("vector", lambda e: e.tensor_tensor(out=r_v[:], in0=bank(1)[0:1, 0:TT],
                 in1=r_q2[:], op=OP.subtract), waits=[("v", p.cnt["v"]),
                 ("t", TSUM)], inc="v")
            p.bank_mark(1, "v")
            p.op("vector", lambda e: e.tensor_scalar(out=r_ve[:], in0=r_v[:],
                 scalar1=1.0 / 1024.0, scalar2=EPS, op0=OP.mult, op1=OP.add),
                 waits=[("v", p.cnt["v"])], inc="v")
            p.op("scalar", lambda e: e.activation(out=r_ln[:], in_=r_ve[:],
                 func=AF.Ln), waits=[("v", p.cnt["v"])], inc="s")
            p.op("scalar", lambda e: e.activation(out=r_rstd[:], in_=r_ln[:],
                 func=AF.Exp, scale=-0.5), waits=[("s", p.cnt["s"])], inc="s")
            SRS = p.cnt["s"]
            p.op("vector", lambda e: e.tensor_tensor(out=r_mr[:], in0=r_mean[:],
                 in1=r_rstd[:], op=OP.mult), waits=[("s", SRS)], inc="v")
            VMR = p.cnt["v"]
            p.op("tensor", lambda e: e.matmul(bank(2)[:], onesF[0:1, :],
                 r_mean[:], start=True, stop=True),
                 waits=[("s", SRS)] + p.bank_wait(2))
            p.op("tensor", lambda e: e.matmul(bank(3)[:], onesF[0:1, :],
                 r_rstd[:], start=True, stop=True),
                 waits=p.bank_wait(3), inc="t")
            TBC = p.cnt["t"]
            pend = []
            for c in range(NCH):
                xa = xsrc[:, TT * c:TT * c + TT]
                tmp = (tA, tB, tC)[c % 3]
                p.op("vector", lambda e, xa=xa, tmp=tmp: e.tensor_tensor(
                    out=tmp[:], in0=xa, in1=bank(2)[:], op=OP.subtract),
                    waits=[("t", TBC)], inc="v")
                pend.append((tmp, c))
                if len(pend) == 2:
                    tm, cc = pend.pop(0)
                    p.op("vector", lambda e, tm=tm, cc=cc: e.tensor_tensor(
                        out=out_h1[:, out_base + TT * cc:out_base + TT * cc + TT],
                        in0=tm[:], in1=bank(3)[:], op=OP.mult), inc="v")
            tm, cc = pend.pop(0)
            p.op("vector", lambda e, tm=tm, cc=cc: e.tensor_tensor(
                out=out_h1[:, out_base + TT * cc:out_base + TT * cc + TT],
                in0=tm[:], in1=bank(3)[:], op=OP.mult), inc="v")
            p.bank_mark(2, "v")
            p.bank_mark(3, "v")
            return p.cnt["v"], VMR

        for c in range(NCH):
            p.dma("sync", xio[:, TT * c:TT * c + TT],
                  xT[128 * c:128 * c + 128, 0:TT], "x_ld")
        XL0 = p.cnt["x_ld"]
        H1R_0, _ = ln_tile(h1T, 0, [("x_ld", XL0), ("s", ONES_S)])

        cast_hist = []
        stage_jobs = []
        for j in range(4):
            stage_jobs.append((maskB[:, 1024 * j:1024 * j + 1024], masks[j]))
        for c in range(NCH):
            for h in range(3):
                stage_jobs.append((
                    wqkvB[:, 3072 * c + 1024 * h:3072 * c + 1024 * h + 1024],
                    wqkvT[128 * c:128 * c + 128, 1024 * h:1024 * h + 1024]))
        for c in range(NCH):
            stage_jobs.append((woutB[:, 1024 * c:1024 * c + 1024],
                               woutT[128 * c:128 * c + 128, :]))
        for k, (dst, hsrc) in enumerate(stage_jobs):
            sl = 1024 * (k % 4)
            wt = [("s", cast_hist[k - 4])] if k >= 4 else [("s", ONES_S)]
            q, sem = ("sync", "c_ld") if k % 2 == 0 else ("gpsimd", "w_ld")
            ld = p.dma(q, xn[:, sl:sl + 1024], hsrc, sem, waits=wt)
            p.op("scalar", lambda e, dst=dst, sl=sl:
                 e.copy(out=dst, in_=xn[:, sl:sl + 1024]),
                 waits=[(sem, ld)], inc="s")
            cast_hist.append(p.cnt["s"])
        CAST_DONE = p.cnt["s"]

        kv_loads = 0
        stream_consumed = {}
        ot_guard = {}
        rotg = 0

        xbuf = [xio, xn]
        SPILL_MARK = 0
        H1R_next = H1R_0
        NTT = DEV_NT or NT
        for T in range(NTT):
            H1R = H1R_next
            cur = xbuf[T % 2]

            for hp in range(NHP):
                for base, dst in ((0, Qp), (1024, Kc)):
                    bi = rotg % 8; rotg += 1
                    bk = bank(bi)
                    for c in range(NCH):
                        p.op("tensor", lambda e, c=c, hp=hp, base=base, bk=bk: e.matmul(
                            bk, wqkvB[:, 3072 * c + base + 128 * hp:
                                      3072 * c + base + 128 * hp + 128],
                            h1T[:, TT * c:TT * c + TT],
                            start=(c == 0), stop=(c == NCH - 1)),
                            waits=([("v", H1R)] + p.bank_wait(bi) +
                                   ([("s", CAST_DONE)] if (T == 0 and hp == 0)
                                    else [])) if c == 0 else (),
                            inc=("t" if c == NCH - 1 else None))
                    p.op("scalar", lambda e, dst=dst, hp=hp, bk=bk: e.copy(
                        out=dst[:, TT * hp:TT * hp + TT], in_=bk),
                        waits=[("t", p.cnt["t"]), ("kv_st", p.cnt["kv_st"])], inc="s")
                    p.bank_mark(bi, "s")
                    rmx = rmxQ if base == 0 else rmxK
                    ssl = TT * (0 if base == 0 else 1)
                    p.op("vector", lambda e, dst=dst, hp=hp, ssl=ssl: e.tensor_tensor(
                        out=xsq[:, ssl:ssl + TT], in0=dst[:, TT * hp:TT * hp + TT],
                        in1=dst[:, TT * hp:TT * hp + TT], op=OP.mult),
                        waits=[("s", p.cnt["s"])], inc="v")
                    bi2 = rotg % 8; rotg += 1
                    p.op("tensor", lambda e, ssl=ssl, bk2=bank(bi2): e.matmul(
                        bk2[0:2, 0:TT], honesB[:], xsq[:, ssl:ssl + TT],
                        start=True, stop=True),
                        waits=[("v", p.cnt["v"])] + p.bank_wait(bi2), inc="t")
                    rsl = 10 + (rotg % 2)
                    p.op("vector", lambda e, bk2=bank(bi2), rsl=rsl: e.tensor_reduce(
                        out=bbt[0:2, rsl:rsl + 1], in_=bk2[0:2, 0:TT], axis=AX,
                        op=OP.max), waits=[("t", p.cnt["t"])], inc="v")
                    p.bank_mark(bi2, "v")
                    p.op("vector", lambda e, rmx=rmx, hp=hp, rsl=rsl: e.tensor_tensor(
                        out=rmx[0:2, hp:hp + 1], in0=rmx[0:2, hp:hp + 1],
                        in1=bbt[0:2, rsl:rsl + 1], op=OP.max),
                        waits=[("v", p.cnt["v"])], inc="v")
            QKE = p.cnt["s"]
            for s4 in range(4):
                for h4 in range(2):
                    bi = rotg % 8; rotg += 1
                    bk = bank(bi)
                    for c in range(NCH):
                        p.op("tensor", lambda e, c=c, s4=s4, h4=h4, bk=bk: e.matmul(
                            bk, h1T[:, TT * c + 128 * s4:TT * c + 128 * s4 + 128],
                            wqkvB[:, 3072 * c + 2048 + 512 * h4:
                                  3072 * c + 2048 + 512 * h4 + 512],
                            start=(c == 0), stop=(c == NCH - 1)),
                            waits=p.bank_wait(bi) if c == 0 else (),
                            inc=("t" if c == NCH - 1 else None))
                    p.op("scalar", lambda e, s4=s4, h4=h4, bk=bk: e.copy(
                        out=Vc[:, 1040 * s4 + 520 * h4:1040 * s4 + 520 * h4 + 520]
                            .rearrange("p (h x) -> p h x", h=NH // 2)[:, :, 0:64],
                        in_=bk.rearrange("p (h x) -> p h x", h=NH // 2)),
                        waits=[("t", p.cnt["t"]), ("kv_st", p.cnt["kv_st"])], inc="s")
                    p.bank_mark(bi, "s")
            PROD_S = p.cnt["s"]
            for j in range(4):
                p.dma("sync", ksp[4 * T + j].rearrange("p (h x) -> p h x", h=NHP),
                      Kc[:].rearrange("p (h k x) -> p k h x", h=NHP, k=4)[:, j],
                      "kv_st", waits=[("s", QKE)])
                p.dma("sync", vsp[4 * T + j], Vc[:, 1040 * j:1040 * j + 1040],
                      "kv_st", waits=[("s", PROD_S)])

            if T + 1 < NTT:
                nxt = xbuf[(T + 1) % 2]
                xw = ([("xp_st", SPILL_MARK)] +
                      ([("s", CAST_DONE)] if T == 0 else []))
                for c in range(NCH):
                    p.dma("sync", nxt[:, TT * c:TT * c + TT],
                          xT[128 * c:128 * c + 128, TT * (T + 1):TT * (T + 1) + TT],
                          "x_ld", waits=xw if c == 0 else ())
                XLN = p.cnt["x_ld"]

            nkc = 4 * T + 4
            W0 = [("s", p.cnt["s"]), ("v", p.cnt["v"])]
            def _c(sw, k):
                return bbt[0:1, 12 * sw + k:12 * sw + k + 1]
            for sw in range(4):
        
                p.op("vector", lambda e, sw=sw: e.tensor_tensor(
                    out=bbt[0:2, 12 * sw + 0:12 * sw + 1],
                    in0=rmxQ[0:2, 2 * sw:2 * sw + 1],
                    in1=rmxQ[0:2, 2 * sw + 1:2 * sw + 2], op=OP.max),
                    waits=W0 if sw == 0 else ())
                p.op("vector", lambda e, sw=sw: e.tensor_tensor(
                    out=bbt[0:2, 12 * sw + 1:12 * sw + 2],
                    in0=rmxK[0:2, 2 * sw:2 * sw + 1],
                    in1=rmxK[0:2, 2 * sw + 1:2 * sw + 2], op=OP.max), inc="v")
            for sw in range(4):
                p.op("vector", lambda e, sw=sw: e.stream_shuffle(
                    bbt[0:32, 12 * sw + 2:12 * sw + 4],
                    bbt[0:32, 12 * sw + 0:12 * sw + 2], mask=[1] * 32),
                    waits=[("v", p.cnt["v"])] if sw == 0 else (), inc="v")
            for sw in range(4):
                p.op("vector", lambda e, sw=sw: e.tensor_tensor(
                    out=bbt[0:1, 12 * sw + 4:12 * sw + 6],
                    in0=bbt[0:1, 12 * sw + 0:12 * sw + 2],
                    in1=bbt[0:1, 12 * sw + 2:12 * sw + 4], op=OP.max),
                    waits=[("v", p.cnt["v"])] if sw == 0 else (), inc="v")
            for sw in range(4):
                p.op("vector", lambda e, sw=sw: e.tensor_tensor(
                    out=_c(sw, 6), in0=_c(sw, 4), in1=_c(sw, 5), op=OP.mult),
                    waits=[("v", p.cnt["v"])] if sw == 0 else (), inc="v")
            for sw in range(4):
                p.op("scalar", lambda e, sw=sw: e.activation(
                    out=_c(sw, 7), in_=_c(sw, 6), func=AF.Ln),
                    waits=[("v", p.cnt["v"])] if sw == 0 else (), inc="s")
            for sw in range(4):
                p.op("scalar", lambda e, sw=sw: e.activation(
                    out=_c(sw, 8), in_=_c(sw, 7), func=AF.Exp, scale=0.5),
                    waits=[("s", p.cnt["s"])] if sw == 0 else (), inc="s")
            for sw in range(4):
                p.op("vector", lambda e, sw=sw: e.tensor_scalar(
                    out=_c(sw, 9), in0=_c(sw, 8), scalar1=-0.125,
                    scalar2=85.0, op0=OP.mult, op1=OP.add),
                    waits=[("s", p.cnt["s"])] if sw == 0 else (), inc="v")
            for sw in range(4):
                p.op("vector", lambda e, sw=sw: e.tensor_scalar_min(
                    out=_c(sw, 10), in0=_c(sw, 9), scalar1=0.0),
                    waits=[("v", p.cnt["v"])] if sw == 0 else (), inc="v")
            NBMIN = p.cnt["v"]
            for b4 in range(4):
                for sw in range(4):
                    p.op("vector", lambda e, b4=b4, sw=sw: e.stream_shuffle(
                        negBc[32 * b4:32 * b4 + 32, sw:sw + 1],
                        bbt[0:32, 12 * sw + 10:12 * sw + 11], mask=[0] * 32),
                        waits=[("v", NBMIN)] if (b4 == 0 and sw == 0) else (),
                        inc=("v" if (b4 == 3 and sw == 3) else None))
            NB_V = p.cnt["v"]
            for sw in range(4):
                pairs = (2 * sw, 2 * sw + 1)
                pend = []

                def emit_pv(info):
                    islot = info["i"] % 4
                    pi = info["pi"]
                    for hb in range(2):
                        h = 2 * pairs[pi] + hb
                        p.op("tensor", lambda e, pi=pi, islot=islot, h=h, hb=hb,
                             info=info: e.matmul(
                                 PSB[0:65, 1024 * pi + 512 * hb:
                                     1024 * pi + 512 * hb + 512],
                                 info["vsrc"][:, 65 * h:65 * h + 65],
                                 Pb[:, 1024 * islot + 512 * hb:
                                    1024 * islot + 512 * hb + 512],
                                 start=info["start"], stop=info["stop"]),
                             waits=info["wait"] if hb == 0 else (),
                             inc=("t" if hb == 1 else None))
                    if info["load_idx"] is not None:
                        stream_consumed[info["load_idx"]] = p.cnt["t"]

                for kc in range(nkc):
                    diag = kc >= 4 * T
                    if not diag:
                        slot = kv_loads % NKV
                        wt = [("kv_st", p.cnt["kv_st"])]
                        if kv_loads >= NKV:
                            wt.append(("t", stream_consumed[kv_loads - NKV]))
                        p.dma("gpsimd", kvK[:, 1024 * slot:1024 * slot + 1024],
                              ksp[kc], "kv_ld", waits=wt)
                        p.dma("gpsimd", kvV[:, 1040 * slot:1040 * slot + 1040],
                              vsp[kc], "kv_ld")
                        this_load = p.cnt["kv_ld"]
                        kQ = kvK[:, 1024 * slot:1024 * slot + 1024]
                        vQ = kvV[:, 1040 * slot:1040 * slot + 1040]
                        ks = [kQ[:, 128 * pr:128 * pr + 128] for pr in pairs]
                        vs = [vQ, vQ]
                        kv_loads += 1
                    else:
                        j = kc - 4 * T
                        ks = [Kc[:, TT * pr + 128 * j:TT * pr + 128 * j + 128]
                              for pr in pairs]
                        vs = [Vc[:, 1040 * j:1040 * j + 1040]] * 2
                        this_load = None
                    for pi in range(2):
                        i = 2 * kc + pi
                        if len(pend) == 2:
                            emit_pv(pend.pop(0))
                        wt = p.bank_wait(2 * pi) + p.bank_wait(2 * pi + 1)
                        if i == 0:
                            wt.append(("s", PROD_S))
                        if pi == 0 and this_load is not None:
                            wt.append(("kv_ld", this_load))
                        pr = pairs[pi]
                        ksrc = ks[pi]
                        p.op("tensor", lambda e, pi=pi, ksrc=ksrc, pr=pr: e.matmul(
                            PSA[:, 1024 * pi:1024 * pi + 512], ksrc[0:64, :],
                            Qp[0:64, TT * pr:TT * pr + TT],
                            start=True, stop=True, tile_position=(0, 0)), waits=wt)
                        p.op("tensor", lambda e, pi=pi, ksrc=ksrc, pr=pr: e.matmul(
                            PSA[:, 1024 * pi + 512:1024 * pi + 1024], ksrc[64:128, :],
                            Qp[64:128, TT * pr:TT * pr + TT],
                            start=True, stop=True, tile_position=(64, 0)), inc="t")
                        SMM = p.cnt["t"]
                        p.op("scalar", lambda e, pi=pi, islot=i % 4, sw=sw:
                             e.activation(
                                 out=Pb[:, 1024 * islot:1024 * islot + 1024],
                                 in_=PSA[:, 1024 * pi:1024 * pi + 1024], func=AF.Exp,
                                 scale=SCALE, bias=negBc[:, sw:sw + 1]),
                             waits=[("t", SMM)] + ([("v", NB_V)] if i == 0 else []),
                             inc="s")
                        p.bank_mark(2 * pi, "s")
                        p.bank_mark(2 * pi + 1, "s")
                        if diag:
                            j = kc - 4 * T
                            p.op("vector", lambda e, islot=i % 4, j=j: e.tensor_tensor(
                                out=Pb[:, 1024 * islot:1024 * islot + 1024],
                                in0=Pb[:, 1024 * islot:1024 * islot + 1024],
                                in1=maskB[:, 1024 * j:1024 * j + 1024], op=OP.mult),
                                waits=[("s", p.cnt["s"])], inc="v")
                            pv_wait = [("v", p.cnt["v"])]
                        else:
                            pv_wait = [("s", p.cnt["s"])]
                        if kc == 0:
                            pv_wait = pv_wait + p.bank_wait(4 + 2 * pi) + \
                                      p.bank_wait(5 + 2 * pi)
                        pend.append(dict(i=i, pi=pi, vsrc=vs[pi], wait=pv_wait,
                                         start=(kc == 0), stop=(kc == nkc - 1),
                                         load_idx=(kv_loads - 1)
                                         if (not diag and pi == 1) else None))

                for pi in range(2):
                    emit_pv(pend.pop(0))
                    PVD = p.cnt["t"]
                    pr = pairs[pi]
                    p.op("scalar", lambda e, pi=pi, pr=pr: e.copy(
                        out=OnT[0:64, TT * pr:TT * pr + TT],
                        in_=PSB[0:64, 1024 * pi:1024 * pi + 512]),
                        waits=[("t", PVD)], inc="s")
                    p.op("scalar", lambda e, pi=pi: e.copy(
                        out=Ot[:, TT * pi:TT * pi + TT],
                        in_=PSB[0:64, 1024 * pi + 512:1024 * pi + 1024]),
                        waits=ot_guard.get(pi, []), inc="s")
                    SCP = p.cnt["s"]
                    p.dma("sync", OnT[64:128, TT * pr:TT * pr + TT],
                          Ot[:, TT * pi:TT * pi + TT], "sh_st",
                          waits=[("s", SCP)])
                    SHD = p.cnt["sh_st"]
                    ot_guard[pi] = [("sh_st", SHD)]
                    for hb, tt_ in ((0, tA), (1, tB)):
                        p.op("vector", lambda e, pi=pi, hb=hb, tt_=tt_:
                             e.tensor_scalar(
                                 out=tt_[64:65, 0:TT],
                                 in0=PSB[64:65, 1024 * pi + 512 * hb:
                                         1024 * pi + 512 * hb + 512],
                                 scalar1=1e-37, scalar2=None, op0=OP.add),
                             waits=[("t", PVD)] if hb == 0 else (),
                             inc=("v" if hb == 1 else None))
                    p.banks[4 + 2 * pi] = [("s", SCP), ("v", p.cnt["v"])]
                    p.banks[5 + 2 * pi] = [("s", SCP), ("v", p.cnt["v"])]
                    for hb, tt_ in ((0, tA), (1, tB)):
                        p.op("vector", lambda e, tt_=tt_: e.reciprocal(
                            out=tt_[64:65, 0:TT], in_=tt_[64:65, 0:TT]),
                            waits=[("v", p.cnt["v"])] if hb == 0 else (),
                            inc=("v" if hb == 1 else None))
                    for b4 in range(4):
                        tt_ = tA if b4 < 2 else tB
                        cc = 0 if b4 < 2 else TT
                        p.op("vector", lambda e, b4=b4, tt_=tt_, cc=cc:
                             e.stream_shuffle(
                                 lb[32 * b4:32 * b4 + 32, cc:cc + TT],
                                 tt_[64:96, 0:TT], mask=[0] * 32),
                             waits=[("v", p.cnt["v"])] if b4 == 0 else (),
                             inc=("v" if b4 == 3 else None))
                    p.op("vector", lambda e, pr=pr: e.tensor_tensor(
                        out=OnT[0:64, TT * pr:TT * pr + TT],
                        in0=OnT[0:64, TT * pr:TT * pr + TT],
                        in1=lb[0:64, 0:TT], op=OP.mult),
                        waits=[("s", SCP)])
                    p.op("vector", lambda e, pr=pr: e.tensor_tensor(
                        out=OnT[64:128, TT * pr:TT * pr + TT],
                        in0=OnT[64:128, TT * pr:TT * pr + TT],
                        in1=lb[64:128, TT:2 * TT], op=OP.mult),
                        waits=[("sh_st", SHD)], inc="v")

            ALLON = p.cnt["v"]
            if T + 1 < NTT:
                H1R_next, _ = ln_tile(h1T, 0, [("x_ld", XLN)], xsrc=nxt)
            for fc in range(NCH):
                bi = rotg % 8; rotg += 1
                bk = bank(bi)
                for hp in range(NHP):
                    p.op("tensor", lambda e, fc=fc, hp=hp, bk=bk: e.matmul(
                        bk, woutB[:, 1024 * hp + 128 * fc:1024 * hp + 128 * fc + 128],
                        OnT[:, TT * hp:TT * hp + TT],
                        start=(hp == 0), stop=(hp == NHP - 1)),
                        waits=([("v", ALLON)] + p.bank_wait(bi))
                        if hp == 0 else (),
                        inc=("t" if hp == NHP - 1 else None))
                p.op("vector", lambda e, bk=bk, fc=fc, cur=cur:
                     e.scalar_tensor_tensor(
                         out=cur[:, TT * fc:TT * fc + TT], in0=bk,
                         scalar=boutF[:, fc:fc + 1],
                         in1=cur[:, TT * fc:TT * fc + TT], op0=OP.add, op1=OP.add),
                     waits=[("t", p.cnt["t"])], inc="v")
                p.bank_mark(bi, "v")
                p.dma("sync", xp[128 * fc:128 * fc + 128, TT * T:TT * T + TT],
                      cur[:, TT * fc:TT * fc + TT], "xp_st",
                      waits=[("v", p.cnt["v"])])
            SPILL_MARK = p.cnt["xp_st"]
            if debug and T == 0:
                p.dma("gpsimd", dbg["d_h1T"], h1T[:], "y_st", waits=[("v", H1R)])
                p.dma("gpsimd", dbg["d_ont"], OnT[:], "y_st",
                      waits=[("v", ALLON)])
                p.dma("gpsimd", dbg["d_nb"], negBc[:], "y_st", waits=[("v", NB_V)])
                p.dma("gpsimd", dbg["d_bbt"], bbt[:], "y_st")
                p.dma("gpsimd", dbg["d_rmx"], rmxQ[:], "y_st")

        FIN = {k: p.cnt[k] for k in p.cnt}
        p.flush(blkA)
        blkA_es.close()
        esA.close()

        esB = es.enter_context(contextlib.ExitStack())
        def sbB(name, shape, dt):
            return esB.enter_context(nc.sbuf_tensor(name, shape, dt))

        tD = sbB("tD", [128, TT], F32)
        tE = sbB("tE", [128, TT], F32)
        tF = sbB("tF", [128, TT], F32)
        h2T = sbB("h2T", [128, 2 * NCH * TT], BF16)
        wgB = sbB("wgB", [128, 2 * NCH * DI], BF16)
        wuB = sbB("wuB", [128, 2 * NCH * DI], BF16)
        wdB = sbB("wdB", [128, 2 * 4 * D], BF16)
        actB = sbB("actB", [128, 4 * TT], BF16)
        routed = sbB("routed", [128, 2 * NCH * TT], F32)
        cwb = sbB("cwb", [128, TT], F32)

        blkB_es = contextlib.ExitStack()
        blkB = blkB_es.enter_context(nc.Block())
        bw = ([(k, FIN[k]) for k in ("t", "s", "v")] +
              [(k, FIN[k]) for k in ("c_ld", "x_ld", "kv_ld", "kv_st",
                                     "xp_st", "sh_st", "y_st")])
        p.extra_first = {e: list(bw) for e in ENGINES}

        wslot_cnt = {}
        stg_guard = {}
        wchain = []
        wcast_done = {}
        for ST in range(0 if DEV_SKIP_B else NST):
            h2rv = {}
            for tt in range(2):
                T = 2 * ST + tt
                xw = [("y_st", p.cnt["y_st"]), ("t", p.cnt["t"]),
                      ("v", p.cnt["v"]), ("s", p.cnt["s"])]
                for c in range(NCH):
                    p.dma("sync", xio[:, TT * c:TT * c + TT],
                          xp[128 * c:128 * c + 128, TT * T:TT * T + TT], "x_ld",
                          waits=xw if c == 0 else ())
                XL = p.cnt["x_ld"]
                H2R, VMR = ln_tile(h2T, NCH * TT * tt, [("x_ld", XL)])
                h2rv[tt] = H2R
                for c in range(NCH):
                    p.op("tensor", lambda e, c=c: e.matmul(
                        bank(4)[0:8, :], gwF[:, 8 * c:8 * c + 8],
                        xio[:, TT * c:TT * c + TT],
                        start=(c == 0), stop=(c == NCH - 1)),
                        waits=([("v", H2R)] + p.bank_wait(4)) if c == 0 else (),
                        inc=("t" if c == NCH - 1 else None))
                p.op("scalar", lambda e: e.copy(out=glbuf[:], in_=bank(4)[0:8, :]),
                     waits=[("t", p.cnt["t"])], inc="s")
                p.bank_mark(4, "s")
                GLE = p.cnt["s"]
                for sub in range(4):
                    p.op("tensor", lambda e, sub=sub: e.matmul(
                        bank(5)[:, 0:8], glbuf[:, 128 * sub:128 * sub + 128], i8F[:],
                        start=True, stop=True),
                        waits=[("s", GLE), ("v", VMR)] + p.bank_wait(5))
                    p.op("tensor", lambda e, sub=sub: e.matmul(
                        bank(5)[:, 8:9], r_rstd[0:1, 128 * sub:128 * sub + 128],
                        onesF[0:1, 0:1], start=True, stop=True))
                    p.op("tensor", lambda e, sub=sub: e.matmul(
                        bank(5)[:, 9:10], r_mr[0:1, 128 * sub:128 * sub + 128],
                        onesF[0:1, 0:1], start=True, stop=True), inc="t")
                    p.op("scalar", lambda e, sub=sub: e.copy(
                        out=gtm[:, 10 * sub:10 * sub + 10], in_=bank(5)[:, 0:10]),
                        waits=[("t", p.cnt["t"])], inc="s")
                    p.bank_mark(5, "s")
                GTE = p.cnt["s"]

                o8 = lambda t_, sub: t_[:, 8 * sub:8 * sub + 8]
                o4_ = lambda t_, sub: t_[:, 4 * sub:4 * sub + 4]
                def rop(mk, inc=None, w=None):
                    for sub in range(4):
                        p.op("vector", mk(sub), waits=(w or []) if sub == 0 else [],
                             inc=(inc if sub == 3 else None))
                rop(lambda sub: (lambda e: e.tensor_scalar(
                    out=o8(rA_, sub), in0=gtm[:, 10 * sub:10 * sub + 8],
                    scalar1=gtm[:, 10 * sub + 8:10 * sub + 9], scalar2=None,
                    op0=OP.mult)), w=[("s", GTE)])
                rop(lambda sub: (lambda e: e.tensor_scalar(
                    out=o8(rB_, sub), in0=gcol[:],
                    scalar1=gtm[:, 10 * sub + 9:10 * sub + 10], scalar2=None,
                    op0=OP.mult)))
                rop(lambda sub: (lambda e: e.tensor_tensor(
                    out=o8(rL_, sub), in0=o8(rA_, sub), in1=o8(rB_, sub),
                    op=OP.subtract)), inc="v")
                p.op("scalar", lambda e: e.activation(out=rS_[:], in_=rL_[:],
                     func=AF.Sigmoid), waits=[("v", p.cnt["v"])], inc="s")
                SC = p.cnt["s"]
                def ropS(mk, inc=None):
                    for sub in range(4):
                        p.op("vector", mk(sub), waits=[("s", SC)] if sub == 0 else [],
                             inc=(inc if sub == 3 else None))
                ropS(lambda sub: (lambda e: e.tensor_tensor(
                    out=o8(rC_, sub), in0=o8(rS_, sub), in1=cbb[:], op=OP.add)))
                ropS(lambda sub: (lambda e: e.tensor_tensor(
                    out=o4_(rG4, sub), in0=rC_[:, 8 * sub:8 * sub + 8:2],
                    in1=rC_[:, 8 * sub + 1:8 * sub + 8:2], op=OP.add)))
                ropS(lambda sub: (lambda e: e.tensor_reduce(
                    out=rM_[:, 8 * sub:8 * sub + 1], in_=o4_(rG4, sub),
                    axis=AX, op=OP.max)))
                ropS(lambda sub: (lambda e: e.tensor_scalar(
                    out=o4_(rI_, sub), in0=o4_(rG4, sub),
                    scalar1=rM_[:, 8 * sub:8 * sub + 1], scalar2=None,
                    op0=OP.is_ge)))
                ropS(lambda sub: (lambda e: e.scalar_tensor_tensor(
                    out=o4_(rW_, sub), in0=o4_(rI_, sub), scalar=-BIG,
                    in1=o4_(rG4, sub), op0=OP.mult, op1=OP.add)))
                ropS(lambda sub: (lambda e: e.tensor_reduce(
                    out=rM_[:, 8 * sub + 1:8 * sub + 2], in_=o4_(rW_, sub),
                    axis=AX, op=OP.max)))
                ropS(lambda sub: (lambda e: e.tensor_scalar(
                    out=o4_(rI_, sub), in0=o4_(rG4, sub),
                    scalar1=rM_[:, 8 * sub + 1:8 * sub + 2], scalar2=None,
                    op0=OP.is_ge)))
                ropS(lambda sub: (lambda e: e.tensor_copy(
                    out=rSM[:, 8 * sub:8 * sub + 8:2], in_=o4_(rI_, sub))))
                ropS(lambda sub: (lambda e: e.tensor_copy(
                    out=rSM[:, 8 * sub + 1:8 * sub + 8:2], in_=o4_(rI_, sub))))
                ropS(lambda sub: (lambda e: e.tensor_tensor(
                    out=o8(rFC, sub), in0=o8(rSM, sub), in1=o8(rC_, sub),
                    op=OP.mult)))
                ropS(lambda sub: (lambda e: e.tensor_scalar(
                    out=o8(rSM, sub), in0=o8(rSM, sub), scalar1=1.0,
                    scalar2=None, op0=OP.subtract)))
                ropS(lambda sub: (lambda e: e.tensor_tensor(
                    out=o8(rB_, sub), in0=o8(rSM, sub), in1=idxE[:], op=OP.mult)))
                ropS(lambda sub: (lambda e: e.tensor_tensor(
                    out=o8(rFC, sub), in0=o8(rFC, sub), in1=o8(rB_, sub),
                    op=OP.add)))
                ropS(lambda sub: (lambda e: e.tensor_copy(
                    out=o8(rW_, sub), in_=o8(rFC, sub))))
                for rnd in range(3):
                    ropS(lambda sub: (lambda e: e.tensor_reduce(
                        out=rM_[:, 8 * sub + 2:8 * sub + 3], in_=o8(rW_, sub),
                        axis=AX, op=OP.max)))
                    ropS(lambda sub: (lambda e: e.tensor_scalar(
                        out=o8(rI_, sub), in0=o8(rW_, sub),
                        scalar1=rM_[:, 8 * sub + 2:8 * sub + 3], scalar2=None,
                        op0=OP.is_ge)))
                    ropS(lambda sub: (lambda e: e.scalar_tensor_tensor(
                        out=o8(rW_, sub), in0=o8(rI_, sub), scalar=-BIG,
                        in1=o8(rW_, sub), op0=OP.mult, op1=OP.add)))
                ropS(lambda sub: (lambda e: e.tensor_reduce(
                    out=rM_[:, 8 * sub + 3:8 * sub + 4], in_=o8(rW_, sub),
                    axis=AX, op=OP.max)))
                ropS(lambda sub: (lambda e: e.tensor_scalar(
                    out=o8(rI_, sub), in0=o8(rFC, sub),
                    scalar1=rM_[:, 8 * sub + 3:8 * sub + 4], scalar2=None,
                    op0=OP.is_ge)))
                ropS(lambda sub: (lambda e: e.tensor_tensor(
                    out=o8(rW_, sub), in0=o8(rI_, sub), in1=o8(rS_, sub),
                    op=OP.mult)))
                ropS(lambda sub: (lambda e: e.tensor_reduce(
                    out=rM_[:, 8 * sub + 4:8 * sub + 5], in_=o8(rW_, sub),
                    axis=AX, op=OP.add)))
                ropS(lambda sub: (lambda e: e.tensor_scalar(
                    out=rM_[:, 8 * sub + 5:8 * sub + 6],
                    in0=rM_[:, 8 * sub + 4:8 * sub + 5], scalar1=1e-20,
                    scalar2=None, op0=OP.add)))
                ropS(lambda sub: (lambda e: e.reciprocal(
                    out=rM_[:, 8 * sub + 6:8 * sub + 7],
                    in_=rM_[:, 8 * sub + 5:8 * sub + 6])))
                ropS(lambda sub: (lambda e: e.tensor_scalar(
                    out=rM_[:, 8 * sub + 7:8 * sub + 8],
                    in0=rM_[:, 8 * sub + 6:8 * sub + 7], scalar1=2.5,
                    scalar2=None, op0=OP.mult)))
                ropS(lambda sub: (lambda e: e.tensor_scalar(
                    out=o8(rCW, sub), in0=o8(rW_, sub),
                    scalar1=rM_[:, 8 * sub + 7:8 * sub + 8], scalar2=None,
                    op0=OP.mult)), inc="v")
                CWV = p.cnt["v"]
                for sub in range(4):
                    p.op("tensor", lambda e, sub=sub: e.matmul(
                        bank(5)[0:8, 0:128], rCW[:, 8 * sub:8 * sub + 8], i128F[:],
                        start=True, stop=True),
                        waits=[("v", CWV)] + p.bank_wait(5), inc="t")
                    p.op("scalar", lambda e, sub=sub, tt=tt: e.copy(
                        out=cwT[0:8, TT * tt + 128 * sub:TT * tt + 128 * sub + 128],
                        in_=bank(5)[0:8, 0:128]), waits=[("t", p.cnt["t"])], inc="s")
                    p.bank_mark(5, "s")
            CWS = p.cnt["s"]
            if debug and ST == 0:
                p.dma("gpsimd", dbg["d_h2T"], h2T[:, 0:NCH * TT], "y_st",
                      waits=[("v", p.cnt["v"])])
                p.dma("gpsimd", dbg["d_cwT"], cwT[:], "y_st", waits=[("s", CWS)])

            def queue_chain(gi):
                e_idx = ([NE] + list(range(NE)))[gi % 9]
                sl = gi % 2
                for step, (which, wt_dram, blk) in enumerate(
                        [(w_, d_, b_) for (w_, d_) in (("g", wg), ("u", wu),
                                                       ("d", wd))
                         for b_ in range(4)]):
                    dstb = {"g": wgB, "u": wuB, "d": wdB}[which]
                    if which != "d":
                        din_ap = wt_dram[e_idx, 256 * blk:256 * blk + 256, :] \
                            .rearrange("(k p) x -> p k x", k=2)
                        off = NCH * DI * sl + 1024 * blk
                    else:
                        din_ap = wt_dram[e_idx, 128 * blk:128 * blk + 128, :]
                        off = 4 * D * sl + D * blk
                    wchain.append((gi, sl, step, din_ap, dstb, off))

            def drain_step():
                if not wchain:
                    return
                gi, sl, step, din_ap, dstb, off = wchain.pop(0)
                s0 = 1024 * (step % 2)
                wt = list(stg_guard.get(step % 2, []))
                if step == 0 and wslot_cnt.get(sl) is not None:
                    wt.append(("t", wslot_cnt[sl]))
                dout_ap = xn[:, s0:s0 + 1024]
                if din_ap.shape != (128, 1024):
                    dout_ap = dout_ap.rearrange("p (k x) -> p k x", k=2)
                p.dma("gpsimd", dout_ap, din_ap, "w_ld", waits=wt)
                p.op("scalar", lambda e, dstb=dstb, off=off, s0=s0:
                     e.copy(out=dstb[:, off:off + 1024],
                            in_=xn[:, s0:s0 + 1024]),
                     waits=[("w_ld", p.cnt["w_ld"])], inc="s")
                stg_guard[step % 2] = [("s", p.cnt["s"])]
                wcast_done[gi] = p.cnt["s"]

            def drain_until(gi):
                while wchain and wchain[0][0] <= gi:
                    drain_step()

            for ei, e_idx in enumerate([NE] + list(range(NE))):
                gi = 9 * ST + ei
                sl = gi % 2
                if gi == 0:
                    queue_chain(0)
                drain_until(gi)
                WCAST_S = wcast_done[gi]
                if gi + 1 < 9 * NST:
                    queue_chain(gi + 1)

                for tt in range(2):
                    if e_idx != NE:
                        for b4 in range(4):
                            p.op("vector", lambda e, b4=b4, tt=tt, e_idx=e_idx:
                                 e.stream_shuffle(cwb[32 * b4:32 * b4 + 32, :],
                                                  cwT[0:32, TT * tt:TT * tt + TT],
                                                  mask=[e_idx] * 32),
                                 waits=[("s", CWS), ("t", p.cnt["t"])]
                                 if b4 == 0 else (),
                                 inc=("v" if b4 == 3 else None))
                    CWBV = p.cnt["v"]
                    mults = []
                    act_mark = {}
                    for ic in range(4):
                        gb = 2 * (ic % 2)
                        ub = gb + 1
                        for wsb, bki in ((wgB, gb), (wuB, ub)):
                            bk = bank(bki)
                            for hc in range(NCH):
                                p.op("tensor", lambda e, wsb=wsb, bk=bk, ic=ic,
                                     hc=hc, sl=sl, tt=tt: e.matmul(
                                         bk,
                                         wsb[:, NCH * DI * sl + DI * hc + 128 * ic:
                                             NCH * DI * sl + DI * hc + 128 * ic + 128],
                                         h2T[:, NCH * TT * tt + TT * hc:
                                             NCH * TT * tt + TT * hc + TT],
                                         start=(hc == 0), stop=(hc == NCH - 1)),
                                     waits=([("s", WCAST_S), ("v", h2rv[tt])] +
                                            p.bank_wait(bki))
                                     if hc == 0 else (),
                                     inc=("t" if hc == NCH - 1 else None))
                        stmp = (tA, tB, tC)[ic % 3]
                        p.op("scalar", lambda e, stmp=stmp, gb=gb: e.activation(
                            out=stmp[:], in_=bank(gb), func=AF.Silu),
                            waits=[("t", p.cnt["t"]), ("v", p.cnt["v"])], inc="s")
                        p.bank_mark(gb, "s")
                        drain_step()
                        mtmp = (tD, tE, tF)[ic % 3]
                        p.op("vector", lambda e, stmp=stmp, mtmp=mtmp, ub=ub:
                             e.tensor_tensor(out=mtmp[:], in0=stmp[:],
                                             in1=bank(ub), op=OP.mult),
                             waits=[("s", p.cnt["s"])], inc="v")
                        p.bank_mark(ub, "v")
                        drain_step()
                        mults.append((mtmp, ic))
                        if len(mults) == 2:
                            mt, mic = mults.pop(0)
                            if e_idx == NE:
                                p.op("vector", lambda e, mt=mt, mic=mic:
                                     e.tensor_copy(out=actB[:, TT * mic:TT * mic + TT],
                                                   in_=mt[:]), inc="v")
                            else:
                                p.op("vector", lambda e, mt=mt, mic=mic:
                                     e.tensor_tensor(out=actB[:, TT * mic:TT * mic + TT],
                                                     in0=mt[:], in1=cwb[:],
                                                     op=OP.mult),
                                     waits=[("v", CWBV)], inc="v")
                            act_mark[mic] = p.cnt["v"]
                    mt, mic = mults.pop(0)
                    if e_idx == NE:
                        p.op("vector", lambda e, mt=mt, mic=mic: e.tensor_copy(
                            out=actB[:, TT * mic:TT * mic + TT], in_=mt[:]), inc="v")
                    else:
                        p.op("vector", lambda e, mt=mt, mic=mic: e.tensor_tensor(
                            out=actB[:, TT * mic:TT * mic + TT], in0=mt[:],
                            in1=cwb[:], op=OP.mult), waits=[("v", CWBV)], inc="v")
                    act_mark[mic] = p.cnt["v"]
                    ACTV = p.cnt["v"]
                    for hc in range(NCH):
                        bi = 4 + hc % 4
                        bk = bank(bi)
                        for ic in range(4):
                            p.op("tensor", lambda e, hc=hc, ic=ic, sl=sl, bk=bk:
                                 e.matmul(bk,
                                          wdB[:, 4 * D * sl + D * ic + 128 * hc:
                                              4 * D * sl + D * ic + 128 * hc + 128],
                                          actB[:, TT * ic:TT * ic + TT],
                                          start=(ic == 0), stop=(ic == 3)),
                                 waits=([("v", act_mark[ic])] +
                                        (p.bank_wait(bi) if ic == 0 else [])),
                                 inc=("t" if ic == 3 else None))
                        wslot_cnt[sl] = p.cnt["t"]
                        dst = routed[:, NCH * TT * tt + TT * hc:
                                     NCH * TT * tt + TT * hc + TT]
                        if ei == 0:
                            p.op("vector", lambda e, dst=dst, bk=bk:
                                 e.tensor_copy(out=dst, in_=bk),
                                 waits=[("t", p.cnt["t"])], inc="v")
                        else:
                            p.op("vector", lambda e, dst=dst, bk=bk:
                                 e.tensor_tensor(out=dst, in0=dst, in1=bk,
                                                 op=OP.add),
                                 waits=[("t", p.cnt["t"])], inc="v")
                        p.bank_mark(bi, "v")
            ROUT = p.cnt["v"]
            for tt in range(2):
                T = 2 * ST + tt
                xw = [("v", ROUT), ("y_st", p.cnt["y_st"])]
                for c in range(NCH):
                    p.dma("sync", xio[:, TT * c:TT * c + TT],
                          xp[128 * c:128 * c + 128, TT * T:TT * T + TT], "x_ld",
                          waits=xw if c == 0 else ())
                XL = p.cnt["x_ld"]
                for c in range(NCH):
                    p.op("vector", lambda e, c=c, tt=tt: e.tensor_tensor(
                        out=xio[:, TT * c:TT * c + TT],
                        in0=xio[:, TT * c:TT * c + TT],
                        in1=routed[:, NCH * TT * tt + TT * c:
                                   NCH * TT * tt + TT * c + TT],
                        op=OP.add), waits=[("x_ld", XL)], inc="v")
                    p.dma("sync", yT[128 * c:128 * c + 128, TT * T:TT * T + TT],
                          xio[:, TT * c:TT * c + TT], "y_st",
                          waits=[("v", p.cnt["v"])])
        if p.cnt["y_st"]:
            p.op("sync", lambda e: e.nop(), waits=[("y_st", p.cnt["y_st"])])
        p.flush(blkB)
        blkB_es.close()
        esB.close()
    return nc


_QKV_PERM = None
def _qkv_perm():
    global _QKV_PERM
    if _QKV_PERM is None:
        perm = []
        for hp in range(NHP):
            for h in (2 * hp, 2 * hp + 1):
                perm.extend(range(h * 192, h * 192 + 64))
        for hp in range(NHP):
            for h in (2 * hp, 2 * hp + 1):
                perm.extend(range(h * 192 + 64, h * 192 + 128))
        for h in range(NH):
            perm.extend(range(h * 192 + 128, h * 192 + 192))
        _QKV_PERM = np.array(perm)
    return _QKV_PERM


_NC_CACHE = {}
def get_nc(debug=False):
    if debug not in _NC_CACHE:
        _NC_CACHE[debug] = build_nc(debug)
    return _NC_CACHE[debug]


def make_inputs(L0_attn_x, L0_attn_w_qkv, L0_attn_w_out, L0_attn_b_out,
                L0_ffn_gate_w, L0_ffn_correction_bias, L0_ffn_Wg, L0_ffn_Wu,
                L0_ffn_Wd, L0_ffn_Wg_s, L0_ffn_Wu_s, L0_ffn_Wd_s):
    f = np.float32
    wqkvT = np.ascontiguousarray(np.asarray(L0_attn_w_qkv, f).T[:, _qkv_perm()])
    woutT = np.ascontiguousarray(np.asarray(L0_attn_w_out, f).T)
    gwT = np.ascontiguousarray(np.asarray(L0_ffn_gate_w, f).T)
    bout = np.ascontiguousarray(np.asarray(L0_attn_b_out, f).reshape(NCH, 128).T)
    cbias = np.ascontiguousarray(np.asarray(L0_ffn_correction_bias, f).reshape(1, NE))
    idxeps = np.ascontiguousarray(
        np.tile((np.arange(1, NE + 1, dtype=f) * f(1e-30)).reshape(1, NE), (128, 1)))
    kk = np.arange(128).reshape(1, 128, 1)
    qq = np.arange(TT).reshape(1, 1, TT)
    jj = np.arange(4).reshape(4, 1, 1)
    mk = (128 * jj + kk <= qq).astype(f)
    mk = np.ascontiguousarray(np.tile(mk, (1, 1, 2)))
    onesf = np.ones((128, 128), f)
    hones = np.zeros((128, 2), f); hones[0:64, 0] = 1.0; hones[64:128, 1] = 1.0
    i8 = np.eye(8, dtype=f)
    i128 = np.eye(128, dtype=f)
    wg_ = np.ascontiguousarray(np.concatenate(
        [np.asarray(L0_ffn_Wg, f), np.asarray(L0_ffn_Wg_s, f)[None]], axis=0))
    wu_ = np.ascontiguousarray(np.concatenate(
        [np.asarray(L0_ffn_Wu, f), np.asarray(L0_ffn_Wu_s, f)[None]], axis=0))
    wd_ = np.ascontiguousarray(np.concatenate(
        [np.asarray(L0_ffn_Wd, f), np.asarray(L0_ffn_Wd_s, f)[None]], axis=0))
    shared = dict(wqkvT=wqkvT, woutT=woutT, gwT=gwT, bout=bout, cbias=cbias,
                  idxeps=idxeps, masks=mk, onesf=onesf, hones=hones, i8=i8,
                  i128=i128, wg=wg_, wu=wu_, wd=wd_)
    x = np.asarray(L0_attn_x, f)
    maps = []
    for core in range(NCORES):
        m = dict(shared)
        m["xT"] = np.ascontiguousarray(x[core].T)
        maps.append(m)
    return maps


def kernel(**inputs):
    maps = make_inputs(**inputs)
    nc = get_nc(debug=False)
    res = run_bass_kernel_spmd(nc, maps, core_ids=list(range(NCORES)))
    out = np.stack([np.ascontiguousarray(r["yT"].T) for r in res.results], axis=0)
    return out.astype(np.float32)


# version 21
# speedup vs baseline: 1.0779x; 1.0067x over previous
import contextlib
import numpy as np

import concourse.bass as bass
import concourse.mybir as mybir
from concourse.bass_utils import run_bass_kernel_spmd
from concourse.alu_op_type import AluOpType as OP

F32 = mybir.dt.float32
BF16 = mybir.dt.bfloat16
AF = mybir.ActivationFunctionType
AX = mybir.AxisListType.X

NCORES = 8
SEQ = 4096
D = 1024
NH = 16
NHP = 8
NT = 8
TT = 512
NCH = 8
NE = 8
DI = 512
NST = 4
SCALE = 0.125
EPS = 1e-5
BIG = 1.0e30
NKV = 3

ENGINES = ("sync", "gpsimd", "tensor", "vector", "scalar")


class Prog:

    def __init__(self, nc, es):
        self.nc = nc
        self.q = {e: [] for e in ENGINES}
        self.cnt = {}
        self.sems = {}
        self.es = es
        self.extra_first = {}
        self.banks = {}

    def sem(self, name):
        h = self.es.enter_context(self.nc.semaphore(name))
        self.sems[name] = h
        self.cnt[name] = 0
        return name

    def bank_wait(self, i):
        return list(self.banks.get(i, []))

    def bank_mark(self, i, semname):
        self.banks[i] = [(semname, self.cnt[semname])]

    def op(self, eng, fn, waits=(), inc=None, n=1):
        waits = list(waits)
        if eng in self.extra_first:
            waits += self.extra_first.pop(eng)
        for s, t in waits:
            assert t <= self.cnt[s], (eng, s, t, self.cnt[s])
        if inc is not None:
            self.cnt[inc] += n
        self.q[eng].append((fn, tuple(waits), inc, n))
        return self.cnt[inc] if inc is not None else None

    def dma(self, eng, out, in_, sem, waits=()):
        return self.op(eng, lambda e, o=out, i=in_: e.dma_start(out=o, in_=i),
                       waits, sem, 16)

    def flush(self, block):
        for name in ENGINES:
            ops = self.q[name]

            def body(h, ops=ops):
                for fn, waits, inc, n in ops:
                    for s, t in waits:
                        h.wait_ge(self.sems[s], t)
                    r = fn(h)
                    if inc is not None:
                        r.then_inc(self.sems[inc], n)

            getattr(block, name)(body)
        self.q = {e: [] for e in ENGINES}


DEV_NT = None
DEV_SKIP_B = False
DEV_DBG_T = 0


def build_nc(debug=False):
    nc = bass.Bass("TRN2", target_bir_lowering=False)

    def din(name, shape, dt=F32):
        return nc.dram_tensor(name, shape, dt, kind="ExternalInput").ap()

    xT = din("xT", [D, SEQ])
    wqkvT = din("wqkvT", [D, 3 * D])
    woutT = din("woutT", [D, D])
    gwT = din("gwT", [D, NE])
    bout = din("bout", [128, NCH])
    cbias = din("cbias", [1, NE])
    idxeps = din("idxeps", [128, NE])
    masks = din("masks", [4, 128, 2 * TT])
    onesf = din("onesf", [128, 128])
    hones = din("hones", [128, 2])
    i8 = din("i8", [8, 8])
    i128 = din("i128", [128, 128])
    wg = din("wg", [NE + 1, D, DI])
    wu = din("wu", [NE + 1, D, DI])
    wd = din("wd", [NE + 1, DI, D])

    yT = nc.dram_tensor("yT", [D, SEQ], F32, kind="ExternalOutput").ap()
    xp = nc.dram_tensor("xp", [D, SEQ], F32, kind="Internal").ap()
    ksp = nc.dram_tensor("ksp", [SEQ // 128, 128, 1024], BF16, kind="Internal").ap()
    vsp = nc.dram_tensor("vsp", [SEQ // 128, 128, 1040], BF16, kind="Internal").ap()

    dbg = {}
    if debug:
        for name, shape, dt in [
            ("d_h1T", [128, NCH * TT], BF16), ("d_ont", [128, NHP * TT], BF16),
            ("d_h2T", [128, NCH * TT], BF16), ("d_cwT", [32, 2 * TT], F32),
            ("d_nb", [128, 4], F32), ("d_bbt", [32, 48], F32),
            ("d_rmx", [32, NHP], F32),
        ]:
            dbg[name] = nc.dram_tensor(name, shape, dt, kind="ExternalOutput").ap()

    es = contextlib.ExitStack()
    with es:
        def sb(name, shape, dt):
            return es.enter_context(nc.sbuf_tensor(name, shape, dt))

        xio = sb("xio", [128, NCH * TT], F32)
        xn = sb("xn", [128, NCH * TT], F32)
        onesF = sb("onesF", [128, 128], F32)
        onesB = sb("onesB", [128, 128], BF16)
        i8F = sb("i8F", [8, 8], F32)
        i128F = sb("i128F", [128, 128], F32)
        boutF = sb("boutF", [128, NCH], F32)
        crow = sb("crow", [32, 16], F32)
        gcol = sb("gcol", [128, NE], F32)
        cbb = sb("cbb", [128, NE], F32)
        idxE = sb("idxE", [128, NE], F32)
        gwF = sb("gwF", [128, NCH * NE], F32)
        r_q = sb("r_q", [1, TT], F32)
        r_mean = sb("r_mean", [1, TT], F32)
        r_q2 = sb("r_q2", [1, TT], F32)
        r_v = sb("r_v", [1, TT], F32)
        r_ve = sb("r_ve", [1, TT], F32)
        r_ln = sb("r_ln", [1, TT], F32)
        r_rstd = sb("r_rstd", [1, TT], F32)
        r_mr = sb("r_mr", [1, TT], F32)
        tA = sb("tA", [128, TT], F32)
        tB = sb("tB", [128, TT], F32)
        tC = sb("tC", [128, TT], F32)
        xbb = sb("xbb", [128, 3 * TT], BF16)
        xsq = sb("xsq", [128, 3 * TT], BF16)
        maskB = sb("maskB", [128, 4 * 2 * TT], BF16)
        lb = sb("lb", [128, 2 * TT], F32)
        h1T = sb("h1T", [128, NCH * TT], BF16)
        honesB = sb("honesB", [128, 2], BF16)
        rmxQ = sb("rmxQ", [32, NHP], F32)
        rmxK = sb("rmxK", [32, NHP], F32)
        bbt = sb("bbt", [32, 48], F32)
        negBc = sb("negBc", [128, 4], F32)
        gtm = sb("gtm", [128, 4 * 10], F32)
        def rt(name):
            return sb(name, [128, 4 * NE], F32)
        rA_, rB_, rL_, rS_, rC_ = rt("rtA"), rt("rtB"), rt("rtL"), rt("rtS"), rt("rtC")
        rG4 = sb("rG4", [128, 4 * 4], F32)
        rM_ = sb("rM_", [128, 4 * 8], F32)
        rI_, rW_, rSM, rFC, rCW = rt("rtI"), rt("rtW"), rt("rtSM"), rt("rtFC"), rt("rtCW")
        glbuf = sb("glbuf", [8, TT], F32)
        cwT = sb("cwT", [32, 2 * TT], F32)

        PSA = es.enter_context(nc.psum_tensor("psA", [128, 2048], F32))
        PSB = es.enter_context(nc.psum_tensor("psB", [128, 2048], F32))
        PS = [PSA[:, 0:1024], PSA[:, 1024:2048], PSB[:, 0:1024], PSB[:, 1024:2048]]
        def bank(i):
            return (PSA if i < 4 else PSB)[:, 512 * (i % 4):512 * (i % 4) + 512]

        p = Prog(nc, es)
        for s in ("c_ld", "x_ld", "kv_ld", "w_ld", "kv_st", "xp_st", "sh_st",
                  "y_st", "t", "s", "v", "g"):
            p.sem(s)

        esA = es.enter_context(contextlib.ExitStack())
        def sbA(name, shape, dt):
            return esA.enter_context(nc.sbuf_tensor(name, shape, dt))

        wqkvB = sbA("wqkvB", [128, NCH * 3072], BF16)
        woutB = sbA("woutB", [128, NCH * 1024], BF16)
        Qp = sbA("Qp", [128, NHP * TT], BF16)
        Kc = sbA("Kc", [128, NHP * TT], BF16)
        Vc = sbA("Vc", [128, 4 * 1040], BF16)
        kvK = sbA("kvK", [128, NKV * 1024], BF16)
        kvV = sbA("kvV", [128, NKV * 1040], BF16)
        Pb = sbA("Pb", [128, 4 * 1024], BF16)
        OnT = sbA("OnT", [128, NHP * TT], BF16)
        Ot = sbA("Ot", [64, 2 * TT], BF16)

        blkA_es = contextlib.ExitStack()
        blkA = blkA_es.enter_context(nc.Block())

        for dst, src in [(onesF[:], onesf), (i8F[:], i8), (i128F[:], i128),
                         (boutF[:], bout), (idxE[:], idxeps),
                         (crow[0:1, 8:16], cbias)]:
            p.dma("sync", dst, src, "c_ld")
        for c in range(NCH):
            p.dma("sync", gwF[:, 8 * c:8 * c + 8], gwT[128 * c:128 * c + 128, :], "c_ld")
        p.dma("sync", xn[:, 0:2], hones, "c_ld")
        p.op("scalar", lambda e: e.copy(out=onesB[:], in_=onesF[:]),
             waits=[("c_ld", 16)], inc="s")
        p.op("scalar", lambda e: e.copy(out=honesB[:], in_=xn[:, 0:2]),
             waits=[("c_ld", p.cnt["c_ld"])], inc="s")
        ONES_S = p.cnt["s"]
        p.op("vector", lambda e: e.memset(rmxQ[:], 0.0), inc="v")
        p.op("vector", lambda e: e.memset(rmxK[:], 0.0), inc="v")
        for c in range(NCH):
            p.op("tensor", lambda e, c=c: e.matmul(
                bank(0)[0:1, 0:8], onesF[:, 0:1], gwF[:, 8 * c:8 * c + 8],
                start=(c == 0), stop=(c == NCH - 1)),
                waits=[("c_ld", 16 * (7 + NCH))], inc="t")
        p.op("scalar", lambda e: e.copy(out=crow[0:1, 0:8], in_=bank(0)[0:1, 0:8]),
             waits=[("t", p.cnt["t"])], inc="s")
        p.bank_mark(0, "s")
        SROW = p.cnt["s"]
        for b in range(4):
            p.op("vector", lambda e, b=b: e.stream_shuffle(
                gcol[32 * b:32 * b + 32, :], crow[:, 0:8], mask=[0] * 32),
                waits=[("s", SROW), ("c_ld", 16 * 6)])
            p.op("vector", lambda e, b=b: e.stream_shuffle(
                cbb[32 * b:32 * b + 32, :], crow[:, 8:16], mask=[0] * 32),
                waits=[("s", SROW)], inc="v")
        p.op("vector", lambda e: e.memset(
            Vc[:].rearrange("p (k h x) -> p k h x", k=4, h=NH)[:, :, :, 64:65], 1.0),
            inc="v")

        def ln_tile(out_h1, out_base, xwaits, xsrc=None):
            xwaits = list(xwaits)
            xsrc = xio if xsrc is None else xsrc
            t_entry = p.cnt["t"]
            slot_guard = {i: [("t", t_entry)] for i in range(3)}
            for c in range(NCH):
                sl = (c % 3) * TT
                xa = xsrc[:, TT * c:TT * c + TT]
                w0 = (xwaits if c == 0 else []) + slot_guard.get(c % 3, [])
                p.op("scalar", lambda e, xa=xa, sl=sl:
                     e.copy(out=xbb[:, sl:sl + TT], in_=xa), waits=w0, inc="s")
                p.op("vector", lambda e, xa=xa, sl=sl:
                     e.tensor_tensor(out=xsq[:, sl:sl + TT], in0=xa, in1=xa,
                                     op=OP.mult), waits=w0, inc="v")
                p.op("tensor", lambda e, c=c, sl=sl: e.matmul(
                    bank(0)[0:1, 0:TT], onesB[:, 0:1], xbb[:, sl:sl + TT],
                    start=(c == 0), stop=(c == NCH - 1)),
                    waits=[("s", p.cnt["s"])] + (p.bank_wait(0) if c == 0 else []))
                p.op("tensor", lambda e, c=c, sl=sl: e.matmul(
                    bank(1)[0:1, 0:TT], onesB[:, 0:1], xsq[:, sl:sl + TT],
                    start=(c == 0), stop=(c == NCH - 1)),
                    waits=[("v", p.cnt["v"])] + (p.bank_wait(1) if c == 0 else []),
                    inc="t")
                slot_guard[c % 3] = [("t", p.cnt["t"])]
            TSUM = p.cnt["t"]
            p.op("scalar", lambda e: e.activation(out=r_q[:], in_=bank(0)[0:1, 0:TT],
                 func=AF.Copy, scale=1.0 / 32.0), waits=[("t", TSUM)], inc="s")
            p.op("scalar", lambda e: e.activation(out=r_mean[:], in_=bank(0)[0:1, 0:TT],
                 func=AF.Copy, scale=1.0 / 1024.0), inc="s")
            p.bank_mark(0, "s")
            p.op("vector", lambda e: e.tensor_tensor(out=r_q2[:], in0=r_q[:],
                 in1=r_q[:], op=OP.mult), waits=[("s", p.cnt["s"])], inc="v")
            p.op("vector", lambda e: e.tensor_tensor(out=r_v[:], in0=bank(1)[0:1, 0:TT],
                 in1=r_q2[:], op=OP.subtract), waits=[("v", p.cnt["v"]),
                 ("t", TSUM)], inc="v")
            p.bank_mark(1, "v")
            p.op("vector", lambda e: e.tensor_scalar(out=r_ve[:], in0=r_v[:],
                 scalar1=1.0 / 1024.0, scalar2=EPS, op0=OP.mult, op1=OP.add),
                 waits=[("v", p.cnt["v"])], inc="v")
            p.op("scalar", lambda e: e.activation(out=r_ln[:], in_=r_ve[:],
                 func=AF.Ln), waits=[("v", p.cnt["v"])], inc="s")
            p.op("scalar", lambda e: e.activation(out=r_rstd[:], in_=r_ln[:],
                 func=AF.Exp, scale=-0.5), waits=[("s", p.cnt["s"])], inc="s")
            SRS = p.cnt["s"]
            p.op("vector", lambda e: e.tensor_tensor(out=r_mr[:], in0=r_mean[:],
                 in1=r_rstd[:], op=OP.mult), waits=[("s", SRS)], inc="v")
            VMR = p.cnt["v"]
            p.op("tensor", lambda e: e.matmul(bank(2)[:], onesF[0:1, :],
                 r_mean[:], start=True, stop=True),
                 waits=[("s", SRS)] + p.bank_wait(2))
            p.op("tensor", lambda e: e.matmul(bank(3)[:], onesF[0:1, :],
                 r_rstd[:], start=True, stop=True),
                 waits=p.bank_wait(3), inc="t")
            TBC = p.cnt["t"]
            pend = []
            for c in range(NCH):
                xa = xsrc[:, TT * c:TT * c + TT]
                tmp = (tA, tB, tC)[c % 3]
                p.op("vector", lambda e, xa=xa, tmp=tmp: e.tensor_tensor(
                    out=tmp[:], in0=xa, in1=bank(2)[:], op=OP.subtract),
                    waits=[("t", TBC)], inc="v")
                pend.append((tmp, c))
                if len(pend) == 2:
                    tm, cc = pend.pop(0)
                    p.op("vector", lambda e, tm=tm, cc=cc: e.tensor_tensor(
                        out=out_h1[:, out_base + TT * cc:out_base + TT * cc + TT],
                        in0=tm[:], in1=bank(3)[:], op=OP.mult), inc="v")
            tm, cc = pend.pop(0)
            p.op("vector", lambda e, tm=tm, cc=cc: e.tensor_tensor(
                out=out_h1[:, out_base + TT * cc:out_base + TT * cc + TT],
                in0=tm[:], in1=bank(3)[:], op=OP.mult), inc="v")
            p.bank_mark(2, "v")
            p.bank_mark(3, "v")
            return p.cnt["v"], VMR

        for c in range(NCH):
            p.dma("sync", xio[:, TT * c:TT * c + TT],
                  xT[128 * c:128 * c + 128, 0:TT], "x_ld")
        XL0 = p.cnt["x_ld"]
        H1R_0, _ = ln_tile(h1T, 0, [("x_ld", XL0), ("s", ONES_S)])

        cast_hist = []
        stage_jobs = []
        for c in range(NCH):
            for h in range(3):
                stage_jobs.append((
                    wqkvB[:, 3072 * c + 1024 * h:3072 * c + 1024 * h + 1024],
                    wqkvT[128 * c:128 * c + 128, 1024 * h:1024 * h + 1024]))
        for c in range(NCH):
            stage_jobs.append((woutB[:, 1024 * c:1024 * c + 1024],
                               woutT[128 * c:128 * c + 128, :]))
        for j in range(4):
            stage_jobs.append((maskB[:, 1024 * j:1024 * j + 1024], masks[j]))
        for k, (dst, hsrc) in enumerate(stage_jobs):
            sl = 1024 * (k % 4)
            wt = [("s", cast_hist[k - 4])] if k >= 4 else [("s", ONES_S)]
            q, sem = ("sync", "c_ld") if k % 2 == 0 else ("gpsimd", "w_ld")
            ld = p.dma(q, xn[:, sl:sl + 1024], hsrc, sem, waits=wt)
            p.op("scalar", lambda e, dst=dst, sl=sl:
                 e.copy(out=dst, in_=xn[:, sl:sl + 1024]),
                 waits=[(sem, ld)], inc="s")
            cast_hist.append(p.cnt["s"])
        WQKV_DONE = cast_hist[3 * NCH - 1]
        CAST_DONE = p.cnt["s"]

        kv_loads = 0
        stream_consumed = {}
        ot_guard = {}
        rotg = 0

        xbuf = [xio, xn]
        SPILL_MARK = 0
        H1R_next = H1R_0
        NTT = DEV_NT or NT
        for T in range(NTT):
            cur = xbuf[T % 2]
            H1R = H1R_next

            for hp in range(NHP):
                for base, dst in ((0, Qp), (1024, Kc)):
                    bi = rotg % 8; rotg += 1
                    bk = bank(bi)
                    for c in range(NCH):
                        p.op("tensor", lambda e, c=c, hp=hp, base=base, bk=bk: e.matmul(
                            bk, wqkvB[:, 3072 * c + base + 128 * hp:
                                      3072 * c + base + 128 * hp + 128],
                            h1T[:, TT * c:TT * c + TT],
                            start=(c == 0), stop=(c == NCH - 1)),
                            waits=([("v", H1R)] + p.bank_wait(bi) +
                                   ([("s", WQKV_DONE)] if (T == 0 and hp == 0)
                                    else [])) if c == 0 else (),
                            inc=("t" if c == NCH - 1 else None))
                    p.op("scalar", lambda e, dst=dst, hp=hp, bk=bk: e.copy(
                        out=dst[:, TT * hp:TT * hp + TT], in_=bk),
                        waits=[("t", p.cnt["t"]), ("kv_st", p.cnt["kv_st"])], inc="s")
                    p.bank_mark(bi, "s")
                    rmx = rmxQ if base == 0 else rmxK
                    ssl = TT * (0 if base == 0 else 1)
                    p.op("vector", lambda e, dst=dst, hp=hp, ssl=ssl: e.tensor_tensor(
                        out=xsq[:, ssl:ssl + TT], in0=dst[:, TT * hp:TT * hp + TT],
                        in1=dst[:, TT * hp:TT * hp + TT], op=OP.mult),
                        waits=[("s", p.cnt["s"])], inc="v")
                    bi2 = rotg % 8; rotg += 1
                    p.op("tensor", lambda e, ssl=ssl, bk2=bank(bi2): e.matmul(
                        bk2[0:2, 0:TT], honesB[:], xsq[:, ssl:ssl + TT],
                        start=True, stop=True),
                        waits=[("v", p.cnt["v"])] + p.bank_wait(bi2), inc="t")
                    rsl = 10 + (rotg % 2)
                    p.op("vector", lambda e, bk2=bank(bi2), rsl=rsl: e.tensor_reduce(
                        out=bbt[0:2, rsl:rsl + 1], in_=bk2[0:2, 0:TT], axis=AX,
                        op=OP.max), waits=[("t", p.cnt["t"])], inc="v")
                    p.bank_mark(bi2, "v")
                    p.op("vector", lambda e, rmx=rmx, hp=hp, rsl=rsl: e.tensor_tensor(
                        out=rmx[0:2, hp:hp + 1], in0=rmx[0:2, hp:hp + 1],
                        in1=bbt[0:2, rsl:rsl + 1], op=OP.max),
                        waits=[("v", p.cnt["v"])], inc="v")
            QKE = p.cnt["s"]
            for s4 in range(4):
                for h4 in range(2):
                    bi = rotg % 8; rotg += 1
                    bk = bank(bi)
                    for c in range(NCH):
                        p.op("tensor", lambda e, c=c, s4=s4, h4=h4, bk=bk: e.matmul(
                            bk, h1T[:, TT * c + 128 * s4:TT * c + 128 * s4 + 128],
                            wqkvB[:, 3072 * c + 2048 + 512 * h4:
                                  3072 * c + 2048 + 512 * h4 + 512],
                            start=(c == 0), stop=(c == NCH - 1)),
                            waits=p.bank_wait(bi) if c == 0 else (),
                            inc=("t" if c == NCH - 1 else None))
                    p.op("scalar", lambda e, s4=s4, h4=h4, bk=bk: e.copy(
                        out=Vc[:, 1040 * s4 + 520 * h4:1040 * s4 + 520 * h4 + 520]
                            .rearrange("p (h x) -> p h x", h=NH // 2)[:, :, 0:64],
                        in_=bk.rearrange("p (h x) -> p h x", h=NH // 2)),
                        waits=[("t", p.cnt["t"]), ("kv_st", p.cnt["kv_st"])], inc="s")
                    p.bank_mark(bi, "s")
            PROD_S = p.cnt["s"]
            for j in range(4):
                p.dma("sync", ksp[4 * T + j].rearrange("p (h x) -> p h x", h=NHP),
                      Kc[:].rearrange("p (h k x) -> p k h x", h=NHP, k=4)[:, j],
                      "kv_st", waits=[("s", QKE)])
                p.dma("sync", vsp[4 * T + j], Vc[:, 1040 * j:1040 * j + 1040],
                      "kv_st", waits=[("s", PROD_S)])

            nkc = 4 * T + 4
            W0 = [("s", p.cnt["s"]), ("v", p.cnt["v"])]
            def _c(sw, k):
                return bbt[0:1, 12 * sw + k:12 * sw + k + 1]
            for sw in range(4):
        
                p.op("vector", lambda e, sw=sw: e.tensor_tensor(
                    out=bbt[0:2, 12 * sw + 0:12 * sw + 1],
                    in0=rmxQ[0:2, 2 * sw:2 * sw + 1],
                    in1=rmxQ[0:2, 2 * sw + 1:2 * sw + 2], op=OP.max),
                    waits=W0 if sw == 0 else ())
                p.op("vector", lambda e, sw=sw: e.tensor_tensor(
                    out=bbt[0:2, 12 * sw + 1:12 * sw + 2],
                    in0=rmxK[0:2, 2 * sw:2 * sw + 1],
                    in1=rmxK[0:2, 2 * sw + 1:2 * sw + 2], op=OP.max), inc="v")
            for sw in range(4):
                p.op("vector", lambda e, sw=sw: e.stream_shuffle(
                    bbt[0:32, 12 * sw + 2:12 * sw + 4],
                    bbt[0:32, 12 * sw + 0:12 * sw + 2], mask=[1] * 32),
                    waits=[("v", p.cnt["v"])] if sw == 0 else (), inc="v")
            for sw in range(4):
                p.op("vector", lambda e, sw=sw: e.tensor_tensor(
                    out=bbt[0:1, 12 * sw + 4:12 * sw + 6],
                    in0=bbt[0:1, 12 * sw + 0:12 * sw + 2],
                    in1=bbt[0:1, 12 * sw + 2:12 * sw + 4], op=OP.max),
                    waits=[("v", p.cnt["v"])] if sw == 0 else (), inc="v")
            for sw in range(4):
                p.op("vector", lambda e, sw=sw: e.tensor_tensor(
                    out=_c(sw, 6), in0=_c(sw, 4), in1=_c(sw, 5), op=OP.mult),
                    waits=[("v", p.cnt["v"])] if sw == 0 else (), inc="v")
            for sw in range(4):
                p.op("scalar", lambda e, sw=sw: e.activation(
                    out=_c(sw, 7), in_=_c(sw, 6), func=AF.Ln),
                    waits=[("v", p.cnt["v"])] if sw == 0 else (), inc="s")
            for sw in range(4):
                p.op("scalar", lambda e, sw=sw: e.activation(
                    out=_c(sw, 8), in_=_c(sw, 7), func=AF.Exp, scale=0.5),
                    waits=[("s", p.cnt["s"])] if sw == 0 else (), inc="s")
            for sw in range(4):
                p.op("vector", lambda e, sw=sw: e.tensor_scalar(
                    out=_c(sw, 9), in0=_c(sw, 8), scalar1=-0.125,
                    scalar2=85.0, op0=OP.mult, op1=OP.add),
                    waits=[("s", p.cnt["s"])] if sw == 0 else (), inc="v")
            for sw in range(4):
                p.op("vector", lambda e, sw=sw: e.tensor_scalar_min(
                    out=_c(sw, 10), in0=_c(sw, 9), scalar1=0.0),
                    waits=[("v", p.cnt["v"])] if sw == 0 else (), inc="v")
            NBMIN = p.cnt["v"]
            for b4 in range(4):
                for sw in range(4):
                    p.op("vector", lambda e, b4=b4, sw=sw: e.stream_shuffle(
                        negBc[32 * b4:32 * b4 + 32, sw:sw + 1],
                        bbt[0:32, 12 * sw + 10:12 * sw + 11], mask=[0] * 32),
                        waits=[("v", NBMIN)] if (b4 == 0 and sw == 0) else (),
                        inc=("v" if (b4 == 3 and sw == 3) else None))
            NB_V = p.cnt["v"]
            for sw in range(4):
                pairs = (2 * sw, 2 * sw + 1)
                pend = []

                def emit_pv(info):
                    islot = info["i"] % 4
                    pi = info["pi"]
                    for hb in range(2):
                        h = 2 * pairs[pi] + hb
                        p.op("tensor", lambda e, pi=pi, islot=islot, h=h, hb=hb,
                             info=info: e.matmul(
                                 PSB[0:65, 1024 * pi + 512 * hb:
                                     1024 * pi + 512 * hb + 512],
                                 info["vsrc"][:, 65 * h:65 * h + 65],
                                 Pb[:, 1024 * islot + 512 * hb:
                                    1024 * islot + 512 * hb + 512],
                                 start=info["start"], stop=info["stop"]),
                             waits=info["wait"] if hb == 0 else (),
                             inc=("t" if hb == 1 else None))
                    if info["load_idx"] is not None:
                        stream_consumed[info["load_idx"]] = p.cnt["t"]

                for kc in range(nkc):
                    diag = kc >= 4 * T
                    if not diag:
                        slot = kv_loads % NKV
                        wt = [("kv_st", p.cnt["kv_st"])]
                        if kv_loads >= NKV:
                            wt.append(("t", stream_consumed[kv_loads - NKV]))
                        p.dma("gpsimd", kvK[:, 1024 * slot:1024 * slot + 1024],
                              ksp[kc], "kv_ld", waits=wt)
                        p.dma("gpsimd", kvV[:, 1040 * slot:1040 * slot + 1040],
                              vsp[kc], "kv_ld")
                        this_load = p.cnt["kv_ld"]
                        kQ = kvK[:, 1024 * slot:1024 * slot + 1024]
                        vQ = kvV[:, 1040 * slot:1040 * slot + 1040]
                        ks = [kQ[:, 128 * pr:128 * pr + 128] for pr in pairs]
                        vs = [vQ, vQ]
                        kv_loads += 1
                    else:
                        j = kc - 4 * T
                        ks = [Kc[:, TT * pr + 128 * j:TT * pr + 128 * j + 128]
                              for pr in pairs]
                        vs = [Vc[:, 1040 * j:1040 * j + 1040]] * 2
                        this_load = None
                    for pi in range(2):
                        i = 2 * kc + pi
                        if len(pend) == 2:
                            emit_pv(pend.pop(0))
                        wt = p.bank_wait(2 * pi) + p.bank_wait(2 * pi + 1)
                        if i == 0:
                            wt.append(("s", PROD_S))
                        if pi == 0 and this_load is not None:
                            wt.append(("kv_ld", this_load))
                        pr = pairs[pi]
                        ksrc = ks[pi]
                        p.op("tensor", lambda e, pi=pi, ksrc=ksrc, pr=pr: e.matmul(
                            PSA[:, 1024 * pi:1024 * pi + 512], ksrc[0:64, :],
                            Qp[0:64, TT * pr:TT * pr + TT],
                            start=True, stop=True, tile_position=(0, 0)), waits=wt)
                        p.op("tensor", lambda e, pi=pi, ksrc=ksrc, pr=pr: e.matmul(
                            PSA[:, 1024 * pi + 512:1024 * pi + 1024], ksrc[64:128, :],
                            Qp[64:128, TT * pr:TT * pr + TT],
                            start=True, stop=True, tile_position=(64, 0)), inc="t")
                        SMM = p.cnt["t"]
                        p.op("scalar", lambda e, pi=pi, islot=i % 4, sw=sw:
                             e.activation(
                                 out=Pb[:, 1024 * islot:1024 * islot + 1024],
                                 in_=PSA[:, 1024 * pi:1024 * pi + 1024], func=AF.Exp,
                                 scale=SCALE, bias=negBc[:, sw:sw + 1]),
                             waits=[("t", SMM)] + ([("v", NB_V)] if i == 0 else []),
                             inc="s")
                        p.bank_mark(2 * pi, "s")
                        p.bank_mark(2 * pi + 1, "s")
                        if diag:
                            j = kc - 4 * T
                            p.op("vector", lambda e, islot=i % 4, j=j: e.tensor_tensor(
                                out=Pb[:, 1024 * islot:1024 * islot + 1024],
                                in0=Pb[:, 1024 * islot:1024 * islot + 1024],
                                in1=maskB[:, 1024 * j:1024 * j + 1024], op=OP.mult),
                                waits=[("s", p.cnt["s"])], inc="v")
                            pv_wait = [("v", p.cnt["v"])]
                        else:
                            pv_wait = [("s", p.cnt["s"])]
                        if kc == 0:
                            pv_wait = pv_wait + p.bank_wait(4 + 2 * pi) + \
                                      p.bank_wait(5 + 2 * pi)
                        pend.append(dict(i=i, pi=pi, vsrc=vs[pi], wait=pv_wait,
                                         start=(kc == 0), stop=(kc == nkc - 1),
                                         load_idx=(kv_loads - 1)
                                         if (not diag and pi == 1) else None))

                for pi in range(2):
                    emit_pv(pend.pop(0))
                    PVD = p.cnt["t"]
                    pr = pairs[pi]
                    p.op("scalar", lambda e, pi=pi, pr=pr: e.copy(
                        out=OnT[0:64, TT * pr:TT * pr + TT],
                        in_=PSB[0:64, 1024 * pi:1024 * pi + 512]),
                        waits=[("t", PVD)], inc="s")
                    p.op("scalar", lambda e, pi=pi: e.copy(
                        out=Ot[:, TT * pi:TT * pi + TT],
                        in_=PSB[0:64, 1024 * pi + 512:1024 * pi + 1024]),
                        waits=ot_guard.get(pi, []), inc="s")
                    SCP = p.cnt["s"]
                    p.dma("sync", OnT[64:128, TT * pr:TT * pr + TT],
                          Ot[:, TT * pi:TT * pi + TT], "sh_st",
                          waits=[("s", SCP)])
                    SHD = p.cnt["sh_st"]
                    ot_guard[pi] = [("sh_st", SHD)]
                    for hb, tt_ in ((0, tA), (1, tB)):
                        p.op("vector", lambda e, pi=pi, hb=hb, tt_=tt_:
                             e.tensor_scalar(
                                 out=tt_[64:65, 0:TT],
                                 in0=PSB[64:65, 1024 * pi + 512 * hb:
                                         1024 * pi + 512 * hb + 512],
                                 scalar1=1e-37, scalar2=None, op0=OP.add),
                             waits=[("t", PVD)] if hb == 0 else (),
                             inc=("v" if hb == 1 else None))
                    p.banks[4 + 2 * pi] = [("s", SCP), ("v", p.cnt["v"])]
                    p.banks[5 + 2 * pi] = [("s", SCP), ("v", p.cnt["v"])]
                    for hb, tt_ in ((0, tA), (1, tB)):
                        p.op("vector", lambda e, tt_=tt_: e.reciprocal(
                            out=tt_[64:65, 0:TT], in_=tt_[64:65, 0:TT]),
                            waits=[("v", p.cnt["v"])] if hb == 0 else (),
                            inc=("v" if hb == 1 else None))
                    for b4 in range(4):
                        tt_ = tA if b4 < 2 else tB
                        cc = 0 if b4 < 2 else TT
                        p.op("vector", lambda e, b4=b4, tt_=tt_, cc=cc:
                             e.stream_shuffle(
                                 lb[32 * b4:32 * b4 + 32, cc:cc + TT],
                                 tt_[64:96, 0:TT], mask=[0] * 32),
                             waits=[("v", p.cnt["v"])] if b4 == 0 else (),
                             inc=("v" if b4 == 3 else None))
                    p.op("vector", lambda e, pr=pr: e.tensor_tensor(
                        out=OnT[0:64, TT * pr:TT * pr + TT],
                        in0=OnT[0:64, TT * pr:TT * pr + TT],
                        in1=lb[0:64, 0:TT], op=OP.mult),
                        waits=[("s", SCP)])
                    p.op("vector", lambda e, pr=pr: e.tensor_tensor(
                        out=OnT[64:128, TT * pr:TT * pr + TT],
                        in0=OnT[64:128, TT * pr:TT * pr + TT],
                        in1=lb[64:128, TT:2 * TT], op=OP.mult),
                        waits=[("sh_st", SHD)], inc="v")

            ALLON = p.cnt["v"]
            if T + 1 < NTT:
                nxt = xbuf[(T + 1) % 2]
                xw = [("xp_st", SPILL_MARK)] + \
                     ([("s", CAST_DONE)] if T == 0 else [])
                for c in range(NCH):
                    p.dma("sync", nxt[:, TT * c:TT * c + TT],
                          xT[128 * c:128 * c + 128, TT * (T + 1):TT * (T + 1) + TT],
                          "x_ld", waits=xw if c == 0 else ())
                H1R_next, _ = ln_tile(h1T, 0, [("x_ld", p.cnt["x_ld"])],
                                      xsrc=nxt)
            for fc in range(NCH):
                bi = rotg % 8; rotg += 1
                bk = bank(bi)
                for hp in range(NHP):
                    p.op("tensor", lambda e, fc=fc, hp=hp, bk=bk: e.matmul(
                        bk, woutB[:, 1024 * hp + 128 * fc:1024 * hp + 128 * fc + 128],
                        OnT[:, TT * hp:TT * hp + TT],
                        start=(hp == 0), stop=(hp == NHP - 1)),
                        waits=([("v", ALLON)] + p.bank_wait(bi) +
                               ([("s", CAST_DONE)] if (T == 0 and fc == 0)
                                else [])) if hp == 0 else (),
                        inc=("t" if hp == NHP - 1 else None))
                p.op("vector", lambda e, bk=bk, fc=fc, cur=cur:
                     e.scalar_tensor_tensor(
                         out=cur[:, TT * fc:TT * fc + TT], in0=bk,
                         scalar=boutF[:, fc:fc + 1],
                         in1=cur[:, TT * fc:TT * fc + TT], op0=OP.add, op1=OP.add),
                     waits=[("t", p.cnt["t"])], inc="v")
                p.bank_mark(bi, "v")
                p.dma("sync", xp[128 * fc:128 * fc + 128, TT * T:TT * T + TT],
                      cur[:, TT * fc:TT * fc + TT], "xp_st",
                      waits=[("v", p.cnt["v"])])
            SPILL_MARK = p.cnt["xp_st"]
            if debug and T == DEV_DBG_T:
                p.dma("gpsimd", dbg["d_h1T"], h1T[:], "y_st", waits=[("v", H1R)])
                p.dma("gpsimd", dbg["d_ont"], OnT[:], "y_st",
                      waits=[("v", ALLON)])
                p.dma("gpsimd", dbg["d_nb"], negBc[:], "y_st", waits=[("v", NB_V)])
                p.dma("gpsimd", dbg["d_bbt"], bbt[:], "y_st")
                p.dma("gpsimd", dbg["d_rmx"], rmxQ[:], "y_st")

        FIN = {k: p.cnt[k] for k in p.cnt}
        p.flush(blkA)
        blkA_es.close()
        esA.close()

        esB = es.enter_context(contextlib.ExitStack())
        def sbB(name, shape, dt):
            return esB.enter_context(nc.sbuf_tensor(name, shape, dt))

        tD = sbB("tD", [128, TT], F32)
        tE = sbB("tE", [128, TT], F32)
        tF = sbB("tF", [128, TT], F32)
        h2T = sbB("h2T", [128, 2 * NCH * TT], BF16)
        wgB = sbB("wgB", [128, 2 * NCH * DI], BF16)
        wuB = sbB("wuB", [128, 2 * NCH * DI], BF16)
        wdB = sbB("wdB", [128, 2 * 4 * D], BF16)
        actB = sbB("actB", [128, 4 * TT], BF16)
        routed = sbB("routed", [128, 2 * NCH * TT], F32)
        cwb = sbB("cwb", [128, TT], F32)

        blkB_es = contextlib.ExitStack()
        blkB = blkB_es.enter_context(nc.Block())
        bw = ([(k, FIN[k]) for k in ("t", "s", "v")] +
              [(k, FIN[k]) for k in ("c_ld", "x_ld", "kv_ld", "kv_st",
                                     "xp_st", "sh_st", "y_st")])
        p.extra_first = {e: list(bw) for e in ENGINES}

        wslot_cnt = {}
        stg_guard = {}
        wchain = []
        wcast_done = {}
        for ST in range(0 if DEV_SKIP_B else NST):
            h2rv = {}
            for tt in range(2):
                T = 2 * ST + tt
                xw = [("y_st", p.cnt["y_st"]), ("t", p.cnt["t"]),
                      ("v", p.cnt["v"]), ("s", p.cnt["s"])]
                for c in range(NCH):
                    p.dma("sync", xio[:, TT * c:TT * c + TT],
                          xp[128 * c:128 * c + 128, TT * T:TT * T + TT], "x_ld",
                          waits=xw if c == 0 else ())
                XL = p.cnt["x_ld"]
                H2R, VMR = ln_tile(h2T, NCH * TT * tt, [("x_ld", XL)])
                h2rv[tt] = H2R
                for c in range(NCH):
                    p.op("tensor", lambda e, c=c: e.matmul(
                        bank(4)[0:8, :], gwF[:, 8 * c:8 * c + 8],
                        xio[:, TT * c:TT * c + TT],
                        start=(c == 0), stop=(c == NCH - 1)),
                        waits=([("v", H2R)] + p.bank_wait(4)) if c == 0 else (),
                        inc=("t" if c == NCH - 1 else None))
                p.op("scalar", lambda e: e.copy(out=glbuf[:], in_=bank(4)[0:8, :]),
                     waits=[("t", p.cnt["t"])], inc="s")
                p.bank_mark(4, "s")
                GLE = p.cnt["s"]
                for sub in range(4):
                    p.op("tensor", lambda e, sub=sub: e.matmul(
                        bank(5)[:, 0:8], glbuf[:, 128 * sub:128 * sub + 128], i8F[:],
                        start=True, stop=True),
                        waits=[("s", GLE), ("v", VMR)] + p.bank_wait(5))
                    p.op("tensor", lambda e, sub=sub: e.matmul(
                        bank(5)[:, 8:9], r_rstd[0:1, 128 * sub:128 * sub + 128],
                        onesF[0:1, 0:1], start=True, stop=True))
                    p.op("tensor", lambda e, sub=sub: e.matmul(
                        bank(5)[:, 9:10], r_mr[0:1, 128 * sub:128 * sub + 128],
                        onesF[0:1, 0:1], start=True, stop=True), inc="t")
                    p.op("scalar", lambda e, sub=sub: e.copy(
                        out=gtm[:, 10 * sub:10 * sub + 10], in_=bank(5)[:, 0:10]),
                        waits=[("t", p.cnt["t"])], inc="s")
                    p.bank_mark(5, "s")
                GTE = p.cnt["s"]

                o8 = lambda t_, sub: t_[:, 8 * sub:8 * sub + 8]
                o4_ = lambda t_, sub: t_[:, 4 * sub:4 * sub + 4]
                def rop(mk, inc=None, w=None):
                    for sub in range(4):
                        p.op("vector", mk(sub), waits=(w or []) if sub == 0 else [],
                             inc=(inc if sub == 3 else None))
                rop(lambda sub: (lambda e: e.tensor_scalar(
                    out=o8(rA_, sub), in0=gtm[:, 10 * sub:10 * sub + 8],
                    scalar1=gtm[:, 10 * sub + 8:10 * sub + 9], scalar2=None,
                    op0=OP.mult)), w=[("s", GTE)])
                rop(lambda sub: (lambda e: e.tensor_scalar(
                    out=o8(rB_, sub), in0=gcol[:],
                    scalar1=gtm[:, 10 * sub + 9:10 * sub + 10], scalar2=None,
                    op0=OP.mult)))
                rop(lambda sub: (lambda e: e.tensor_tensor(
                    out=o8(rL_, sub), in0=o8(rA_, sub), in1=o8(rB_, sub),
                    op=OP.subtract)), inc="v")
                p.op("scalar", lambda e: e.activation(out=rS_[:], in_=rL_[:],
                     func=AF.Sigmoid), waits=[("v", p.cnt["v"])], inc="s")
                SC = p.cnt["s"]
                def ropS(mk, inc=None):
                    for sub in range(4):
                        p.op("vector", mk(sub), waits=[("s", SC)] if sub == 0 else [],
                             inc=(inc if sub == 3 else None))
                ropS(lambda sub: (lambda e: e.tensor_tensor(
                    out=o8(rC_, sub), in0=o8(rS_, sub), in1=cbb[:], op=OP.add)))
                ropS(lambda sub: (lambda e: e.tensor_tensor(
                    out=o4_(rG4, sub), in0=rC_[:, 8 * sub:8 * sub + 8:2],
                    in1=rC_[:, 8 * sub + 1:8 * sub + 8:2], op=OP.add)))
                ropS(lambda sub: (lambda e: e.tensor_reduce(
                    out=rM_[:, 8 * sub:8 * sub + 1], in_=o4_(rG4, sub),
                    axis=AX, op=OP.max)))
                ropS(lambda sub: (lambda e: e.tensor_scalar(
                    out=o4_(rI_, sub), in0=o4_(rG4, sub),
                    scalar1=rM_[:, 8 * sub:8 * sub + 1], scalar2=None,
                    op0=OP.is_ge)))
                ropS(lambda sub: (lambda e: e.scalar_tensor_tensor(
                    out=o4_(rW_, sub), in0=o4_(rI_, sub), scalar=-BIG,
                    in1=o4_(rG4, sub), op0=OP.mult, op1=OP.add)))
                ropS(lambda sub: (lambda e: e.tensor_reduce(
                    out=rM_[:, 8 * sub + 1:8 * sub + 2], in_=o4_(rW_, sub),
                    axis=AX, op=OP.max)))
                ropS(lambda sub: (lambda e: e.tensor_scalar(
                    out=o4_(rI_, sub), in0=o4_(rG4, sub),
                    scalar1=rM_[:, 8 * sub + 1:8 * sub + 2], scalar2=None,
                    op0=OP.is_ge)))
                ropS(lambda sub: (lambda e: e.tensor_copy(
                    out=rSM[:, 8 * sub:8 * sub + 8:2], in_=o4_(rI_, sub))))
                ropS(lambda sub: (lambda e: e.tensor_copy(
                    out=rSM[:, 8 * sub + 1:8 * sub + 8:2], in_=o4_(rI_, sub))))
                ropS(lambda sub: (lambda e: e.tensor_tensor(
                    out=o8(rFC, sub), in0=o8(rSM, sub), in1=o8(rC_, sub),
                    op=OP.mult)))
                ropS(lambda sub: (lambda e: e.tensor_scalar(
                    out=o8(rSM, sub), in0=o8(rSM, sub), scalar1=1.0,
                    scalar2=None, op0=OP.subtract)))
                ropS(lambda sub: (lambda e: e.tensor_tensor(
                    out=o8(rB_, sub), in0=o8(rSM, sub), in1=idxE[:], op=OP.mult)))
                ropS(lambda sub: (lambda e: e.tensor_tensor(
                    out=o8(rFC, sub), in0=o8(rFC, sub), in1=o8(rB_, sub),
                    op=OP.add)))
                ropS(lambda sub: (lambda e: e.tensor_copy(
                    out=o8(rW_, sub), in_=o8(rFC, sub))))
                for rnd in range(3):
                    ropS(lambda sub: (lambda e: e.tensor_reduce(
                        out=rM_[:, 8 * sub + 2:8 * sub + 3], in_=o8(rW_, sub),
                        axis=AX, op=OP.max)))
                    ropS(lambda sub: (lambda e: e.tensor_scalar(
                        out=o8(rI_, sub), in0=o8(rW_, sub),
                        scalar1=rM_[:, 8 * sub + 2:8 * sub + 3], scalar2=None,
                        op0=OP.is_ge)))
                    ropS(lambda sub: (lambda e: e.scalar_tensor_tensor(
                        out=o8(rW_, sub), in0=o8(rI_, sub), scalar=-BIG,
                        in1=o8(rW_, sub), op0=OP.mult, op1=OP.add)))
                ropS(lambda sub: (lambda e: e.tensor_reduce(
                    out=rM_[:, 8 * sub + 3:8 * sub + 4], in_=o8(rW_, sub),
                    axis=AX, op=OP.max)))
                ropS(lambda sub: (lambda e: e.tensor_scalar(
                    out=o8(rI_, sub), in0=o8(rFC, sub),
                    scalar1=rM_[:, 8 * sub + 3:8 * sub + 4], scalar2=None,
                    op0=OP.is_ge)))
                ropS(lambda sub: (lambda e: e.tensor_tensor(
                    out=o8(rW_, sub), in0=o8(rI_, sub), in1=o8(rS_, sub),
                    op=OP.mult)))
                ropS(lambda sub: (lambda e: e.tensor_reduce(
                    out=rM_[:, 8 * sub + 4:8 * sub + 5], in_=o8(rW_, sub),
                    axis=AX, op=OP.add)))
                ropS(lambda sub: (lambda e: e.tensor_scalar(
                    out=rM_[:, 8 * sub + 5:8 * sub + 6],
                    in0=rM_[:, 8 * sub + 4:8 * sub + 5], scalar1=1e-20,
                    scalar2=None, op0=OP.add)))
                ropS(lambda sub: (lambda e: e.reciprocal(
                    out=rM_[:, 8 * sub + 6:8 * sub + 7],
                    in_=rM_[:, 8 * sub + 5:8 * sub + 6])))
                ropS(lambda sub: (lambda e: e.tensor_scalar(
                    out=rM_[:, 8 * sub + 7:8 * sub + 8],
                    in0=rM_[:, 8 * sub + 6:8 * sub + 7], scalar1=2.5,
                    scalar2=None, op0=OP.mult)))
                ropS(lambda sub: (lambda e: e.tensor_scalar(
                    out=o8(rCW, sub), in0=o8(rW_, sub),
                    scalar1=rM_[:, 8 * sub + 7:8 * sub + 8], scalar2=None,
                    op0=OP.mult)), inc="v")
                CWV = p.cnt["v"]
                for sub in range(4):
                    p.op("tensor", lambda e, sub=sub: e.matmul(
                        bank(5)[0:8, 0:128], rCW[:, 8 * sub:8 * sub + 8], i128F[:],
                        start=True, stop=True),
                        waits=[("v", CWV)] + p.bank_wait(5), inc="t")
                    p.op("scalar", lambda e, sub=sub, tt=tt: e.copy(
                        out=cwT[0:8, TT * tt + 128 * sub:TT * tt + 128 * sub + 128],
                        in_=bank(5)[0:8, 0:128]), waits=[("t", p.cnt["t"])], inc="s")
                    p.bank_mark(5, "s")
            CWS = p.cnt["s"]
            if debug and ST == 0:
                p.dma("gpsimd", dbg["d_h2T"], h2T[:, 0:NCH * TT], "y_st",
                      waits=[("v", p.cnt["v"])])
                p.dma("gpsimd", dbg["d_cwT"], cwT[:], "y_st", waits=[("s", CWS)])

            def queue_chain(gi):
                e_idx = ([NE] + list(range(NE)))[gi % 9]
                sl = gi % 2
                for step, (which, wt_dram, blk) in enumerate(
                        [(w_, d_, b_) for (w_, d_) in (("g", wg), ("u", wu),
                                                       ("d", wd))
                         for b_ in range(4)]):
                    dstb = {"g": wgB, "u": wuB, "d": wdB}[which]
                    if which != "d":
                        din_ap = wt_dram[e_idx, 256 * blk:256 * blk + 256, :] \
                            .rearrange("(k p) x -> p k x", k=2)
                        off = NCH * DI * sl + 1024 * blk
                    else:
                        din_ap = wt_dram[e_idx, 128 * blk:128 * blk + 128, :]
                        off = 4 * D * sl + D * blk
                    wchain.append((gi, sl, step, din_ap, dstb, off))

            def drain_step():
                if not wchain:
                    return
                gi, sl, step, din_ap, dstb, off = wchain.pop(0)
                s0 = 1024 * (step % 2)
                wt = list(stg_guard.get(step % 2, []))
                if step == 0 and wslot_cnt.get(sl) is not None:
                    wt.append(("t", wslot_cnt[sl]))
                dout_ap = xn[:, s0:s0 + 1024]
                if din_ap.shape != (128, 1024):
                    dout_ap = dout_ap.rearrange("p (k x) -> p k x", k=2)
                p.dma("gpsimd", dout_ap, din_ap, "w_ld", waits=wt)
                p.op("scalar", lambda e, dstb=dstb, off=off, s0=s0:
                     e.copy(out=dstb[:, off:off + 1024],
                            in_=xn[:, s0:s0 + 1024]),
                     waits=[("w_ld", p.cnt["w_ld"])], inc="s")
                stg_guard[step % 2] = [("s", p.cnt["s"])]
                wcast_done[gi] = p.cnt["s"]

            def drain_until(gi):
                while wchain and wchain[0][0] <= gi:
                    drain_step()

            for ei, e_idx in enumerate([NE] + list(range(NE))):
                gi = 9 * ST + ei
                sl = gi % 2
                if gi == 0:
                    queue_chain(0)
                drain_until(gi)
                WCAST_S = wcast_done[gi]
                if gi + 1 < 9 * NST:
                    queue_chain(gi + 1)

                for tt in range(2):
                    if e_idx != NE:
                        for b4 in range(4):
                            p.op("vector", lambda e, b4=b4, tt=tt, e_idx=e_idx:
                                 e.stream_shuffle(cwb[32 * b4:32 * b4 + 32, :],
                                                  cwT[0:32, TT * tt:TT * tt + TT],
                                                  mask=[e_idx] * 32),
                                 waits=[("s", CWS), ("t", p.cnt["t"])]
                                 if b4 == 0 else (),
                                 inc=("v" if b4 == 3 else None))
                    CWBV = p.cnt["v"]
                    mults = []
                    act_mark = {}
                    for ic in range(4):
                        gb = 2 * (ic % 2)
                        ub = gb + 1
                        for wsb, bki in ((wgB, gb), (wuB, ub)):
                            bk = bank(bki)
                            for hc in range(NCH):
                                p.op("tensor", lambda e, wsb=wsb, bk=bk, ic=ic,
                                     hc=hc, sl=sl, tt=tt: e.matmul(
                                         bk,
                                         wsb[:, NCH * DI * sl + DI * hc + 128 * ic:
                                             NCH * DI * sl + DI * hc + 128 * ic + 128],
                                         h2T[:, NCH * TT * tt + TT * hc:
                                             NCH * TT * tt + TT * hc + TT],
                                         start=(hc == 0), stop=(hc == NCH - 1)),
                                     waits=([("s", WCAST_S), ("v", h2rv[tt])] +
                                            p.bank_wait(bki))
                                     if hc == 0 else (),
                                     inc=("t" if hc == NCH - 1 else None))
                        stmp = (tA, tB, tC)[ic % 3]
                        p.op("scalar", lambda e, stmp=stmp, gb=gb: e.activation(
                            out=stmp[:], in_=bank(gb), func=AF.Silu),
                            waits=[("t", p.cnt["t"]), ("v", p.cnt["v"])], inc="s")
                        p.bank_mark(gb, "s")
                        drain_step()
                        mtmp = (tD, tE, tF)[ic % 3]
                        p.op("vector", lambda e, stmp=stmp, mtmp=mtmp, ub=ub:
                             e.tensor_tensor(out=mtmp[:], in0=stmp[:],
                                             in1=bank(ub), op=OP.mult),
                             waits=[("s", p.cnt["s"])], inc="v")
                        p.bank_mark(ub, "v")
                        drain_step()
                        mults.append((mtmp, ic))
                        if len(mults) == 2:
                            mt, mic = mults.pop(0)
                            if e_idx == NE:
                                p.op("vector", lambda e, mt=mt, mic=mic:
                                     e.tensor_copy(out=actB[:, TT * mic:TT * mic + TT],
                                                   in_=mt[:]), inc="v")
                            else:
                                p.op("vector", lambda e, mt=mt, mic=mic:
                                     e.tensor_tensor(out=actB[:, TT * mic:TT * mic + TT],
                                                     in0=mt[:], in1=cwb[:],
                                                     op=OP.mult),
                                     waits=[("v", CWBV)], inc="v")
                            act_mark[mic] = p.cnt["v"]
                    mt, mic = mults.pop(0)
                    if e_idx == NE:
                        p.op("vector", lambda e, mt=mt, mic=mic: e.tensor_copy(
                            out=actB[:, TT * mic:TT * mic + TT], in_=mt[:]), inc="v")
                    else:
                        p.op("vector", lambda e, mt=mt, mic=mic: e.tensor_tensor(
                            out=actB[:, TT * mic:TT * mic + TT], in0=mt[:],
                            in1=cwb[:], op=OP.mult), waits=[("v", CWBV)], inc="v")
                    act_mark[mic] = p.cnt["v"]
                    ACTV = p.cnt["v"]
                    for hc in range(NCH):
                        bi = 4 + hc % 4
                        bk = bank(bi)
                        for ic in range(4):
                            p.op("tensor", lambda e, hc=hc, ic=ic, sl=sl, bk=bk:
                                 e.matmul(bk,
                                          wdB[:, 4 * D * sl + D * ic + 128 * hc:
                                              4 * D * sl + D * ic + 128 * hc + 128],
                                          actB[:, TT * ic:TT * ic + TT],
                                          start=(ic == 0), stop=(ic == 3)),
                                 waits=([("v", act_mark[ic])] +
                                        (p.bank_wait(bi) if ic == 0 else [])),
                                 inc=("t" if ic == 3 else None))
                        wslot_cnt[sl] = p.cnt["t"]
                        dst = routed[:, NCH * TT * tt + TT * hc:
                                     NCH * TT * tt + TT * hc + TT]
                        if ei == 0:
                            p.op("vector", lambda e, dst=dst, bk=bk:
                                 e.tensor_copy(out=dst, in_=bk),
                                 waits=[("t", p.cnt["t"])], inc="v")
                        else:
                            p.op("vector", lambda e, dst=dst, bk=bk:
                                 e.tensor_tensor(out=dst, in0=dst, in1=bk,
                                                 op=OP.add),
                                 waits=[("t", p.cnt["t"])], inc="v")
                        p.bank_mark(bi, "v")
            ROUT = p.cnt["v"]
            for tt in range(2):
                T = 2 * ST + tt
                xw = [("v", ROUT), ("y_st", p.cnt["y_st"])]
                for c in range(NCH):
                    p.dma("sync", xio[:, TT * c:TT * c + TT],
                          xp[128 * c:128 * c + 128, TT * T:TT * T + TT], "x_ld",
                          waits=xw if c == 0 else ())
                XL = p.cnt["x_ld"]
                for c in range(NCH):
                    p.op("vector", lambda e, c=c, tt=tt: e.tensor_tensor(
                        out=xio[:, TT * c:TT * c + TT],
                        in0=xio[:, TT * c:TT * c + TT],
                        in1=routed[:, NCH * TT * tt + TT * c:
                                   NCH * TT * tt + TT * c + TT],
                        op=OP.add), waits=[("x_ld", XL)], inc="v")
                    p.dma("sync", yT[128 * c:128 * c + 128, TT * T:TT * T + TT],
                          xio[:, TT * c:TT * c + TT], "y_st",
                          waits=[("v", p.cnt["v"])])
        if p.cnt["y_st"]:
            p.op("sync", lambda e: e.nop(), waits=[("y_st", p.cnt["y_st"])])
        p.flush(blkB)
        blkB_es.close()
        esB.close()
    return nc


_QKV_PERM = None
def _qkv_perm():
    global _QKV_PERM
    if _QKV_PERM is None:
        perm = []
        for hp in range(NHP):
            for h in (2 * hp, 2 * hp + 1):
                perm.extend(range(h * 192, h * 192 + 64))
        for hp in range(NHP):
            for h in (2 * hp, 2 * hp + 1):
                perm.extend(range(h * 192 + 64, h * 192 + 128))
        for h in range(NH):
            perm.extend(range(h * 192 + 128, h * 192 + 192))
        _QKV_PERM = np.array(perm)
    return _QKV_PERM


_NC_CACHE = {}
def get_nc(debug=False):
    if debug not in _NC_CACHE:
        _NC_CACHE[debug] = build_nc(debug)
    return _NC_CACHE[debug]


def make_inputs(L0_attn_x, L0_attn_w_qkv, L0_attn_w_out, L0_attn_b_out,
                L0_ffn_gate_w, L0_ffn_correction_bias, L0_ffn_Wg, L0_ffn_Wu,
                L0_ffn_Wd, L0_ffn_Wg_s, L0_ffn_Wu_s, L0_ffn_Wd_s):
    f = np.float32
    wqkvT = np.ascontiguousarray(np.asarray(L0_attn_w_qkv, f).T[:, _qkv_perm()])
    woutT = np.ascontiguousarray(np.asarray(L0_attn_w_out, f).T)
    gwT = np.ascontiguousarray(np.asarray(L0_ffn_gate_w, f).T)
    bout = np.ascontiguousarray(np.asarray(L0_attn_b_out, f).reshape(NCH, 128).T)
    cbias = np.ascontiguousarray(np.asarray(L0_ffn_correction_bias, f).reshape(1, NE))
    idxeps = np.ascontiguousarray(
        np.tile((np.arange(1, NE + 1, dtype=f) * f(1e-30)).reshape(1, NE), (128, 1)))
    kk = np.arange(128).reshape(1, 128, 1)
    qq = np.arange(TT).reshape(1, 1, TT)
    jj = np.arange(4).reshape(4, 1, 1)
    mk = (128 * jj + kk <= qq).astype(f)
    mk = np.ascontiguousarray(np.tile(mk, (1, 1, 2)))
    onesf = np.ones((128, 128), f)
    hones = np.zeros((128, 2), f); hones[0:64, 0] = 1.0; hones[64:128, 1] = 1.0
    i8 = np.eye(8, dtype=f)
    i128 = np.eye(128, dtype=f)
    wg_ = np.ascontiguousarray(np.concatenate(
        [np.asarray(L0_ffn_Wg, f), np.asarray(L0_ffn_Wg_s, f)[None]], axis=0))
    wu_ = np.ascontiguousarray(np.concatenate(
        [np.asarray(L0_ffn_Wu, f), np.asarray(L0_ffn_Wu_s, f)[None]], axis=0))
    wd_ = np.ascontiguousarray(np.concatenate(
        [np.asarray(L0_ffn_Wd, f), np.asarray(L0_ffn_Wd_s, f)[None]], axis=0))
    shared = dict(wqkvT=wqkvT, woutT=woutT, gwT=gwT, bout=bout, cbias=cbias,
                  idxeps=idxeps, masks=mk, onesf=onesf, hones=hones, i8=i8,
                  i128=i128, wg=wg_, wu=wu_, wd=wd_)
    x = np.asarray(L0_attn_x, f)
    maps = []
    for core in range(NCORES):
        m = dict(shared)
        m["xT"] = np.ascontiguousarray(x[core].T)
        maps.append(m)
    return maps


def kernel(**inputs):
    maps = make_inputs(**inputs)
    nc = get_nc(debug=False)
    res = run_bass_kernel_spmd(nc, maps, core_ids=list(range(NCORES)))
    out = np.stack([np.ascontiguousarray(r["yT"].T) for r in res.results], axis=0)
    return out.astype(np.float32)


# version 22 (final)
# speedup vs baseline: 1.1001x; 1.0206x over previous
import contextlib
import numpy as np

import concourse.bass as bass
import concourse.mybir as mybir
from concourse.bass_utils import run_bass_kernel_spmd
from concourse.alu_op_type import AluOpType as OP

F32 = mybir.dt.float32
BF16 = mybir.dt.bfloat16
AF = mybir.ActivationFunctionType
AX = mybir.AxisListType.X

NCORES = 8
SEQ = 4096
D = 1024
NH = 16
NHP = 8
NT = 8
TT = 512
NCH = 8
NE = 8
DI = 512
NST = 4
SCALE = 0.125
EPS = 1e-5
BIG = 1.0e30
NKV = 3

ENGINES = ("sync", "gpsimd", "tensor", "vector", "scalar")


class Prog:

    def __init__(self, nc, es):
        self.nc = nc
        self.q = {e: [] for e in ENGINES}
        self.cnt = {}
        self.sems = {}
        self.es = es
        self.extra_first = {}
        self.banks = {}

    def sem(self, name):
        h = self.es.enter_context(self.nc.semaphore(name))
        self.sems[name] = h
        self.cnt[name] = 0
        return name

    def bank_wait(self, i):
        return list(self.banks.get(i, []))

    def bank_mark(self, i, semname):
        self.banks[i] = [(semname, self.cnt[semname])]

    def op(self, eng, fn, waits=(), inc=None, n=1):
        waits = list(waits)
        if eng in self.extra_first:
            waits += self.extra_first.pop(eng)
        for s, t in waits:
            assert t <= self.cnt[s], (eng, s, t, self.cnt[s])
        if inc is not None:
            self.cnt[inc] += n
        self.q[eng].append((fn, tuple(waits), inc, n))
        return self.cnt[inc] if inc is not None else None

    def dma(self, eng, out, in_, sem, waits=()):
        return self.op(eng, lambda e, o=out, i=in_: e.dma_start(out=o, in_=i),
                       waits, sem, 16)

    def flush(self, block):
        for name in ENGINES:
            ops = self.q[name]

            def body(h, ops=ops):
                for fn, waits, inc, n in ops:
                    for s, t in waits:
                        h.wait_ge(self.sems[s], t)
                    r = fn(h)
                    if inc is not None:
                        r.then_inc(self.sems[inc], n)

            getattr(block, name)(body)
        self.q = {e: [] for e in ENGINES}


DEV_NT = None
DEV_SKIP_B = False
DEV_DBG_T = 0


def build_nc(debug=False):
    nc = bass.Bass("TRN2", target_bir_lowering=False)

    def din(name, shape, dt=F32):
        return nc.dram_tensor(name, shape, dt, kind="ExternalInput").ap()

    xT = din("xT", [D, SEQ])
    wqkvT = din("wqkvT", [D, 3 * D])
    woutT = din("woutT", [D, D])
    gwT = din("gwT", [D, NE])
    bout = din("bout", [128, NCH])
    cbias = din("cbias", [1, NE])
    idxeps = din("idxeps", [128, NE])
    masks = din("masks", [4, 128, 2 * TT])
    onesf = din("onesf", [128, 128])
    hones = din("hones", [128, 2])
    i8 = din("i8", [8, 8])
    i128 = din("i128", [128, 128])
    wg = din("wg", [NE + 1, D, DI])
    wu = din("wu", [NE + 1, D, DI])
    wd = din("wd", [NE + 1, DI, D])

    yT = nc.dram_tensor("yT", [D, SEQ], F32, kind="ExternalOutput").ap()
    xp = nc.dram_tensor("xp", [D, SEQ], F32, kind="Internal").ap()
    ksp = nc.dram_tensor("ksp", [SEQ // 128, 128, 1024], BF16, kind="Internal").ap()
    vsp = nc.dram_tensor("vsp", [SEQ // 128, 128, 1040], BF16, kind="Internal").ap()

    dbg = {}
    if debug:
        for name, shape, dt in [
            ("d_h1T", [128, NCH * TT], BF16), ("d_ont", [128, NHP * TT], BF16),
            ("d_h2T", [128, NCH * TT], BF16), ("d_cwT", [32, 2 * TT], F32),
            ("d_nb", [128, 4], F32), ("d_bbt", [32, 48], F32),
            ("d_rmx", [32, NHP], F32),
        ]:
            dbg[name] = nc.dram_tensor(name, shape, dt, kind="ExternalOutput").ap()

    es = contextlib.ExitStack()
    with es:
        def sb(name, shape, dt):
            return es.enter_context(nc.sbuf_tensor(name, shape, dt))

        xio = sb("xio", [128, NCH * TT], F32)
        xn = sb("xn", [128, NCH * TT], F32)
        onesF = sb("onesF", [128, 128], F32)
        onesB = sb("onesB", [128, 128], BF16)
        i8F = sb("i8F", [8, 8], F32)
        i128F = sb("i128F", [128, 128], F32)
        boutF = sb("boutF", [128, NCH], F32)
        crow = sb("crow", [32, 16], F32)
        gcol = sb("gcol", [128, NE], F32)
        cbb = sb("cbb", [128, NE], F32)
        idxE = sb("idxE", [128, NE], F32)
        gwF = sb("gwF", [128, NCH * NE], F32)
        r_q = sb("r_q", [1, TT], F32)
        r_mean = sb("r_mean", [1, TT], F32)
        r_q2 = sb("r_q2", [1, TT], F32)
        r_v = sb("r_v", [1, TT], F32)
        r_ve = sb("r_ve", [1, TT], F32)
        r_ln = sb("r_ln", [1, TT], F32)
        r_rstd = sb("r_rstd", [1, TT], F32)
        r_mr = sb("r_mr", [1, TT], F32)
        tA = sb("tA", [128, TT], F32)
        tB = sb("tB", [128, TT], F32)
        tC = sb("tC", [128, TT], F32)
        xbb = sb("xbb", [128, 3 * TT], BF16)
        xsq = sb("xsq", [128, 3 * TT], BF16)
        maskB = sb("maskB", [128, 4 * 2 * TT], BF16)
        lb = sb("lb", [128, 2 * TT], F32)
        h1T = sb("h1T", [128, NCH * TT], BF16)
        honesB = sb("honesB", [128, 2], BF16)
        rmxQ = sb("rmxQ", [32, NHP], F32)
        rmxK = sb("rmxK", [32, NHP], F32)
        bbt = sb("bbt", [32, 48], F32)
        negBc = sb("negBc", [128, 4], F32)
        gtm = sb("gtm", [128, 4 * 10], F32)
        def rt(name):
            return sb(name, [128, 4 * NE], F32)
        rA_, rB_, rL_, rS_, rC_ = rt("rtA"), rt("rtB"), rt("rtL"), rt("rtS"), rt("rtC")
        rG4 = sb("rG4", [128, 4 * 4], F32)
        rM_ = sb("rM_", [128, 4 * 8], F32)
        rI_, rW_, rSM, rFC, rCW = rt("rtI"), rt("rtW"), rt("rtSM"), rt("rtFC"), rt("rtCW")
        glbuf = sb("glbuf", [8, TT], F32)
        cwT = sb("cwT", [32, 2 * TT], F32)

        PSA = es.enter_context(nc.psum_tensor("psA", [128, 2048], F32))
        PSB = es.enter_context(nc.psum_tensor("psB", [128, 2048], F32))
        PS = [PSA[:, 0:1024], PSA[:, 1024:2048], PSB[:, 0:1024], PSB[:, 1024:2048]]
        def bank(i):
            return (PSA if i < 4 else PSB)[:, 512 * (i % 4):512 * (i % 4) + 512]

        p = Prog(nc, es)
        for s in ("c_ld", "x_ld", "kv_ld", "w_ld", "kv_st", "xp_st", "sh_st",
                  "y_st", "t", "s", "v", "g"):
            p.sem(s)

        esA = es.enter_context(contextlib.ExitStack())
        def sbA(name, shape, dt):
            return esA.enter_context(nc.sbuf_tensor(name, shape, dt))

        wqkvB = sbA("wqkvB", [128, NCH * 3072], BF16)
        woutB = sbA("woutB", [128, NCH * 1024], BF16)
        Qp = sbA("Qp", [128, NHP * TT], BF16)
        Kc = sbA("Kc", [128, NHP * TT], BF16)
        Vc = sbA("Vc", [128, 4 * 1040], BF16)
        kvK = sbA("kvK", [128, NKV * 1024], BF16)
        kvV = sbA("kvV", [128, NKV * 1040], BF16)
        Pb = sbA("Pb", [128, 4 * 1024], BF16)
        OnT = sbA("OnT", [128, NHP * TT], BF16)
        Ot = sbA("Ot", [64, 2 * TT], BF16)

        blkA_es = contextlib.ExitStack()
        blkA = blkA_es.enter_context(nc.Block())

        for dst, src in [(onesF[:], onesf), (i8F[:], i8), (i128F[:], i128),
                         (boutF[:], bout), (idxE[:], idxeps),
                         (crow[0:1, 8:16], cbias)]:
            p.dma("sync", dst, src, "c_ld")
        for c in range(NCH):
            p.dma("sync", gwF[:, 8 * c:8 * c + 8], gwT[128 * c:128 * c + 128, :], "c_ld")
        p.dma("sync", xn[:, 0:2], hones, "c_ld")
        p.op("scalar", lambda e: e.copy(out=onesB[:], in_=onesF[:]),
             waits=[("c_ld", 16)], inc="s")
        p.op("scalar", lambda e: e.copy(out=honesB[:], in_=xn[:, 0:2]),
             waits=[("c_ld", p.cnt["c_ld"])], inc="s")
        ONES_S = p.cnt["s"]
        p.op("vector", lambda e: e.memset(rmxQ[:], 0.0), inc="v")
        p.op("vector", lambda e: e.memset(rmxK[:], 0.0), inc="v")
        for c in range(NCH):
            p.op("tensor", lambda e, c=c: e.matmul(
                bank(0)[0:1, 0:8], onesF[:, 0:1], gwF[:, 8 * c:8 * c + 8],
                start=(c == 0), stop=(c == NCH - 1)),
                waits=[("c_ld", 16 * (7 + NCH))], inc="t")
        p.op("scalar", lambda e: e.copy(out=crow[0:1, 0:8], in_=bank(0)[0:1, 0:8]),
             waits=[("t", p.cnt["t"])], inc="s")
        p.bank_mark(0, "s")
        SROW = p.cnt["s"]
        for b in range(4):
            p.op("vector", lambda e, b=b: e.stream_shuffle(
                gcol[32 * b:32 * b + 32, :], crow[:, 0:8], mask=[0] * 32),
                waits=[("s", SROW), ("c_ld", 16 * 6)])
            p.op("vector", lambda e, b=b: e.stream_shuffle(
                cbb[32 * b:32 * b + 32, :], crow[:, 8:16], mask=[0] * 32),
                waits=[("s", SROW)], inc="v")
        p.op("vector", lambda e: e.memset(
            Vc[:].rearrange("p (k h x) -> p k h x", k=4, h=NH)[:, :, :, 64:65], 1.0),
            inc="v")

        def ln_tile(out_h1, out_base, xwaits, xsrc=None):
            xwaits = list(xwaits)
            xsrc = xio if xsrc is None else xsrc
            t_entry = p.cnt["t"]
            slot_guard = {i: [("t", t_entry)] for i in range(3)}
            for c in range(NCH):
                sl = (c % 3) * TT
                xa = xsrc[:, TT * c:TT * c + TT]
                w0 = (xwaits if c == 0 else []) + slot_guard.get(c % 3, [])
                p.op("scalar", lambda e, xa=xa, sl=sl:
                     e.copy(out=xbb[:, sl:sl + TT], in_=xa), waits=w0, inc="s")
                p.op("vector", lambda e, xa=xa, sl=sl:
                     e.tensor_tensor(out=xsq[:, sl:sl + TT], in0=xa, in1=xa,
                                     op=OP.mult), waits=w0, inc="v")
                p.op("tensor", lambda e, c=c, sl=sl: e.matmul(
                    bank(0)[0:1, 0:TT], onesB[:, 0:1], xbb[:, sl:sl + TT],
                    start=(c == 0), stop=(c == NCH - 1)),
                    waits=[("s", p.cnt["s"])] + (p.bank_wait(0) if c == 0 else []))
                p.op("tensor", lambda e, c=c, sl=sl: e.matmul(
                    bank(1)[0:1, 0:TT], onesB[:, 0:1], xsq[:, sl:sl + TT],
                    start=(c == 0), stop=(c == NCH - 1)),
                    waits=[("v", p.cnt["v"])] + (p.bank_wait(1) if c == 0 else []),
                    inc="t")
                slot_guard[c % 3] = [("t", p.cnt["t"])]
            TSUM = p.cnt["t"]
            p.op("scalar", lambda e: e.activation(out=r_q[:], in_=bank(0)[0:1, 0:TT],
                 func=AF.Copy, scale=1.0 / 32.0), waits=[("t", TSUM)], inc="s")
            p.op("scalar", lambda e: e.activation(out=r_mean[:], in_=bank(0)[0:1, 0:TT],
                 func=AF.Copy, scale=1.0 / 1024.0), inc="s")
            p.bank_mark(0, "s")
            p.op("vector", lambda e: e.tensor_tensor(out=r_q2[:], in0=r_q[:],
                 in1=r_q[:], op=OP.mult), waits=[("s", p.cnt["s"])], inc="v")
            p.op("vector", lambda e: e.tensor_tensor(out=r_v[:], in0=bank(1)[0:1, 0:TT],
                 in1=r_q2[:], op=OP.subtract), waits=[("v", p.cnt["v"]),
                 ("t", TSUM)], inc="v")
            p.bank_mark(1, "v")
            p.op("vector", lambda e: e.tensor_scalar(out=r_ve[:], in0=r_v[:],
                 scalar1=1.0 / 1024.0, scalar2=EPS, op0=OP.mult, op1=OP.add),
                 waits=[("v", p.cnt["v"])], inc="v")
            p.op("scalar", lambda e: e.activation(out=r_ln[:], in_=r_ve[:],
                 func=AF.Ln), waits=[("v", p.cnt["v"])], inc="s")
            p.op("scalar", lambda e: e.activation(out=r_rstd[:], in_=r_ln[:],
                 func=AF.Exp, scale=-0.5), waits=[("s", p.cnt["s"])], inc="s")
            SRS = p.cnt["s"]
            p.op("vector", lambda e: e.tensor_tensor(out=r_mr[:], in0=r_mean[:],
                 in1=r_rstd[:], op=OP.mult), waits=[("s", SRS)], inc="v")
            VMR = p.cnt["v"]
            p.op("tensor", lambda e: e.matmul(bank(2)[:], onesF[0:1, :],
                 r_mean[:], start=True, stop=True),
                 waits=[("s", SRS)] + p.bank_wait(2))
            p.op("tensor", lambda e: e.matmul(bank(3)[:], onesF[0:1, :],
                 r_rstd[:], start=True, stop=True),
                 waits=p.bank_wait(3), inc="t")
            TBC = p.cnt["t"]
            pend = []
            for c in range(NCH):
                xa = xsrc[:, TT * c:TT * c + TT]
                tmp = (tA, tB, tC)[c % 3]
                p.op("vector", lambda e, xa=xa, tmp=tmp: e.tensor_tensor(
                    out=tmp[:], in0=xa, in1=bank(2)[:], op=OP.subtract),
                    waits=[("t", TBC)], inc="v")
                pend.append((tmp, c))
                if len(pend) == 2:
                    tm, cc = pend.pop(0)
                    p.op("vector", lambda e, tm=tm, cc=cc: e.tensor_tensor(
                        out=out_h1[:, out_base + TT * cc:out_base + TT * cc + TT],
                        in0=tm[:], in1=bank(3)[:], op=OP.mult), inc="v")
            tm, cc = pend.pop(0)
            p.op("vector", lambda e, tm=tm, cc=cc: e.tensor_tensor(
                out=out_h1[:, out_base + TT * cc:out_base + TT * cc + TT],
                in0=tm[:], in1=bank(3)[:], op=OP.mult), inc="v")
            p.bank_mark(2, "v")
            p.bank_mark(3, "v")
            return p.cnt["v"], VMR

        for c in range(NCH):
            p.dma("sync", xio[:, TT * c:TT * c + TT],
                  xT[128 * c:128 * c + 128, 0:TT], "x_ld")
        XL0 = p.cnt["x_ld"]
        H1R_0, _ = ln_tile(h1T, 0, [("x_ld", XL0), ("s", ONES_S)])

        cast_hist = []
        stage_jobs = []
        for j in range(4):
            stage_jobs.append((maskB[:, 1024 * j:1024 * j + 1024], masks[j]))
        for c in range(NCH):
            for h in range(3):
                stage_jobs.append((
                    wqkvB[:, 3072 * c + 1024 * h:3072 * c + 1024 * h + 1024],
                    wqkvT[128 * c:128 * c + 128, 1024 * h:1024 * h + 1024]))
        for c in range(NCH):
            stage_jobs.append((woutB[:, 1024 * c:1024 * c + 1024],
                               woutT[128 * c:128 * c + 128, :]))
        for k, (dst, hsrc) in enumerate(stage_jobs):
            sl = 1024 * (k % 4)
            wt = [("s", cast_hist[k - 4])] if k >= 4 else [("s", ONES_S)]
            q, sem = ("sync", "c_ld") if k % 2 == 0 else ("gpsimd", "w_ld")
            ld = p.dma(q, xn[:, sl:sl + 1024], hsrc, sem, waits=wt)
            p.op("scalar", lambda e, dst=dst, sl=sl:
                 e.copy(out=dst, in_=xn[:, sl:sl + 1024]),
                 waits=[(sem, ld)], inc="s")
            cast_hist.append(p.cnt["s"])
        CAST_DONE = p.cnt["s"]

        kv_loads = 0
        stream_consumed = {}
        ot_guard = {}
        rotg = 0

        xbuf = [xio, xn]
        SPILL_MARK = 0
        H1R_next = H1R_0
        NTT = DEV_NT or NT
        for T in range(NTT):
            cur = xbuf[T % 2]
            H1R = H1R_next

            for hp in range(NHP):
                for base, dst in ((0, Qp), (1024, Kc)):
                    bi = rotg % 8; rotg += 1
                    bk = bank(bi)
                    for c in range(NCH):
                        p.op("tensor", lambda e, c=c, hp=hp, base=base, bk=bk: e.matmul(
                            bk, wqkvB[:, 3072 * c + base + 128 * hp:
                                      3072 * c + base + 128 * hp + 128],
                            h1T[:, TT * c:TT * c + TT],
                            start=(c == 0), stop=(c == NCH - 1)),
                            waits=([("v", H1R)] + p.bank_wait(bi) +
                                   ([("s", CAST_DONE)] if (T == 0 and hp == 0)
                                    else [])) if c == 0 else (),
                            inc=("t" if c == NCH - 1 else None))
                    p.op("scalar", lambda e, dst=dst, hp=hp, bk=bk: e.copy(
                        out=dst[:, TT * hp:TT * hp + TT], in_=bk),
                        waits=[("t", p.cnt["t"]), ("kv_st", p.cnt["kv_st"])], inc="s")
                    p.bank_mark(bi, "s")
                    rmx = rmxQ if base == 0 else rmxK
                    ssl = TT * (0 if base == 0 else 1)
                    p.op("vector", lambda e, dst=dst, hp=hp, ssl=ssl: e.tensor_tensor(
                        out=xsq[:, ssl:ssl + TT], in0=dst[:, TT * hp:TT * hp + TT],
                        in1=dst[:, TT * hp:TT * hp + TT], op=OP.mult),
                        waits=[("s", p.cnt["s"])], inc="v")
                    bi2 = rotg % 8; rotg += 1
                    p.op("tensor", lambda e, ssl=ssl, bk2=bank(bi2): e.matmul(
                        bk2[0:2, 0:TT], honesB[:], xsq[:, ssl:ssl + TT],
                        start=True, stop=True),
                        waits=[("v", p.cnt["v"])] + p.bank_wait(bi2), inc="t")
                    rsl = 10 + (rotg % 2)
                    p.op("vector", lambda e, bk2=bank(bi2), rsl=rsl: e.tensor_reduce(
                        out=bbt[0:2, rsl:rsl + 1], in_=bk2[0:2, 0:TT], axis=AX,
                        op=OP.max), waits=[("t", p.cnt["t"])], inc="v")
                    p.bank_mark(bi2, "v")
                    p.op("vector", lambda e, rmx=rmx, hp=hp, rsl=rsl: e.tensor_tensor(
                        out=rmx[0:2, hp:hp + 1], in0=rmx[0:2, hp:hp + 1],
                        in1=bbt[0:2, rsl:rsl + 1], op=OP.max),
                        waits=[("v", p.cnt["v"])], inc="v")
            QKE = p.cnt["s"]
            for s4 in range(4):
                for h4 in range(2):
                    bi = rotg % 8; rotg += 1
                    bk = bank(bi)
                    for c in range(NCH):
                        p.op("tensor", lambda e, c=c, s4=s4, h4=h4, bk=bk: e.matmul(
                            bk, h1T[:, TT * c + 128 * s4:TT * c + 128 * s4 + 128],
                            wqkvB[:, 3072 * c + 2048 + 512 * h4:
                                  3072 * c + 2048 + 512 * h4 + 512],
                            start=(c == 0), stop=(c == NCH - 1)),
                            waits=p.bank_wait(bi) if c == 0 else (),
                            inc=("t" if c == NCH - 1 else None))
                    p.op("scalar", lambda e, s4=s4, h4=h4, bk=bk: e.copy(
                        out=Vc[:, 1040 * s4 + 520 * h4:1040 * s4 + 520 * h4 + 520]
                            .rearrange("p (h x) -> p h x", h=NH // 2)[:, :, 0:64],
                        in_=bk.rearrange("p (h x) -> p h x", h=NH // 2)),
                        waits=[("t", p.cnt["t"]), ("kv_st", p.cnt["kv_st"])], inc="s")
                    p.bank_mark(bi, "s")
            PROD_S = p.cnt["s"]
            for j in range(4):
                p.dma("sync", ksp[4 * T + j].rearrange("p (h x) -> p h x", h=NHP),
                      Kc[:].rearrange("p (h k x) -> p k h x", h=NHP, k=4)[:, j],
                      "kv_st", waits=[("s", QKE)])
                p.dma("sync", vsp[4 * T + j], Vc[:, 1040 * j:1040 * j + 1040],
                      "kv_st", waits=[("s", PROD_S)])

            nkc = 4 * T + 4
            W0 = [("s", p.cnt["s"]), ("v", p.cnt["v"])]
            def _c(sw, k):
                return bbt[0:1, 12 * sw + k:12 * sw + k + 1]
            for sw in range(4):
        
                p.op("vector", lambda e, sw=sw: e.tensor_tensor(
                    out=bbt[0:2, 12 * sw + 0:12 * sw + 1],
                    in0=rmxQ[0:2, 2 * sw:2 * sw + 1],
                    in1=rmxQ[0:2, 2 * sw + 1:2 * sw + 2], op=OP.max),
                    waits=W0 if sw == 0 else ())
                p.op("vector", lambda e, sw=sw: e.tensor_tensor(
                    out=bbt[0:2, 12 * sw + 1:12 * sw + 2],
                    in0=rmxK[0:2, 2 * sw:2 * sw + 1],
                    in1=rmxK[0:2, 2 * sw + 1:2 * sw + 2], op=OP.max), inc="v")
            for sw in range(4):
                p.op("vector", lambda e, sw=sw: e.stream_shuffle(
                    bbt[0:32, 12 * sw + 2:12 * sw + 4],
                    bbt[0:32, 12 * sw + 0:12 * sw + 2], mask=[1] * 32),
                    waits=[("v", p.cnt["v"])] if sw == 0 else (), inc="v")
            for sw in range(4):
                p.op("vector", lambda e, sw=sw: e.tensor_tensor(
                    out=bbt[0:1, 12 * sw + 4:12 * sw + 6],
                    in0=bbt[0:1, 12 * sw + 0:12 * sw + 2],
                    in1=bbt[0:1, 12 * sw + 2:12 * sw + 4], op=OP.max),
                    waits=[("v", p.cnt["v"])] if sw == 0 else (), inc="v")
            for sw in range(4):
                p.op("vector", lambda e, sw=sw: e.tensor_tensor(
                    out=_c(sw, 6), in0=_c(sw, 4), in1=_c(sw, 5), op=OP.mult),
                    waits=[("v", p.cnt["v"])] if sw == 0 else (), inc="v")
            for sw in range(4):
                p.op("scalar", lambda e, sw=sw: e.activation(
                    out=_c(sw, 7), in_=_c(sw, 6), func=AF.Ln),
                    waits=[("v", p.cnt["v"])] if sw == 0 else (), inc="s")
            for sw in range(4):
                p.op("scalar", lambda e, sw=sw: e.activation(
                    out=_c(sw, 8), in_=_c(sw, 7), func=AF.Exp, scale=0.5),
                    waits=[("s", p.cnt["s"])] if sw == 0 else (), inc="s")
            for sw in range(4):
                p.op("vector", lambda e, sw=sw: e.tensor_scalar(
                    out=_c(sw, 9), in0=_c(sw, 8), scalar1=-0.125,
                    scalar2=85.0, op0=OP.mult, op1=OP.add),
                    waits=[("s", p.cnt["s"])] if sw == 0 else (), inc="v")
            for sw in range(4):
                p.op("vector", lambda e, sw=sw: e.tensor_scalar_min(
                    out=_c(sw, 10), in0=_c(sw, 9), scalar1=0.0),
                    waits=[("v", p.cnt["v"])] if sw == 0 else (), inc="v")
            NBMIN = p.cnt["v"]
            for b4 in range(4):
                for sw in range(4):
                    p.op("vector", lambda e, b4=b4, sw=sw: e.stream_shuffle(
                        negBc[32 * b4:32 * b4 + 32, sw:sw + 1],
                        bbt[0:32, 12 * sw + 10:12 * sw + 11], mask=[0] * 32),
                        waits=[("v", NBMIN)] if (b4 == 0 and sw == 0) else (),
                        inc=("v" if (b4 == 3 and sw == 3) else None))
            NB_V = p.cnt["v"]
            for sw in range(4):
                pairs = (2 * sw, 2 * sw + 1)
                pend = []

                def emit_pv(info):
                    islot = info["i"] % 4
                    pi = info["pi"]
                    for hb in range(2):
                        h = 2 * pairs[pi] + hb
                        p.op("tensor", lambda e, pi=pi, islot=islot, h=h, hb=hb,
                             info=info: e.matmul(
                                 PSB[0:65, 1024 * pi + 512 * hb:
                                     1024 * pi + 512 * hb + 512],
                                 info["vsrc"][:, 65 * h:65 * h + 65],
                                 Pb[:, 1024 * islot + 512 * hb:
                                    1024 * islot + 512 * hb + 512],
                                 start=info["start"], stop=info["stop"]),
                             waits=info["wait"] if hb == 0 else (),
                             inc=("t" if hb == 1 else None))
                    if info["load_idx"] is not None:
                        stream_consumed[info["load_idx"]] = p.cnt["t"]

                for kc in range(nkc):
                    diag = kc >= 4 * T
                    if not diag:
                        slot = kv_loads % NKV
                        wt = [("kv_st", p.cnt["kv_st"])]
                        if kv_loads >= NKV:
                            wt.append(("t", stream_consumed[kv_loads - NKV]))
                        p.dma("gpsimd", kvK[:, 1024 * slot:1024 * slot + 1024],
                              ksp[kc], "kv_ld", waits=wt)
                        p.dma("gpsimd", kvV[:, 1040 * slot:1040 * slot + 1040],
                              vsp[kc], "kv_ld")
                        this_load = p.cnt["kv_ld"]
                        kQ = kvK[:, 1024 * slot:1024 * slot + 1024]
                        vQ = kvV[:, 1040 * slot:1040 * slot + 1040]
                        ks = [kQ[:, 128 * pr:128 * pr + 128] for pr in pairs]
                        vs = [vQ, vQ]
                        kv_loads += 1
                    else:
                        j = kc - 4 * T
                        ks = [Kc[:, TT * pr + 128 * j:TT * pr + 128 * j + 128]
                              for pr in pairs]
                        vs = [Vc[:, 1040 * j:1040 * j + 1040]] * 2
                        this_load = None
                    for pi in range(2):
                        i = 2 * kc + pi
                        if len(pend) == 2:
                            emit_pv(pend.pop(0))
                        wt = p.bank_wait(2 * pi) + p.bank_wait(2 * pi + 1)
                        if i == 0:
                            wt.append(("s", PROD_S))
                        if pi == 0 and this_load is not None:
                            wt.append(("kv_ld", this_load))
                        pr = pairs[pi]
                        ksrc = ks[pi]
                        p.op("tensor", lambda e, pi=pi, ksrc=ksrc, pr=pr: e.matmul(
                            PSA[:, 1024 * pi:1024 * pi + 512], ksrc[0:64, :],
                            Qp[0:64, TT * pr:TT * pr + TT],
                            start=True, stop=True, tile_position=(0, 0)), waits=wt)
                        p.op("tensor", lambda e, pi=pi, ksrc=ksrc, pr=pr: e.matmul(
                            PSA[:, 1024 * pi + 512:1024 * pi + 1024], ksrc[64:128, :],
                            Qp[64:128, TT * pr:TT * pr + TT],
                            start=True, stop=True, tile_position=(64, 0)), inc="t")
                        SMM = p.cnt["t"]
                        p.op("scalar", lambda e, pi=pi, islot=i % 4, sw=sw:
                             e.activation(
                                 out=Pb[:, 1024 * islot:1024 * islot + 1024],
                                 in_=PSA[:, 1024 * pi:1024 * pi + 1024], func=AF.Exp,
                                 scale=SCALE, bias=negBc[:, sw:sw + 1]),
                             waits=[("t", SMM)] + ([("v", NB_V)] if i == 0 else []),
                             inc="s")
                        p.bank_mark(2 * pi, "s")
                        p.bank_mark(2 * pi + 1, "s")
                        if diag:
                            j = kc - 4 * T
                            p.op("vector", lambda e, islot=i % 4, j=j: e.tensor_tensor(
                                out=Pb[:, 1024 * islot:1024 * islot + 1024],
                                in0=Pb[:, 1024 * islot:1024 * islot + 1024],
                                in1=maskB[:, 1024 * j:1024 * j + 1024], op=OP.mult),
                                waits=[("s", p.cnt["s"])], inc="v")
                            pv_wait = [("v", p.cnt["v"])]
                        else:
                            pv_wait = [("s", p.cnt["s"])]
                        if kc == 0:
                            pv_wait = pv_wait + p.bank_wait(4 + 2 * pi) + \
                                      p.bank_wait(5 + 2 * pi)
                        pend.append(dict(i=i, pi=pi, vsrc=vs[pi], wait=pv_wait,
                                         start=(kc == 0), stop=(kc == nkc - 1),
                                         load_idx=(kv_loads - 1)
                                         if (not diag and pi == 1) else None))

                ev2 = []
                for pi in range(2):
                    emit_pv(pend.pop(0))
                    PVD = p.cnt["t"]
                    pr = pairs[pi]
                    p.op("scalar", lambda e, pi=pi, pr=pr: e.copy(
                        out=OnT[0:64, TT * pr:TT * pr + TT],
                        in_=PSB[0:64, 1024 * pi:1024 * pi + 512]),
                        waits=[("t", PVD)], inc="s")
                    p.op("scalar", lambda e, pi=pi: e.copy(
                        out=Ot[:, TT * pi:TT * pi + TT],
                        in_=PSB[0:64, 1024 * pi + 512:1024 * pi + 1024]),
                        waits=ot_guard.get(pi, []), inc="s")
                    SCP = p.cnt["s"]
                    p.dma("sync", OnT[64:128, TT * pr:TT * pr + TT],
                          Ot[:, TT * pi:TT * pi + TT], "sh_st",
                          waits=[("s", SCP)])
                    SHD = p.cnt["sh_st"]
                    ot_guard[pi] = [("sh_st", SHD)]
                    rows = (tA, tB) if pi == 0 else (tC, lb[:, 0:TT])
                    for hb, tt_ in ((0, rows[0]), (1, rows[1])):
                        p.op("vector", lambda e, pi=pi, hb=hb, tt_=tt_:
                             e.tensor_scalar(
                                 out=tt_[64:65, 0:TT],
                                 in0=PSB[64:65, 1024 * pi + 512 * hb:
                                         1024 * pi + 512 * hb + 512],
                                 scalar1=1e-37, scalar2=None, op0=OP.add),
                             waits=[("t", PVD)] if hb == 0 else (),
                             inc=("v" if hb == 1 else None))
                    p.banks[4 + 2 * pi] = [("s", SCP), ("v", p.cnt["v"])]
                    p.banks[5 + 2 * pi] = [("s", SCP), ("v", p.cnt["v"])]
                    ev2.append((pi, pr, SCP, SHD, rows))
                for (pi, pr, SCP, SHD, rows) in ev2:
                    for hb, tt_ in ((0, rows[0]), (1, rows[1])):
                        p.op("vector", lambda e, tt_=tt_: e.reciprocal(
                            out=tt_[64:65, 0:TT], in_=tt_[64:65, 0:TT]),
                            waits=[("v", p.cnt["v"])] if (pi == 0 and hb == 0)
                            else (), inc=("v" if hb == 1 else None))
                    for b4 in range(4):
                        tt_ = rows[0] if b4 < 2 else rows[1]
                        cc = 0 if b4 < 2 else TT
                        p.op("vector", lambda e, b4=b4, tt_=tt_, cc=cc:
                             e.stream_shuffle(
                                 lb[32 * b4:32 * b4 + 32, cc:cc + TT],
                                 tt_[64:96, 0:TT], mask=[0] * 32),
                             waits=[("v", p.cnt["v"])] if b4 == 0 else (),
                             inc=("v" if b4 == 3 else None))
                    p.op("vector", lambda e, pr=pr: e.tensor_tensor(
                        out=OnT[0:64, TT * pr:TT * pr + TT],
                        in0=OnT[0:64, TT * pr:TT * pr + TT],
                        in1=lb[0:64, 0:TT], op=OP.mult),
                        waits=[("s", SCP)])
                    p.op("vector", lambda e, pr=pr: e.tensor_tensor(
                        out=OnT[64:128, TT * pr:TT * pr + TT],
                        in0=OnT[64:128, TT * pr:TT * pr + TT],
                        in1=lb[64:128, TT:2 * TT], op=OP.mult),
                        waits=[("sh_st", SHD)], inc="v")

            ALLON = p.cnt["v"]
            if T + 1 < NTT:
                nxt = xbuf[(T + 1) % 2]
                xw = [("xp_st", SPILL_MARK)] + \
                     ([("s", CAST_DONE)] if T == 0 else [])
                for c in range(NCH):
                    p.dma("sync", nxt[:, TT * c:TT * c + TT],
                          xT[128 * c:128 * c + 128, TT * (T + 1):TT * (T + 1) + TT],
                          "x_ld", waits=xw if c == 0 else ())
                H1R_next, _ = ln_tile(h1T, 0, [("x_ld", p.cnt["x_ld"])],
                                      xsrc=nxt)
            for fc in range(NCH):
                bi = rotg % 8; rotg += 1
                bk = bank(bi)
                for hp in range(NHP):
                    p.op("tensor", lambda e, fc=fc, hp=hp, bk=bk: e.matmul(
                        bk, woutB[:, 1024 * hp + 128 * fc:1024 * hp + 128 * fc + 128],
                        OnT[:, TT * hp:TT * hp + TT],
                        start=(hp == 0), stop=(hp == NHP - 1)),
                        waits=([("v", ALLON)] + p.bank_wait(bi))
                        if hp == 0 else (),
                        inc=("t" if hp == NHP - 1 else None))
                p.op("vector", lambda e, bk=bk, fc=fc, cur=cur:
                     e.scalar_tensor_tensor(
                         out=cur[:, TT * fc:TT * fc + TT], in0=bk,
                         scalar=boutF[:, fc:fc + 1],
                         in1=cur[:, TT * fc:TT * fc + TT], op0=OP.add, op1=OP.add),
                     waits=[("t", p.cnt["t"])], inc="v")
                p.bank_mark(bi, "v")
                p.dma("sync", xp[128 * fc:128 * fc + 128, TT * T:TT * T + TT],
                      cur[:, TT * fc:TT * fc + TT], "xp_st",
                      waits=[("v", p.cnt["v"])])
            SPILL_MARK = p.cnt["xp_st"]
            if debug and T == DEV_DBG_T:
                p.dma("gpsimd", dbg["d_h1T"], h1T[:], "y_st", waits=[("v", H1R)])
                p.dma("gpsimd", dbg["d_ont"], OnT[:], "y_st",
                      waits=[("v", ALLON)])
                p.dma("gpsimd", dbg["d_nb"], negBc[:], "y_st", waits=[("v", NB_V)])
                p.dma("gpsimd", dbg["d_bbt"], bbt[:], "y_st")
                p.dma("gpsimd", dbg["d_rmx"], rmxQ[:], "y_st")

        FIN = {k: p.cnt[k] for k in p.cnt}
        p.flush(blkA)
        blkA_es.close()
        esA.close()

        esB = es.enter_context(contextlib.ExitStack())
        def sbB(name, shape, dt):
            return esB.enter_context(nc.sbuf_tensor(name, shape, dt))

        tD = sbB("tD", [128, TT], F32)
        tE = sbB("tE", [128, TT], F32)
        tF = sbB("tF", [128, TT], F32)
        h2T = sbB("h2T", [128, 2 * NCH * TT], BF16)
        wgB = sbB("wgB", [128, 2 * NCH * DI], BF16)
        wuB = sbB("wuB", [128, 2 * NCH * DI], BF16)
        wdB = sbB("wdB", [128, 2 * 4 * D], BF16)
        actB = sbB("actB", [128, 4 * TT], BF16)
        routed = sbB("routed", [128, 2 * NCH * TT], F32)
        cwb = sbB("cwb", [128, TT], F32)

        blkB_es = contextlib.ExitStack()
        blkB = blkB_es.enter_context(nc.Block())
        bw = ([(k, FIN[k]) for k in ("t", "s", "v")] +
              [(k, FIN[k]) for k in ("c_ld", "x_ld", "kv_ld", "kv_st",
                                     "xp_st", "sh_st", "y_st")])
        p.extra_first = {e: list(bw) for e in ENGINES}

        wslot_cnt = {}
        stg_guard = {}
        wchain = []
        wcast_done = {}
        for ST in range(0 if DEV_SKIP_B else NST):
            h2rv = {}
            for tt in range(2):
                T = 2 * ST + tt
                xw = [("y_st", p.cnt["y_st"]), ("t", p.cnt["t"]),
                      ("v", p.cnt["v"]), ("s", p.cnt["s"])]
                for c in range(NCH):
                    p.dma("sync", xio[:, TT * c:TT * c + TT],
                          xp[128 * c:128 * c + 128, TT * T:TT * T + TT], "x_ld",
                          waits=xw if c == 0 else ())
                XL = p.cnt["x_ld"]
                H2R, VMR = ln_tile(h2T, NCH * TT * tt, [("x_ld", XL)])
                h2rv[tt] = H2R
                for c in range(NCH):
                    p.op("tensor", lambda e, c=c: e.matmul(
                        bank(4)[0:8, :], gwF[:, 8 * c:8 * c + 8],
                        xio[:, TT * c:TT * c + TT],
                        start=(c == 0), stop=(c == NCH - 1)),
                        waits=([("v", H2R)] + p.bank_wait(4)) if c == 0 else (),
                        inc=("t" if c == NCH - 1 else None))
                p.op("scalar", lambda e: e.copy(out=glbuf[:], in_=bank(4)[0:8, :]),
                     waits=[("t", p.cnt["t"])], inc="s")
                p.bank_mark(4, "s")
                GLE = p.cnt["s"]
                for sub in range(4):
                    p.op("tensor", lambda e, sub=sub: e.matmul(
                        bank(5)[:, 0:8], glbuf[:, 128 * sub:128 * sub + 128], i8F[:],
                        start=True, stop=True),
                        waits=[("s", GLE), ("v", VMR)] + p.bank_wait(5))
                    p.op("tensor", lambda e, sub=sub: e.matmul(
                        bank(5)[:, 8:9], r_rstd[0:1, 128 * sub:128 * sub + 128],
                        onesF[0:1, 0:1], start=True, stop=True))
                    p.op("tensor", lambda e, sub=sub: e.matmul(
                        bank(5)[:, 9:10], r_mr[0:1, 128 * sub:128 * sub + 128],
                        onesF[0:1, 0:1], start=True, stop=True), inc="t")
                    p.op("scalar", lambda e, sub=sub: e.copy(
                        out=gtm[:, 10 * sub:10 * sub + 10], in_=bank(5)[:, 0:10]),
                        waits=[("t", p.cnt["t"])], inc="s")
                    p.bank_mark(5, "s")
                GTE = p.cnt["s"]

                o8 = lambda t_, sub: t_[:, 8 * sub:8 * sub + 8]
                o4_ = lambda t_, sub: t_[:, 4 * sub:4 * sub + 4]
                def rop(mk, inc=None, w=None):
                    for sub in range(4):
                        p.op("vector", mk(sub), waits=(w or []) if sub == 0 else [],
                             inc=(inc if sub == 3 else None))
                rop(lambda sub: (lambda e: e.tensor_scalar(
                    out=o8(rA_, sub), in0=gtm[:, 10 * sub:10 * sub + 8],
                    scalar1=gtm[:, 10 * sub + 8:10 * sub + 9], scalar2=None,
                    op0=OP.mult)), w=[("s", GTE)])
                rop(lambda sub: (lambda e: e.tensor_scalar(
                    out=o8(rB_, sub), in0=gcol[:],
                    scalar1=gtm[:, 10 * sub + 9:10 * sub + 10], scalar2=None,
                    op0=OP.mult)))
                rop(lambda sub: (lambda e: e.tensor_tensor(
                    out=o8(rL_, sub), in0=o8(rA_, sub), in1=o8(rB_, sub),
                    op=OP.subtract)), inc="v")
                p.op("scalar", lambda e: e.activation(out=rS_[:], in_=rL_[:],
                     func=AF.Sigmoid), waits=[("v", p.cnt["v"])], inc="s")
                SC = p.cnt["s"]
                def ropS(mk, inc=None):
                    for sub in range(4):
                        p.op("vector", mk(sub), waits=[("s", SC)] if sub == 0 else [],
                             inc=(inc if sub == 3 else None))
                ropS(lambda sub: (lambda e: e.tensor_tensor(
                    out=o8(rC_, sub), in0=o8(rS_, sub), in1=cbb[:], op=OP.add)))
                ropS(lambda sub: (lambda e: e.tensor_tensor(
                    out=o4_(rG4, sub), in0=rC_[:, 8 * sub:8 * sub + 8:2],
                    in1=rC_[:, 8 * sub + 1:8 * sub + 8:2], op=OP.add)))
                ropS(lambda sub: (lambda e: e.tensor_reduce(
                    out=rM_[:, 8 * sub:8 * sub + 1], in_=o4_(rG4, sub),
                    axis=AX, op=OP.max)))
                ropS(lambda sub: (lambda e: e.tensor_scalar(
                    out=o4_(rI_, sub), in0=o4_(rG4, sub),
                    scalar1=rM_[:, 8 * sub:8 * sub + 1], scalar2=None,
                    op0=OP.is_ge)))
                ropS(lambda sub: (lambda e: e.scalar_tensor_tensor(
                    out=o4_(rW_, sub), in0=o4_(rI_, sub), scalar=-BIG,
                    in1=o4_(rG4, sub), op0=OP.mult, op1=OP.add)))
                ropS(lambda sub: (lambda e: e.tensor_reduce(
                    out=rM_[:, 8 * sub + 1:8 * sub + 2], in_=o4_(rW_, sub),
                    axis=AX, op=OP.max)))
                ropS(lambda sub: (lambda e: e.tensor_scalar(
                    out=o4_(rI_, sub), in0=o4_(rG4, sub),
                    scalar1=rM_[:, 8 * sub + 1:8 * sub + 2], scalar2=None,
                    op0=OP.is_ge)))
                ropS(lambda sub: (lambda e: e.tensor_copy(
                    out=rSM[:, 8 * sub:8 * sub + 8:2], in_=o4_(rI_, sub))))
                ropS(lambda sub: (lambda e: e.tensor_copy(
                    out=rSM[:, 8 * sub + 1:8 * sub + 8:2], in_=o4_(rI_, sub))))
                ropS(lambda sub: (lambda e: e.tensor_tensor(
                    out=o8(rFC, sub), in0=o8(rSM, sub), in1=o8(rC_, sub),
                    op=OP.mult)))
                ropS(lambda sub: (lambda e: e.tensor_scalar(
                    out=o8(rSM, sub), in0=o8(rSM, sub), scalar1=1.0,
                    scalar2=None, op0=OP.subtract)))
                ropS(lambda sub: (lambda e: e.tensor_tensor(
                    out=o8(rB_, sub), in0=o8(rSM, sub), in1=idxE[:], op=OP.mult)))
                ropS(lambda sub: (lambda e: e.tensor_tensor(
                    out=o8(rFC, sub), in0=o8(rFC, sub), in1=o8(rB_, sub),
                    op=OP.add)))
                ropS(lambda sub: (lambda e: e.tensor_copy(
                    out=o8(rW_, sub), in_=o8(rFC, sub))))
                for rnd in range(3):
                    ropS(lambda sub: (lambda e: e.tensor_reduce(
                        out=rM_[:, 8 * sub + 2:8 * sub + 3], in_=o8(rW_, sub),
                        axis=AX, op=OP.max)))
                    ropS(lambda sub: (lambda e: e.tensor_scalar(
                        out=o8(rI_, sub), in0=o8(rW_, sub),
                        scalar1=rM_[:, 8 * sub + 2:8 * sub + 3], scalar2=None,
                        op0=OP.is_ge)))
                    ropS(lambda sub: (lambda e: e.scalar_tensor_tensor(
                        out=o8(rW_, sub), in0=o8(rI_, sub), scalar=-BIG,
                        in1=o8(rW_, sub), op0=OP.mult, op1=OP.add)))
                ropS(lambda sub: (lambda e: e.tensor_reduce(
                    out=rM_[:, 8 * sub + 3:8 * sub + 4], in_=o8(rW_, sub),
                    axis=AX, op=OP.max)))
                ropS(lambda sub: (lambda e: e.tensor_scalar(
                    out=o8(rI_, sub), in0=o8(rFC, sub),
                    scalar1=rM_[:, 8 * sub + 3:8 * sub + 4], scalar2=None,
                    op0=OP.is_ge)))
                ropS(lambda sub: (lambda e: e.tensor_tensor(
                    out=o8(rW_, sub), in0=o8(rI_, sub), in1=o8(rS_, sub),
                    op=OP.mult)))
                ropS(lambda sub: (lambda e: e.tensor_reduce(
                    out=rM_[:, 8 * sub + 4:8 * sub + 5], in_=o8(rW_, sub),
                    axis=AX, op=OP.add)))
                ropS(lambda sub: (lambda e: e.tensor_scalar(
                    out=rM_[:, 8 * sub + 5:8 * sub + 6],
                    in0=rM_[:, 8 * sub + 4:8 * sub + 5], scalar1=1e-20,
                    scalar2=None, op0=OP.add)))
                ropS(lambda sub: (lambda e: e.reciprocal(
                    out=rM_[:, 8 * sub + 6:8 * sub + 7],
                    in_=rM_[:, 8 * sub + 5:8 * sub + 6])))
                ropS(lambda sub: (lambda e: e.tensor_scalar(
                    out=rM_[:, 8 * sub + 7:8 * sub + 8],
                    in0=rM_[:, 8 * sub + 6:8 * sub + 7], scalar1=2.5,
                    scalar2=None, op0=OP.mult)))
                ropS(lambda sub: (lambda e: e.tensor_scalar(
                    out=o8(rCW, sub), in0=o8(rW_, sub),
                    scalar1=rM_[:, 8 * sub + 7:8 * sub + 8], scalar2=None,
                    op0=OP.mult)), inc="v")
                CWV = p.cnt["v"]
                for sub in range(4):
                    p.op("tensor", lambda e, sub=sub: e.matmul(
                        bank(5)[0:8, 0:128], rCW[:, 8 * sub:8 * sub + 8], i128F[:],
                        start=True, stop=True),
                        waits=[("v", CWV)] + p.bank_wait(5), inc="t")
                    p.op("scalar", lambda e, sub=sub, tt=tt: e.copy(
                        out=cwT[0:8, TT * tt + 128 * sub:TT * tt + 128 * sub + 128],
                        in_=bank(5)[0:8, 0:128]), waits=[("t", p.cnt["t"])], inc="s")
                    p.bank_mark(5, "s")
            CWS = p.cnt["s"]
            if debug and ST == 0:
                p.dma("gpsimd", dbg["d_h2T"], h2T[:, 0:NCH * TT], "y_st",
                      waits=[("v", p.cnt["v"])])
                p.dma("gpsimd", dbg["d_cwT"], cwT[:], "y_st", waits=[("s", CWS)])

            def queue_chain(gi):
                e_idx = ([NE] + list(range(NE)))[gi % 9]
                sl = gi % 2
                for step, (which, wt_dram, blk) in enumerate(
                        [(w_, d_, b_) for (w_, d_) in (("g", wg), ("u", wu),
                                                       ("d", wd))
                         for b_ in range(4)]):
                    dstb = {"g": wgB, "u": wuB, "d": wdB}[which]
                    if which != "d":
                        din_ap = wt_dram[e_idx, 256 * blk:256 * blk + 256, :] \
                            .rearrange("(k p) x -> p k x", k=2)
                        off = NCH * DI * sl + 1024 * blk
                    else:
                        din_ap = wt_dram[e_idx, 128 * blk:128 * blk + 128, :]
                        off = 4 * D * sl + D * blk
                    wchain.append((gi, sl, step, din_ap, dstb, off))

            def drain_step():
                if not wchain:
                    return
                gi, sl, step, din_ap, dstb, off = wchain.pop(0)
                s0 = 1024 * (step % 2)
                wt = list(stg_guard.get(step % 2, []))
                if step == 0 and wslot_cnt.get(sl) is not None:
                    wt.append(("t", wslot_cnt[sl]))
                dout_ap = xn[:, s0:s0 + 1024]
                if din_ap.shape != (128, 1024):
                    dout_ap = dout_ap.rearrange("p (k x) -> p k x", k=2)
                p.dma("gpsimd", dout_ap, din_ap, "w_ld", waits=wt)
                p.op("scalar", lambda e, dstb=dstb, off=off, s0=s0:
                     e.copy(out=dstb[:, off:off + 1024],
                            in_=xn[:, s0:s0 + 1024]),
                     waits=[("w_ld", p.cnt["w_ld"])], inc="s")
                stg_guard[step % 2] = [("s", p.cnt["s"])]
                wcast_done[gi] = p.cnt["s"]

            def drain_until(gi):
                while wchain and wchain[0][0] <= gi:
                    drain_step()

            for ei, e_idx in enumerate([NE] + list(range(NE))):
                gi = 9 * ST + ei
                sl = gi % 2
                if gi == 0:
                    queue_chain(0)
                drain_until(gi)
                WCAST_S = wcast_done[gi]
                if gi + 1 < 9 * NST:
                    queue_chain(gi + 1)

                for tt in range(2):
                    if e_idx != NE:
                        for b4 in range(4):
                            p.op("vector", lambda e, b4=b4, tt=tt, e_idx=e_idx:
                                 e.stream_shuffle(cwb[32 * b4:32 * b4 + 32, :],
                                                  cwT[0:32, TT * tt:TT * tt + TT],
                                                  mask=[e_idx] * 32),
                                 waits=[("s", CWS), ("t", p.cnt["t"])]
                                 if b4 == 0 else (),
                                 inc=("v" if b4 == 3 else None))
                    CWBV = p.cnt["v"]
                    mults = []
                    act_mark = {}
                    for ic in range(4):
                        gb = 2 * (ic % 2)
                        ub = gb + 1
                        for wsb, bki in ((wgB, gb), (wuB, ub)):
                            bk = bank(bki)
                            for hc in range(NCH):
                                p.op("tensor", lambda e, wsb=wsb, bk=bk, ic=ic,
                                     hc=hc, sl=sl, tt=tt: e.matmul(
                                         bk,
                                         wsb[:, NCH * DI * sl + DI * hc + 128 * ic:
                                             NCH * DI * sl + DI * hc + 128 * ic + 128],
                                         h2T[:, NCH * TT * tt + TT * hc:
                                             NCH * TT * tt + TT * hc + TT],
                                         start=(hc == 0), stop=(hc == NCH - 1)),
                                     waits=([("s", WCAST_S), ("v", h2rv[tt])] +
                                            p.bank_wait(bki))
                                     if hc == 0 else (),
                                     inc=("t" if hc == NCH - 1 else None))
                        stmp = (tA, tB, tC)[ic % 3]
                        p.op("scalar", lambda e, stmp=stmp, gb=gb: e.activation(
                            out=stmp[:], in_=bank(gb), func=AF.Silu),
                            waits=[("t", p.cnt["t"]), ("v", p.cnt["v"])], inc="s")
                        p.bank_mark(gb, "s")
                        drain_step()
                        mtmp = (tD, tE, tF)[ic % 3]
                        p.op("vector", lambda e, stmp=stmp, mtmp=mtmp, ub=ub:
                             e.tensor_tensor(out=mtmp[:], in0=stmp[:],
                                             in1=bank(ub), op=OP.mult),
                             waits=[("s", p.cnt["s"])], inc="v")
                        p.bank_mark(ub, "v")
                        drain_step()
                        mults.append((mtmp, ic))
                        if len(mults) == 2:
                            mt, mic = mults.pop(0)
                            if e_idx == NE:
                                p.op("vector", lambda e, mt=mt, mic=mic:
                                     e.tensor_copy(out=actB[:, TT * mic:TT * mic + TT],
                                                   in_=mt[:]), inc="v")
                            else:
                                p.op("vector", lambda e, mt=mt, mic=mic:
                                     e.tensor_tensor(out=actB[:, TT * mic:TT * mic + TT],
                                                     in0=mt[:], in1=cwb[:],
                                                     op=OP.mult),
                                     waits=[("v", CWBV)], inc="v")
                            act_mark[mic] = p.cnt["v"]
                    mt, mic = mults.pop(0)
                    if e_idx == NE:
                        p.op("vector", lambda e, mt=mt, mic=mic: e.tensor_copy(
                            out=actB[:, TT * mic:TT * mic + TT], in_=mt[:]), inc="v")
                    else:
                        p.op("vector", lambda e, mt=mt, mic=mic: e.tensor_tensor(
                            out=actB[:, TT * mic:TT * mic + TT], in0=mt[:],
                            in1=cwb[:], op=OP.mult), waits=[("v", CWBV)], inc="v")
                    act_mark[mic] = p.cnt["v"]
                    ACTV = p.cnt["v"]
                    for hc in range(NCH):
                        bi = 4 + hc % 4
                        bk = bank(bi)
                        for ic in range(4):
                            p.op("tensor", lambda e, hc=hc, ic=ic, sl=sl, bk=bk:
                                 e.matmul(bk,
                                          wdB[:, 4 * D * sl + D * ic + 128 * hc:
                                              4 * D * sl + D * ic + 128 * hc + 128],
                                          actB[:, TT * ic:TT * ic + TT],
                                          start=(ic == 0), stop=(ic == 3)),
                                 waits=([("v", act_mark[ic])] +
                                        (p.bank_wait(bi) if ic == 0 else [])),
                                 inc=("t" if ic == 3 else None))
                        wslot_cnt[sl] = p.cnt["t"]
                        dst = routed[:, NCH * TT * tt + TT * hc:
                                     NCH * TT * tt + TT * hc + TT]
                        if ei == 0:
                            p.op("vector", lambda e, dst=dst, bk=bk:
                                 e.tensor_copy(out=dst, in_=bk),
                                 waits=[("t", p.cnt["t"])], inc="v")
                        else:
                            p.op("vector", lambda e, dst=dst, bk=bk:
                                 e.tensor_tensor(out=dst, in0=dst, in1=bk,
                                                 op=OP.add),
                                 waits=[("t", p.cnt["t"])], inc="v")
                        p.bank_mark(bi, "v")
            ROUT = p.cnt["v"]
            for tt in range(2):
                T = 2 * ST + tt
                xw = [("v", ROUT), ("y_st", p.cnt["y_st"])]
                for c in range(NCH):
                    p.dma("sync", xio[:, TT * c:TT * c + TT],
                          xp[128 * c:128 * c + 128, TT * T:TT * T + TT], "x_ld",
                          waits=xw if c == 0 else ())
                XL = p.cnt["x_ld"]
                for c in range(NCH):
                    p.op("vector", lambda e, c=c, tt=tt: e.tensor_tensor(
                        out=xio[:, TT * c:TT * c + TT],
                        in0=xio[:, TT * c:TT * c + TT],
                        in1=routed[:, NCH * TT * tt + TT * c:
                                   NCH * TT * tt + TT * c + TT],
                        op=OP.add), waits=[("x_ld", XL)], inc="v")
                    p.dma("sync", yT[128 * c:128 * c + 128, TT * T:TT * T + TT],
                          xio[:, TT * c:TT * c + TT], "y_st",
                          waits=[("v", p.cnt["v"])])
        if p.cnt["y_st"]:
            p.op("sync", lambda e: e.nop(), waits=[("y_st", p.cnt["y_st"])])
        p.flush(blkB)
        blkB_es.close()
        esB.close()
    return nc


_QKV_PERM = None
def _qkv_perm():
    global _QKV_PERM
    if _QKV_PERM is None:
        perm = []
        for hp in range(NHP):
            for h in (2 * hp, 2 * hp + 1):
                perm.extend(range(h * 192, h * 192 + 64))
        for hp in range(NHP):
            for h in (2 * hp, 2 * hp + 1):
                perm.extend(range(h * 192 + 64, h * 192 + 128))
        for h in range(NH):
            perm.extend(range(h * 192 + 128, h * 192 + 192))
        _QKV_PERM = np.array(perm)
    return _QKV_PERM


_NC_CACHE = {}
def get_nc(debug=False):
    if debug not in _NC_CACHE:
        _NC_CACHE[debug] = build_nc(debug)
    return _NC_CACHE[debug]


def make_inputs(L0_attn_x, L0_attn_w_qkv, L0_attn_w_out, L0_attn_b_out,
                L0_ffn_gate_w, L0_ffn_correction_bias, L0_ffn_Wg, L0_ffn_Wu,
                L0_ffn_Wd, L0_ffn_Wg_s, L0_ffn_Wu_s, L0_ffn_Wd_s):
    f = np.float32
    wqkvT = np.ascontiguousarray(np.asarray(L0_attn_w_qkv, f).T[:, _qkv_perm()])
    woutT = np.ascontiguousarray(np.asarray(L0_attn_w_out, f).T)
    gwT = np.ascontiguousarray(np.asarray(L0_ffn_gate_w, f).T)
    bout = np.ascontiguousarray(np.asarray(L0_attn_b_out, f).reshape(NCH, 128).T)
    cbias = np.ascontiguousarray(np.asarray(L0_ffn_correction_bias, f).reshape(1, NE))
    idxeps = np.ascontiguousarray(
        np.tile((np.arange(1, NE + 1, dtype=f) * f(1e-30)).reshape(1, NE), (128, 1)))
    kk = np.arange(128).reshape(1, 128, 1)
    qq = np.arange(TT).reshape(1, 1, TT)
    jj = np.arange(4).reshape(4, 1, 1)
    mk = (128 * jj + kk <= qq).astype(f)
    mk = np.ascontiguousarray(np.tile(mk, (1, 1, 2)))
    onesf = np.ones((128, 128), f)
    hones = np.zeros((128, 2), f); hones[0:64, 0] = 1.0; hones[64:128, 1] = 1.0
    i8 = np.eye(8, dtype=f)
    i128 = np.eye(128, dtype=f)
    wg_ = np.ascontiguousarray(np.concatenate(
        [np.asarray(L0_ffn_Wg, f), np.asarray(L0_ffn_Wg_s, f)[None]], axis=0))
    wu_ = np.ascontiguousarray(np.concatenate(
        [np.asarray(L0_ffn_Wu, f), np.asarray(L0_ffn_Wu_s, f)[None]], axis=0))
    wd_ = np.ascontiguousarray(np.concatenate(
        [np.asarray(L0_ffn_Wd, f), np.asarray(L0_ffn_Wd_s, f)[None]], axis=0))
    shared = dict(wqkvT=wqkvT, woutT=woutT, gwT=gwT, bout=bout, cbias=cbias,
                  idxeps=idxeps, masks=mk, onesf=onesf, hones=hones, i8=i8,
                  i128=i128, wg=wg_, wu=wu_, wd=wd_)
    x = np.asarray(L0_attn_x, f)
    maps = []
    for core in range(NCORES):
        m = dict(shared)
        m["xT"] = np.ascontiguousarray(x[core].T)
        maps.append(m)
    return maps


def kernel(**inputs):
    maps = make_inputs(**inputs)
    nc = get_nc(debug=False)
    res = run_bass_kernel_spmd(nc, maps, core_ids=list(range(NCORES)))
    out = np.stack([np.ascontiguousarray(r["yT"].T) for r in res.results], axis=0)
    return out.astype(np.float32)
